# version 04
# baseline (speedup 1.0000x reference)
import contextlib
import os
import numpy as np

import concourse.bass as bass
import concourse.mybir as mybir
from concourse.bass_utils import run_bass_kernel_spmd

F32 = mybir.dt.float32
BF16 = mybir.dt.bfloat16
I32 = mybir.dt.int32
Alu = mybir.AluOpType
AFT = mybir.ActivationFunctionType
AX = mybir.AxisListType

D = 2048
T = 1024
NCH = 16
HALO = 512
NQT = 8
NHEAD = 16
QC = 512
INTER = 8192

STAGE = int(os.environ.get("KSTAGE", "99"))
KDBG = os.environ.get("KDBG", "")
TSUB1 = 256
TSUB2 = 512

TWO_PI = 6.283185307179586
RC1 = float(np.float32(TWO_PI))
RC2 = float(np.float32(TWO_PI - RC1))
RC3 = float(np.float32(TWO_PI - RC1 - RC2))
INV2PI = float(np.float32(1.0 / TWO_PI))
PIO2 = float(np.float32(TWO_PI / 4.0))
PI_F = float(np.float32(TWO_PI / 2.0))


class P:

    ENGS = ("sp", "pe", "v", "s", "g")

    def __init__(self, nc):
        self.nc = nc
        self.q = {e: [] for e in self.ENGS}
        self.sem_count = {}
        self.sem_handles = {}
        self.last_w = {}
        self.readers = {}
        self.waited = {e: {} for e in self.ENGS}
        self._stack = contextlib.ExitStack()
        for name in ("dsp", "dg", "dsc", "cpe", "cv", "cs", "cg", "ccc"):
            self.sem_handles[name] = self._stack.enter_context(nc.semaphore(name))
            self.sem_count[name] = 0

    def close(self):
        self._stack.close()

    def _wait(self, eng, sem, val):
        if val <= 0 or self.waited[eng].get(sem, 0) >= val:
            return
        self.waited[eng][sem] = val
        self.q[eng].append(("w", self.sem_handles[sem], val))

    def emit(self, eng, builder, reads=(), writes=(), sem=None, inc=None):
        for r in reads:
            lw = self.last_w.get(r)
            if lw:
                self._wait(eng, lw[0], lw[1])
        for r in writes:
            lw = self.last_w.get(r)
            if lw:
                self._wait(eng, lw[0], lw[1])
            for s_, v_ in self.readers.get(r, {}).items():
                self._wait(eng, s_, v_)
        if sem is None:
            sem = {"sp": "dsp", "pe": "cpe", "v": "cv", "s": "cs", "g": "cg"}[eng]
            inc_ = 16 if sem in ("dsp", "dg", "dsc") else 1
        else:
            inc_ = 1
        if inc is not None:
            inc_ = inc
        self.sem_count[sem] += inc_
        val = self.sem_count[sem]
        self.q[eng].append(("i", builder, self.sem_handles[sem], inc_))
        for r in writes:
            self.last_w[r] = (sem, val)
            self.readers[r] = {}
        for r in reads:
            d = self.readers.setdefault(r, {})
            d[sem] = max(d.get(sem, 0), val)
        return sem, val

    def dma(self, eng, out, in_, reads=(), writes=(), **kw):
        sem = {"sp": "dsp", "g": "dg", "s": "dsc"}[eng]
        return self.emit(eng, lambda e: e.dma_start(out=out, in_=in_, **kw),
                         reads=reads, writes=writes, sem=sem, inc=16)

    def fence(self, eng):
        for sem, cnt in self.sem_count.items():
            self._wait(eng, sem, cnt)

    def fence_all(self):
        for e in self.ENGS:
            self.fence(e)

    def finalize(self, block):
        def runner(lst):
            def f(e):
                for item in lst:
                    if item[0] == "w":
                        e.wait_ge(item[1], item[2])
                    else:
                        _, builder, h, inc_ = item
                        inst = builder(e)
                        if inc_:
                            inst.then_inc(h, inc_)
            return f
        block.sync(runner(self.q["sp"]))
        block.tensor(runner(self.q["pe"]))
        block.vector(runner(self.q["v"]))
        block.scalar(runner(self.q["s"]))
        block.gpsimd(runner(self.q["g"]))


def ts(i, n):
    return bass.ts(i, n)


def build_nc():
    nc = bass.Bass("TRN2", target_bir_lowering=False, num_devices=8)
    g = {"nc": nc}

    def inp(name, shape, dtype=F32):
        g[name] = nc.dram_tensor(name, shape, dtype, kind="ExternalInput").ap()

    inp("x_own", [D, T])
    inp("x_halo", [D, HALO])
    inp("posmeta", [128, 1])
    inp("tilemask", [128, NQT * 8])
    inp("wq_comp", [D, QC])
    inp("wq_up", [QC, D])
    inp("wk", [D, 128])
    inp("wv", [D, 128])
    inp("q_norm_w", [1, 128])
    inp("k_norm_w", [1, 128])
    inp("sinkrep", [128, NHEAD])
    inp("out_w1", [4, QC, D])
    inp("out_w2", [INTER, D])
    inp("wqT", [D, D])
    inp("wkT", [D, D])
    inp("wvT", [D, D])
    inp("bqT", [128, NCH])
    inp("bkT", [128, NCH])
    inp("bvT", [128, NCH])
    inp("wfc_shard", [INTER // 8, D])
    inp("wproj_shard", [D // 8, INTER])
    g["y"] = nc.dram_tensor("y", [D, T], F32, kind="ExternalOutput").ap()

    g["w1bf"] = nc.dram_tensor("w1bf", [4, QC, D], BF16, kind="Internal").ap()
    g["w2bf"] = nc.dram_tensor("w2bf", [NCH, INTER, 128], BF16, kind="Internal").ap()
    QSH = 8 * 2048 * 128 + 2 * 8192 * 128
    g["q_cin"] = nc.dram_tensor("q_cin", [QSH // 512, 512], BF16, kind="Internal").ap()
    g["q_cout"] = nc.dram_tensor("q_cout", [8, QSH // 512, 512], BF16, kind="Internal").ap()
    KVSH = NCH * 128 * T + 8 * 128 * D
    g["kv_cin"] = nc.dram_tensor("kv_cin", [KVSH // 512, 512], BF16, kind="Internal").ap()
    g["kv_cout"] = nc.dram_tensor("kv_cout", [2, KVSH // 512, 512], BF16, kind="Internal").ap()

    def sball(name, shape, dtype):
        g[name] = nc.alloc_sbuf_tensor(name, shape, dtype).ap()

    sball("A_x", [128, NCH * T], F32)
    sball("AB1", [128, 16384], BF16)
    sball("AB2", [128, 16384], BF16)
    start, _ = nc.bump_sbuf(32768)
    g["AB3"] = nc.alloc_sbuf_tensor_at("AB3", [128, 16384], BF16, offset=start).ap()
    g["qw"] = nc.alloc_sbuf_tensor_at("qw", [128, 2048], F32, offset=start).ap()
    g["qu"] = nc.alloc_sbuf_tensor_at("qu", [128, 1024], F32, offset=start + 8192).ap()
    g["qA"] = nc.alloc_sbuf_tensor_at("qA", [128, 1024], F32, offset=start + 12288).ap()
    g["qB"] = nc.alloc_sbuf_tensor_at("qB", [128, 1024], F32, offset=start + 16384).ap()
    g["qI"] = nc.alloc_sbuf_tensor_at("qI", [128, 1024], I32, offset=start + 20480).ap()
    g["qdq"] = nc.alloc_sbuf_tensor_at("qdq", [128, 1024], BF16, offset=start + 24576).ap()
    g["qT"] = nc.alloc_sbuf_tensor_at("qT", [128, 1024], BF16, offset=start + 26624).ap()
    g["qBK"] = nc.alloc_sbuf_tensor_at("qBK", [128, 160], F32, offset=start + 28672).ap()

    sball("STR", [128, 4096], BF16)
    sball("KVB", [128, 4096], BF16)
    sball("VTM", [128, 1536], BF16)
    sball("SC1", [128, 1024], BF16)
    sball("SC2", [128, 1024], BF16)
    sball("SMALL", [128, 1024], F32)
    sball("RSC", [128, 1024], F32)
    sball("ROPE", [64, 1024], F32)
    sball("KKI", [32, 1024], I32)
    sball("CST", [128, 512], F32)
    sball("CSTB", [128, 768], BF16)
    sball("IDE", [128, 128], BF16)
    sball("MSKB", [128, 256], BF16)
    sball("SHR", [128, 48], F32)
    sball("ROWF", [1, 1024], F32)
    sball("QUPD", [128, 16], I32)
    sball("CEPS", [128, 2], F32)
    nc.const_aps.register(F32, 1e-5, g["CEPS"][:, 0:1])
    nc.const_aps.register(F32, 1e-6, g["CEPS"][:, 1:2])

    g["ps_a"] = nc.alloc_psum_tensor("ps_a", [128, 1024], F32).ap()
    g["ps_b"] = nc.alloc_psum_tensor("ps_b", [128, 1024], F32).ap()
    g["ps_c"] = nc.alloc_psum_tensor("ps_c", [128, 1024], F32).ap()
    g["ps_t"] = nc.alloc_psum_tensor("ps_t", [128, 1024], BF16).ap()
    g["ps_s"] = nc.alloc_psum_tensor("ps_s", [128, 512], F32).ap()

    assert nc.sbuf_bytes_remaining >= 0

    with nc.Block() as block:
        p = P(nc)
        build_program(nc, p, g)
        p.finalize(block)
        p.close()
    return nc


def build_program(nc, p, g):
    build_consts(nc, p, g)
    qgen = quant_generator(nc, p, g) if STAGE >= 5 else None

    def pump(n):
        if qgen is None:
            return
        for _ in range(n):
            try:
                next(qgen)
            except StopIteration:
                break

    load_x(nc, p, g)
    layer_norm(nc, p, g, dst=g["AB1"], tag="ln1")
    pump(30)
    if STAGE >= 2:
        attn_a_qkv(nc, p, g, pump)
    if STAGE >= 3:
        attn_a_scores(nc, p, g, pump)
    if STAGE >= 4:
        attn_a_out(nc, p, g, pump)
    if STAGE >= 5:
        pump(10 ** 9)
        quant_allgather(nc, p, g)
        layer_norm(nc, p, g, dst=g["AB1"], tag="ln2")
        ffn(nc, p, g, call=0)
    if STAGE >= 6:
        layer_norm(nc, p, g, dst=g["AB1"], tag="ln3")
        attn_b(nc, p, g)
    if STAGE >= 7:
        layer_norm(nc, p, g, dst=g["AB1"], tag="ln4")
        ffn(nc, p, g, call=1)
    write_out(nc, p, g)


def load_x(nc, p, g):
    for ch in range(NCH):
        p.dma("sp", g["A_x"][:, ts(ch, T)], g["x_own"][ts(ch, 128), :], writes=[f"x{ch}"])


def build_consts(nc, p, g):
    CST, CSTB, MSKB, SHR = g["CST"], g["CSTB"], g["MSKB"], g["SHR"]
    p.emit("g", lambda e: e.memset(g["CEPS"][:, 0:1], 1e-5), writes=["CEPS"])
    p.emit("g", lambda e: e.memset(g["CEPS"][:, 1:2], 1e-6), reads=["CEPS"], writes=["CEPS"])
    p.emit("g", lambda e: e.memset(g["IDE"][:], 0.0), writes=["IDE"])
    p.emit("g", lambda e: e.affine_select(g["IDE"][:], g["IDE"][:], pattern=[[-1, 128]],
                                          compare_op=Alu.not_equal, fill=1.0, base=0,
                                          channel_multiplier=1), reads=["IDE"], writes=["IDE"])
    M0 = MSKB[:, 0:128]
    M4 = MSKB[:, 128:256]
    p.emit("g", lambda e: e.memset(M0, 0.0), writes=["M0"])
    p.emit("g", lambda e: e.affine_select(M0, M0, pattern=[[-1, 128]], compare_op=Alu.is_ge,
                                          fill=-1e30, base=0, channel_multiplier=1),
           reads=["M0"], writes=["M0"])
    p.emit("g", lambda e: e.memset(M4, 0.0), writes=["M4"])
    p.emit("g", lambda e: e.affine_select(M4, M4, pattern=[[1, 128]], compare_op=Alu.is_ge,
                                          fill=-1e30, base=-1, channel_multiplier=-1),
           reads=["M4"], writes=["M4"])
    consts = {"c2": (0, 2.0), "c5": (1, 5.0), "c1": (2, 1.0), "c15": (3, 15.0),
              "c128": (4, 128.0), "cth": (5, 10000.0)}
    for nm, (i, vv) in consts.items():
        g[nm] = CST[:, i:i + 1]
        p.emit("g", lambda e, col=g[nm], vv=vv: e.memset(col, vv), writes=["CST"])
    c04 = CST[:, 8:9]; cstep = CST[:, 9:10]
    g["rs128"] = CST[:, 10:11]
    sq128 = CST[:, 11:12]
    r5 = CST[:, 12:13]
    g["r15"] = CST[:, 13:14]
    p.emit("v", lambda e: e.reciprocal(r5, g["c5"]), reads=["CST"], writes=["CST"])
    p.emit("v", lambda e: e.reciprocal(g["r15"], g["c15"]), reads=["CST"], writes=["CST"])
    p.emit("v", lambda e: e.tensor_tensor(c04, g["c2"], r5, Alu.mult),
           reads=["CST"], writes=["CST"])
    p.emit("v", lambda e: e.tensor_tensor(cstep, g["c1"], c04, Alu.subtract),
           reads=["CST"], writes=["CST"])
    p.emit("v", lambda e: e.tensor_tensor(cstep, cstep, g["r15"], Alu.mult),
           reads=["CST"], writes=["CST"])
    p.emit("s", lambda e: e.activation(sq128, g["c128"], AFT.Sqrt),
           reads=["CST"], writes=["CST"])
    p.emit("v", lambda e: e.reciprocal(g["rs128"], sq128), reads=["CST"], writes=["CST"])
    SH_S = SHR[:, 0:16]; SH_I = SHR[:, 16:32]; kio = SHR[:, 32:48]
    g["SH_S"], g["SH_I"] = SH_S, SH_I
    g["SH_S2"] = SHR[:, 16:32]
    p.emit("g", lambda e: e.iota(kio, pattern=[[1, 16]], base=0, channel_multiplier=0,
                                 allow_small_or_imprecise_dtypes=True), writes=["SHR"])
    p.emit("v", lambda e: e.tensor_scalar(SH_S, kio, cstep, c04, Alu.mult, Alu.add),
           reads=["SHR", "CST"], writes=["SHR"])
    p.emit("g", lambda e: e.memset(SH_S[:, 15:16], 1.0), reads=["SHR"], writes=["SHR"])
    p.emit("v", lambda e: e.tensor_tensor(g["SH_S2"], SH_S, SH_S, Alu.mult),
           reads=["SHR"], writes=["SHR"])
    g["nr15"] = CST[:, 15:16]
    p.emit("v", lambda e: e.tensor_single_scalar(g["nr15"], g["r15"], -1.0, Alu.mult),
           reads=["CST"], writes=["CST"])
    lnth32 = CST[0:32, 16:17]
    p.emit("s", lambda e: e.activation(lnth32, g["cth"][0:32, :], AFT.Ln),
           reads=["CST"], writes=["CST"])
    icol = CST[0:32, 20:21]
    p.emit("g", lambda e: e.iota(icol, pattern=[[0, 1]], base=0, channel_multiplier=1,
                                 allow_small_or_imprecise_dtypes=True),
           reads=["CST"], writes=["CST"])
    p.emit("v", lambda e: e.tensor_single_scalar(icol, icol, -1.0 / 32.0, Alu.mult),
           reads=["CST"], writes=["CST"])
    p.emit("v", lambda e: e.tensor_tensor(icol, icol, lnth32, Alu.mult),
           reads=["CST"], writes=["CST"])
    p.emit("s", lambda e: e.activation(icol, icol, AFT.Exp), reads=["CST"], writes=["CST"])
    g["invcol"] = icol
    g["CSTinv_dummy"] = None
    p.emit("v", lambda e: e.tensor_copy(CST[0:32, 22:23], icol), reads=["CST"], writes=["CSTinv"])
    g["invcol"] = CST[0:32, 22:23]
    g["koff"] = CST[:, 14:15]
    p.dma("sp", g["koff"], g["posmeta"][:], reads=["CST"], writes=["CSTko"])
    g["onescol_b"] = CSTB[0:128, 0:1]
    p.emit("g", lambda e: e.memset(g["onescol_b"], 1.0), writes=["CSTB"])
    g["onesrow_b"] = CSTB[0:1, 4:132]
    p.emit("g", lambda e: e.memset(g["onesrow_b"], 1.0), reads=["CSTB"], writes=["CSTB"])
    g["sink_s"] = CST[:, 168:184]
    p.dma("sp", g["sink_s"], g["sinkrep"], writes=["CSTsink"])
    g["tilemask_s"] = CST[:, 104:168]
    p.dma("sp", g["tilemask_s"], g["tilemask"], writes=["CSTtm"])
    g["bqT_s"] = CST[:, 184:200]
    g["bkT_s"] = CST[:, 200:216]
    g["bvT_s"] = CST[:, 216:232]
    p.dma("sp", g["bqT_s"], g["bqT"], writes=["CSTbias"])
    p.dma("sp", g["bkT_s"], g["bkT"], writes=["CSTbias"])
    p.dma("sp", g["bvT_s"], g["bvT"], writes=["CSTbias"])
    g["negsink"] = CST[:, 64:80]
    p.emit("v", lambda e: e.tensor_single_scalar(g["negsink"], g["sink_s"], -1.0, Alu.mult),
           reads=["CSTsink"], writes=["CST"])
    qnwr = CST[0:1, 240:368]; knwr = CST[0:1, 368:496]; tmpr = g["SMALL"][0:1, 0:128]
    p.dma("sp", qnwr, g["q_norm_w"], reads=["CST"], writes=["CSTnw"])
    p.dma("sp", knwr, g["k_norm_w"], reads=["CSTnw"], writes=["CSTnw"])
    p.emit("v", lambda e: e.tensor_single_scalar(qnwr, qnwr, g["rs128"][0:1, 0:1], Alu.mult),
           reads=["CSTnw", "CST"], writes=["CSTnw"])
    CB = g["CSTB"]
    g["qnw_hi"], g["qnw_lo"] = CB[0:1, 132:260], CB[0:1, 260:388]
    g["knw_hi"], g["knw_lo"] = CB[0:1, 388:516], CB[0:1, 516:644]
    for srcr, hi, lo in ((qnwr, g["qnw_hi"], g["qnw_lo"]),
                         (knwr, g["knw_hi"], g["knw_lo"])):
        p.emit("v", lambda e, srcr=srcr, hi=hi: e.tensor_copy(hi, srcr),
               reads=["CSTnw"], writes=["QNWr"])
        p.emit("v", lambda e, srcr=srcr, hi=hi: e.tensor_tensor(
            tmpr, srcr, hi, Alu.subtract),
               reads=["CSTnw", "QNWr"], writes=["SMALL"])
        p.emit("v", lambda e, lo=lo: e.tensor_copy(lo, tmpr), reads=["SMALL"], writes=["QNWr"])
    if STAGE >= 4:
        for gi in range(4):
            p.dma("g", g["w1bf"][gi], g["out_w1"][gi], writes=[f"w1bf{gi}"])
        for mo in range(NCH):
            p.dma("g", g["w2bf"][mo], g["out_w2"][:, ts(mo, 128)], writes=[f"w2bf{mo}"])


def rope_tables(nc, p, g, pos_lo, n):
    RSC = g["RSC"]
    ang = RSC[0:32, 0:n]
    kkf = g["SMALL"][0:32, 0:n]
    kki = g["KKI"][0:32, 0:n]
    p.emit("g", lambda e: e.iota(ang, pattern=[[1, n]], base=0, channel_multiplier=0,
                                 allow_small_or_imprecise_dtypes=True), writes=["RSC"])
    p.emit("v", lambda e: e.tensor_single_scalar(ang, ang, float(pos_lo), Alu.add),
           reads=["RSC"], writes=["RSC"])
    p.emit("v", lambda e: e.tensor_single_scalar(ang, ang, g["koff"][0:32, 0:1], Alu.add),
           reads=["RSC", "CSTko"], writes=["RSC"])
    p.emit("v", lambda e: e.tensor_single_scalar(ang, ang, g["invcol"], Alu.mult),
           reads=["RSC", "CSTinv"], writes=["RSC"])
    p.emit("s", lambda e: e.activation(kki, ang, AFT.Copy, bias=0.0, scale=INV2PI),
           reads=["RSC"], writes=["KKI"])
    p.emit("v", lambda e: e.tensor_copy(kkf, kki), reads=["KKI"], writes=["SMALL"])
    for c in (RC1, RC2, RC3):
        p.emit("v", lambda e, c=c: e.scalar_tensor_tensor(ang, kkf, -c, ang, Alu.mult, Alu.add),
               reads=["RSC", "SMALL"], writes=["RSC"])
    p.emit("s", lambda e: e.activation(g["ROPE"][32:64, 0:n], ang, AFT.Sin),
           reads=["RSC"], writes=["rope_sin"])
    p.emit("v", lambda e: e.tensor_single_scalar(ang, ang, PIO2, Alu.add),
           reads=["RSC"], writes=["RSC"])
    p.emit("v", lambda e: e.tensor_single_scalar(kkf, ang, PI_F, Alu.is_gt),
           reads=["RSC"], writes=["SMALL"])
    p.emit("v", lambda e: e.scalar_tensor_tensor(ang, kkf, -RC1, ang, Alu.mult, Alu.add),
           reads=["RSC", "SMALL"], writes=["RSC"])
    p.emit("s", lambda e: e.activation(g["ROPE"][0:32, 0:n], ang, AFT.Sin),
           reads=["RSC"], writes=["rope_cos"])


def ln_stats_bcast(nc, p, g, n, sum_ps, sum_nm, sq_ps, sq_nm, m_ps, m_nm, r_ps, r_nm):
    mrow = g["RSC"][0:1, 0:n]
    vrow = g["SMALL"][0:1, 0:n]
    m2row = g["ROWF"][0:1, 0:n]
    p.emit("v", lambda e: e.tensor_single_scalar(mrow, sum_ps[0:1, 0:n], 1.0 / D, Alu.mult),
           reads=[sum_nm, sq_nm], writes=["RSC"])
    p.emit("v", lambda e: e.tensor_single_scalar(vrow, sq_ps[0:1, 0:n], 1.0 / D, Alu.mult),
           reads=[sum_nm, sq_nm], writes=["SMALL"])
    p.emit("s", lambda e: e.activation(m2row, mrow, AFT.Square),
           reads=["RSC"], writes=["ROWF"])
    p.emit("v", lambda e: e.tensor_tensor(vrow, vrow, m2row, Alu.subtract),
           reads=["SMALL", "ROWF"], writes=["SMALL"])
    p.emit("s", lambda e: e.activation(vrow, vrow, AFT.Sqrt, bias=1e-5, scale=1.0),
           reads=["SMALL", "CEPS"], writes=["SMALL"])
    p.emit("v", lambda e: e.reciprocal(vrow, vrow), reads=["SMALL"], writes=["SMALL"])
    bcast_row(nc, p, g, mrow, "RSC", m_ps, m_nm, n)
    bcast_row(nc, p, g, vrow, "SMALL", r_ps, r_nm, n)


def bcast_row(nc, p, g, row, row_nm, dest_ps, dest_nm, n):
    onesb = g["onesrow_b"][0:1, 0:128]
    bh = g["SC1"][0:1, 0:n]
    bl = g["SC2"][0:1, 0:n]
    p.emit("v", lambda e: e.tensor_copy(bh, row), reads=[row_nm], writes=["SC1"])
    p.emit("v", lambda e: e.tensor_tensor(bl, row, bh, Alu.subtract),
           reads=[row_nm, "SC1"], writes=["SC2"])
    for half in range((n + 511) // 512):
        sl = slice(half * 512, min(n, half * 512 + 512))
        p.emit("pe", lambda e, sl=sl: e.matmul(dest_ps[:, sl], onesb, bh[:, sl],
                                               start=True, stop=False),
               reads=["SC1", "CSTB"], writes=[dest_nm])
        p.emit("pe", lambda e, sl=sl: e.matmul(dest_ps[:, sl], onesb, bl[:, sl],
                                               start=False, stop=True),
               reads=["SC2", "CSTB"], writes=[dest_nm])


def layer_norm(nc, p, g, dst, tag):
    A_x, ps_a, ps_b, ps_c = g["A_x"], g["ps_a"], g["ps_b"], g["ps_c"]
    onescol = g["onescol_b"]
    for ch in range(NCH):
        xc = A_x[:, ts(ch, T)]
        xb = g["SC1"][:, 0:T]
        p.emit("v", lambda e, xc=xc, xb=xb: e.tensor_copy(xb, xc),
               reads=[f"x{ch}"], writes=["SC1"])
        for hf in range(2):
            shf = slice(hf * 512, hf * 512 + 512)
            p.emit("pe", lambda e, xb=xb, ch=ch, shf=shf: e.matmul(
                ps_c[0:1, shf], onescol, xb[:, shf],
                start=(ch == 0), stop=(ch == NCH - 1)),
                reads=["SC1", "CSTB"], writes=["ps_c"])
        x2 = g["SC2"][:, 0:T]
        p.emit("s", lambda e, xc=xc, x2=x2: e.activation(x2, xc, AFT.Square),
               reads=[f"x{ch}"], writes=["SC2"])
        for hf in range(2):
            shf = slice(hf * 512, hf * 512 + 512)
            p.emit("pe", lambda e, x2=x2, ch=ch, shf=shf: e.matmul(
                ps_b[0:1, shf], onescol, x2[:, shf],
                start=(ch == 0), stop=(ch == NCH - 1)),
                reads=["SC2", "CSTB"], writes=["ps_b"])
    ln_stats_bcast(nc, p, g, T, ps_c, "ps_c", ps_b, "ps_b", ps_a, "ps_a", ps_c, "ps_c")
    for ch in range(NCH):
        xc = A_x[:, ts(ch, T)]
        oc = dst[:, ts(ch, T)]
        tmpf = g["RSC"][:, 0:T]
        p.emit("v", lambda e, xc=xc: e.tensor_tensor(tmpf, xc, ps_a[:, 0:T], Alu.subtract),
               reads=[f"x{ch}", "ps_a"], writes=["RSC"])
        p.emit("v", lambda e, oc=oc: e.tensor_tensor(oc, tmpf, ps_c[:, 0:T], Alu.mult),
               reads=["RSC", "ps_c"], writes=[f"{tag}_{ch}", f"AB1:{ch}"])


def stream_colblock(nc, p, g, w_ap, wname, kdim, m, slot):
    STR = g["STR"]
    nk = kdim // 128
    slab = STR[:, slot * 2048: slot * 2048 + nk * 128]
    dst3 = slab.rearrange("p (k m) -> p k m", k=nk)
    src = w_ap[:, ts(m, 128)].rearrange("(k p) m -> p k m", p=128)
    p.dma("g", dst3, src, reads=[wname], writes=[f"STR{slot}"])
    return [slab[:, ts(k, 128)] for k in range(nk)]


def rms_head(nc, p, g, src, src_nm, dst, dst_nm, whi, wlo, n, stat_ps, stat_nm, bc_ps, bc_nm):
    sq = g["SC1"][:, 0:n]
    p.emit("s", lambda e: e.activation(sq, src[:, 0:n], AFT.Square),
           reads=[src_nm], writes=["SC1"])
    for half in range((n + 511) // 512):
        sl = slice(half * 512, min(n, half * 512 + 512))
        p.emit("pe", lambda e, sl=sl: e.matmul(stat_ps[0:1, sl], g["onescol_b"], sq[:, sl],
                                               start=True, stop=True),
               reads=["SC1", "CSTB"], writes=[stat_nm])
    rrow = g["ROWF"][0:1, 0:n]
    p.emit("s", lambda e: e.activation(rrow, stat_ps[0:1, 0:n], AFT.Sqrt, bias=1e-6,
                                       scale=1.0 / 128.0),
           reads=[stat_nm, "CEPS"], writes=["ROWF"])
    p.emit("v", lambda e: e.reciprocal(rrow, rrow), reads=["ROWF"], writes=["ROWF"])
    bh = g["SC1"][0:1, 0:n]
    bl = g["SC2"][0:1, 0:n]
    p.emit("v", lambda e: e.tensor_copy(bh, rrow), reads=["ROWF"], writes=["SC1"])
    p.emit("v", lambda e: e.tensor_tensor(bl, rrow, bh, Alu.subtract),
           reads=["ROWF", "SC1"], writes=["SC2"])
    for half in range((n + 511) // 512):
        sl = slice(half * 512, min(n, half * 512 + 512))
        p.emit("pe", lambda e, sl=sl: e.matmul(bc_ps[:, sl], whi, bh[:, sl],
                                               start=True, stop=False),
               reads=["SC1", "QNWr"], writes=[bc_nm])
        p.emit("pe", lambda e, sl=sl: e.matmul(bc_ps[:, sl], whi, bl[:, sl],
                                               start=False, stop=False),
               reads=["SC2", "QNWr"], writes=[bc_nm])
        p.emit("pe", lambda e, sl=sl: e.matmul(bc_ps[:, sl], wlo, bh[:, sl],
                                               start=False, stop=True),
               reads=["SC1", "QNWr"], writes=[bc_nm])
    p.emit("v", lambda e: e.tensor_tensor(dst[:, 0:n], src[:, 0:n], bc_ps[:, 0:n], Alu.mult),
           reads=[src_nm, bc_nm], writes=[dst_nm])


def rope_apply(nc, p, g, ps, ps_nm, out, out_nm, n):
    ROPE, RSC = g["ROPE"], g["RSC"]
    cos = ROPE[0:32, 0:n]
    sin = ROPE[32:64, 0:n]
    p.emit("v", lambda e: e.tensor_tensor(out[0:32, 0:n], ps[0:32, 0:n], cos, Alu.mult),
           reads=[ps_nm, "rope_cos"], writes=[out_nm])
    p.emit("v", lambda e: e.tensor_tensor(RSC[0:32, 0:n], ps[32:64, 0:n], sin, Alu.mult),
           reads=[ps_nm, "rope_sin"], writes=["RSC"])
    p.emit("v", lambda e: e.tensor_tensor(out[0:32, 0:n], out[0:32, 0:n], RSC[0:32, 0:n],
                                          Alu.subtract),
           reads=[out_nm, "RSC"], writes=[out_nm])
    p.emit("v", lambda e: e.tensor_tensor(out[32:64, 0:n], ps[0:32, 0:n], sin, Alu.mult),
           reads=[ps_nm, "rope_sin"], writes=[out_nm])
    p.emit("v", lambda e: e.tensor_tensor(RSC[32:64, 0:n], ps[32:64, 0:n], cos, Alu.mult),
           reads=[ps_nm, "rope_cos"], writes=["RSC"])
    p.emit("v", lambda e: e.tensor_tensor(out[32:64, 0:n], out[32:64, 0:n], RSC[32:64, 0:n],
                                          Alu.add),
           reads=[out_nm, "RSC"], writes=[out_nm])
    p.emit("v", lambda e: e.tensor_copy(out[64:128, 0:n], ps[64:128, 0:n]),
           reads=[ps_nm], writes=[out_nm])


def attn_a_qkv(nc, p, g, pump):
    AB1, AB2, KVB = g["AB1"], g["AB2"], g["KVB"]
    ps_a = g["ps_a"]
    for m in range(4):
        stiles = stream_colblock(nc, p, g, g["wq_comp"], "wq_comp", D, m, m % 2)
        pq, pqn = (ps_a, "ps_a") if m % 2 == 0 else (g["ps_b"], "ps_b")
        for half in range(2):
            sl = slice(half * 512, half * 512 + 512)
            for kc in range(NCH):
                mov = AB1[:, ts(kc, T)][:, sl]
                p.emit("pe", lambda e, stiles=stiles, kc=kc, mov=mov, sl=sl, pq=pq: e.matmul(
                    pq[:, sl], stiles[kc], mov, start=(kc == 0), stop=(kc == NCH - 1)),
                    reads=[f"STR{m % 2}", f"ln1_{kc}", f"AB1:{kc}"], writes=[pqn])
        qc = KVB[:, ts(m, T)]
        p.emit("v", lambda e, qc=qc, pq=pq: e.tensor_copy(qc, pq[:, 0:T]),
               reads=[pqn], writes=[f"qc{m}", "KVB"])
        pump(4)
    rope_tables(nc, p, g, 512, 1024)
    for h in range(NHEAD):
        stiles = stream_colblock(nc, p, g, g["wq_up"], "wq_up", QC, h, h % 2)
        pq, pqn = (ps_a, "ps_a") if h % 2 == 0 else (g["ps_b"], "ps_b")
        for half in range(2):
            sl = slice(half * 512, half * 512 + 512)
            for kq in range(4):
                p.emit("pe", lambda e, stiles=stiles, kq=kq, sl=sl, pq=pq: e.matmul(
                    pq[:, sl], stiles[kq], KVB[:, ts(kq, T)][:, sl],
                    start=(kq == 0), stop=(kq == 3)),
                    reads=[f"STR{h % 2}", f"qc{kq}"], writes=[pqn])
        rope_apply(nc, p, g, pq, pqn, g["SMALL"], "SMALL", T)
        rms_head(nc, p, g, g["SMALL"], "SMALL", AB2[:, ts(h, T)], f"q{h}",
                 g["qnw_hi"], g["qnw_lo"], T, g["ps_c"], "ps_c", g["ps_c"], "ps_c")
        pump(6)


def attn_a_scores(nc, p, g, pump):
    AB1, AB2, KVB, VTM = g["AB1"], g["AB2"], g["KVB"], g["VTM"]
    ps_a, ps_b, ps_c, ps_t = g["ps_a"], g["ps_b"], g["ps_c"], g["ps_t"]
    RSC = g["RSC"]
    kslab = stream_colblock(nc, p, g, g["wk"], "wk", D, 0, 0)
    vslab = stream_colblock(nc, p, g, g["wv"], "wv", D, 0, 1)
    for ch in range(NCH):
        slot = ch % 2
        xh = RSC[:, slot * 512:(slot + 1) * 512]
        p.dma("sp", xh, g["x_halo"][ts(ch, 128), :], writes=["RSC"])
        xb = g["SC1"][:, 0:512]
        p.emit("v", lambda e, xh=xh, xb=xb: e.tensor_copy(xb, xh),
               reads=["RSC"], writes=["SC1"])
        p.emit("pe", lambda e, xb=xb, ch=ch: e.matmul(ps_c[0:1, 0:512], g["onescol_b"], xb,
                                                      start=(ch == 0), stop=(ch == NCH - 1)),
               reads=["SC1", "CSTB"], writes=["ps_c"])
        x2 = g["SC2"][:, 0:512]
        p.emit("s", lambda e, xh=xh, x2=x2: e.activation(x2, xh, AFT.Square),
               reads=["RSC"], writes=["SC2"])
        p.emit("pe", lambda e, x2=x2, ch=ch: e.matmul(ps_b[0:1, 0:512], g["onescol_b"], x2,
                                                      start=(ch == 0), stop=(ch == NCH - 1)),
               reads=["SC2", "CSTB"], writes=["ps_b"])
    ln_stats_bcast(nc, p, g, 512, ps_c, "ps_c", ps_b, "ps_b", ps_a, "ps_a", ps_b, "ps_b")
    for blk in range(3):
        rope_tables(nc, p, g, blk * 512, 512)
        for kc in range(NCH):
            if blk == 0:
                slot = kc % 2
                xh = RSC[:, slot * 512:(slot + 1) * 512]
                p.dma("sp", xh, g["x_halo"][ts(kc, 128), :], writes=["RSC"])
                hl = g["SC1"][:, 0:512] if slot == 0 else g["SC1"][:, 512:1024]
                tmq = g["SMALL"][:, 0:512]
                p.emit("v", lambda e, xh=xh, tmq=tmq: e.tensor_tensor(
                    tmq, xh, ps_a[:, 0:512], Alu.subtract),
                    reads=["RSC", "ps_a"], writes=["SMALL"])
                p.emit("v", lambda e, hl=hl, tmq=tmq: e.tensor_tensor(
                    hl, tmq, ps_b[:, 0:512], Alu.mult),
                    reads=["SMALL", "ps_b"], writes=["SC1"])
                mov, mnm = hl, ["SC1"]
            else:
                off = (blk - 1) * 512
                mov = AB1[:, ts(kc, T)][:, off:off + 512]
                mnm = [f"ln1_{kc}", f"AB1:{kc}"]
            p.emit("pe", lambda e, mov=mov, kc=kc: e.matmul(
                ps_c[:, 0:512], kslab[kc], mov, start=(kc == 0), stop=(kc == NCH - 1)),
                reads=mnm + ["STR0"], writes=["ps_c"])
            p.emit("pe", lambda e, mov=mov, kc=kc: e.matmul(
                ps_c[:, 512:1024], vslab[kc], mov, start=(kc == 0), stop=(kc == NCH - 1)),
                reads=mnm + ["STR1"], writes=["ps_c"])
        rope_apply(nc, p, g, ps_c[:, 0:512], "ps_c", g["SMALL"], "SMALL", 512)
        vb = g["SC2"][:, 0:512]
        p.emit("v", lambda e: e.tensor_copy(vb, ps_c[:, 512:1024]),
               reads=["ps_c"], writes=["SC2"])
        for i in range(4):
            p.emit("pe", lambda e, i=i: e.transpose(ps_t[:, ts(i, 128)], vb[:, ts(i, 128)],
                                                    g["IDE"][:]),
                   reads=["SC2", "IDE"], writes=["ps_t"])
        p.emit("v", lambda e, blk=blk: e.tensor_copy(VTM[:, blk * 512:(blk + 1) * 512],
                                                     ps_t[:, 0:512]),
               reads=["ps_t"], writes=["VTM"])
        rms_head(nc, p, g, g["SMALL"], "SMALL", KVB[:, blk * 512:(blk + 1) * 512],
                 "KVB", g["knw_hi"], g["knw_lo"], 512,
                 g["ps_c"], "ps_c", g["ps_b"], "ps_b")
        pump(6)
    for h in range(NHEAD):
        outb = ps_b if h % 2 else ps_a
        outbnm = "ps_b" if h % 2 else "ps_a"
        for qt in range(NQT):
            klo = qt * 128
            stat = AB2[:, ts(h, T)][:, ts(qt, 128)]
            p.emit("pe", lambda e, stat=stat, klo=klo: e.matmul(
                ps_c[:, 0:512], stat, KVB[:, klo:klo + 512], start=True, stop=True),
                reads=[f"q{h}", "KVB"], writes=["ps_c"])
            p.emit("pe", lambda e, stat=stat, klo=klo: e.matmul(
                ps_c[:, 512:640], stat, KVB[:, klo + 512:klo + 640], start=True, stop=True),
                reads=[f"q{h}", "KVB"], writes=["ps_c"])
            p.emit("v", lambda e: e.tensor_tensor(ps_c[:, 0:128], ps_c[:, 0:128],
                                                  g["MSKB"][:, 128:256], Alu.add),
                   reads=["ps_c", "M4"], writes=["ps_c"])
            p.emit("v", lambda e: e.tensor_tensor(ps_c[:, 512:640], ps_c[:, 512:640],
                                                  g["MSKB"][:, 0:128], Alu.add),
                   reads=["ps_c", "M0"], writes=["ps_c"])
            for pos in range(4):
                tmcol = g["tilemask_s"][:, qt * 8 + pos:qt * 8 + pos + 1]
                p.emit("v", lambda e, pos=pos, tmcol=tmcol: e.tensor_single_scalar(
                    ps_c[:, pos * 128:(pos + 1) * 128], ps_c[:, pos * 128:(pos + 1) * 128],
                    tmcol, Alu.add),
                    reads=["ps_c", "CSTtm"], writes=["ps_c"])
            nmax = g["CST"][:, 88:89]
            p.emit("v", lambda e: e.tensor_reduce(nmax, ps_c[:, 0:640], AX.X, Alu.max,
                                                  negate=True),
                   reads=["ps_c"], writes=["CSTn"])
            nsk = g["negsink"][:, h:h + 1]
            p.emit("v", lambda e, nsk=nsk: e.tensor_single_scalar(nmax, nmax, nsk, Alu.min),
                   reads=["CSTn", "CST"], writes=["CSTn"])
            Pr = g["SC1"][:, 0:640]
            srow = g["CST"][:, 89:90]
            p.emit("s", lambda e: e.activation(Pr, ps_c[:, 0:640], AFT.Exp, bias=nmax,
                                               scale=1.0, accum_out=srow),
                   reads=["ps_c", "CSTn"], writes=["SC1", "CSTs"])
            esk = g["CST"][:, 90:91]
            snk = g["sink_s"][:, h:h + 1]
            p.emit("s", lambda e, snk=snk: e.activation(esk, snk, AFT.Exp, bias=nmax, scale=1.0),
                   reads=["CSTn"], writes=["CSTe"])
            p.emit("v", lambda e: e.tensor_tensor(srow, srow, esk, Alu.add),
                   reads=["CSTs", "CSTe"], writes=["CSTs"])
            p.emit("v", lambda e: e.reciprocal(srow, srow), reads=["CSTs"], writes=["CSTs"])
            p.emit("v", lambda e: e.tensor_single_scalar(Pr, Pr, srow, Alu.mult),
                   reads=["SC1", "CSTs"], writes=["SC1"])
            for i in range(5):
                p.emit("pe", lambda e, i=i: e.transpose(ps_t[:, ts(i, 128)], Pr[:, ts(i, 128)],
                                                        g["IDE"][:]),
                       reads=["SC1", "IDE"], writes=["ps_t"])
            p.emit("v", lambda e: e.tensor_copy(g["SC2"][:, 0:640], ps_t[:, 0:640]),
                   reads=["ps_t"], writes=["SC2"])
            for i in range(5):
                kt = qt + i
                p.emit("pe", lambda e, i=i, kt=kt, outb=outb, qt=qt: e.matmul(
                    outb[:, ts(qt, 128)], VTM[:, ts(kt, 128)], g["SC2"][:, ts(i, 128)],
                    start=(i == 0), stop=(i == 4)),
                    reads=["SC2", "VTM"], writes=[outbnm])
        ao = AB1[:, ts(h, T)]
        p.emit("v", lambda e, ao=ao, outb=outb: e.tensor_copy(ao, outb[:, 0:T]),
               reads=[outbnm], writes=[f"ao{h}", f"AB1:{h}"])
        pump(8)


def attn_a_out(nc, p, g, pump):
    AB1, AB2, STR = g["AB1"], g["AB2"], g["STR"]
    ps_a, ps_b = g["ps_a"], g["ps_b"]
    for sub in range(T // TSUB1):
        sl = slice(sub * TSUB1, (sub + 1) * TSUB1)
        for gi in range(4):
            for mi in range(NCH):
                slot = (gi * NCH + mi) % 2
                slab = STR[:, slot * 2048:slot * 2048 + 512]
                dst3 = slab.rearrange("p (k m) -> p k m", k=4)
                src = g["w1bf"][gi][:, ts(mi, 128)].rearrange("(k p) m -> p k m", p=128)
                p.dma("sp", dst3, src, reads=[f"w1bf{gi}"], writes=[f"STR{slot}"])
                stiles = [slab[:, ts(_k, 128)] for _k in range(4)]
                ic = gi * NCH + mi
                pp, ppn = (ps_a, "ps_a") if ic % 2 == 0 else (g["ps_c"], "ps_c")
                for kg in range(4):
                    hh = 4 * gi + kg
                    mov = AB1[:, ts(hh, T)][:, sl]
                    p.emit("pe", lambda e, stiles=stiles, kg=kg, mov=mov, pp=pp: e.matmul(
                        pp[:, 0:TSUB1], stiles[kg], mov, start=(kg == 0), stop=(kg == 3)),
                        reads=[f"STR{slot}", f"ao{hh}", f"AB1:{hh}"], writes=[ppn])
                dst = AB2[:, ic * TSUB1:(ic + 1) * TSUB1]
                p.emit("v", lambda e, dst=dst, pp=pp: e.tensor_copy(dst, pp[:, 0:TSUB1]),
                       reads=[ppn], writes=[f"ic{ic}", f"AB2:{ic // 4}"])
            pump(2)
        for mo in range(NCH):
            for quarter in range(4):
                slot = quarter % 2
                slab = STR[:, slot * 2048:slot * 2048 + 2048]
                dst3 = slab.rearrange("p (k m) -> p k m", k=16)
                src = g["w2bf"][mo][quarter * 2048:(quarter + 1) * 2048, :].rearrange(
                    "(k p) m -> p k m", p=128)
                p.dma("sp", dst3, src, reads=[f"w2bf{mo}"], writes=[f"STR{slot}"])
                stiles = [slab[:, ts(_k, 128)] for _k in range(16)]
                pp, ppn = (ps_b, "ps_b") if mo % 2 == 0 else (g["ps_s"], "ps_s")
                for kq in range(16):
                    ki = quarter * 16 + kq
                    mov = AB2[:, ki * TSUB1:(ki + 1) * TSUB1]
                    p.emit("pe", lambda e, stiles=stiles, kq=kq, mov=mov, ki=ki, pp=pp: e.matmul(
                        pp[:, 0:TSUB1], stiles[kq], mov, start=(ki == 0), stop=(ki == 63)),
                        reads=[f"STR{slot}", f"ic{ki}", f"AB2:{ki // 4}"], writes=[ppn])
            xc = g["A_x"][:, ts(mo, T)][:, sl]
            p.emit("v", lambda e, xc=xc, pp=pp: e.tensor_tensor(xc, xc, pp[:, 0:TSUB1], Alu.add),
                   reads=[ppn, f"x{mo}"], writes=[f"x{mo}"])
            pump(2)


def quant_generator(nc, p, g):
    qw, qA, qB, qI, qdq, qT, qBK = (g["qw"], g["qA"], g["qB"],
                                    g["qI"], g["qdq"], g["qT"], g["qBK"])
    SH_S, SH_S2 = g["SH_S"], g["SH_S2"]
    ps_t = g["ps_t"]
    chunk_srcs = []
    for cch in range(8):
        for half in range(2):
            chunk_srcs.append((g["wfc_shard"][ts(cch, 128), half * 1024:(half + 1) * 1024],
                               cch * 2048 * 128, 2048, half))
    for cch in range(2):
        for part in range(8):
            chunk_srcs.append((g["wproj_shard"][ts(cch, 128), part * 1024:(part + 1) * 1024],
                               8 * 2048 * 128 + cch * 8192 * 128, 8192, part))
    NB = 16
    amax = qBK[:, 0:NB]; base_ = qBK[:, NB:2 * NB]; scl = qBK[:, 2 * NB:3 * NB]
    isc = qBK[:, 3 * NB:4 * NB]; bscore = qBK[:, 4 * NB:5 * NB]
    tmpb = qBK[:, 5 * NB:6 * NB]; nres = qBK[:, 6 * NB:7 * NB]; sbest = qBK[:, 7 * NB:8 * NB]
    bsc2 = qBK[:, 8 * NB:9 * NB]
    qcin_flat = g["q_cin"].rearrange("a b -> (a b)")
    for ci, (src, off, cw, part) in enumerate(chunk_srcs):
        slot = ci % 2
        wc = qw[:, slot * 1024:(slot + 1) * 1024]
        wc3 = wc.rearrange("p (nb b) -> p nb b", b=64)
        p.dma("sp", wc, src, writes=[f"qw{slot}"])
        yield
        p.emit("v", lambda e, wc3=wc3: e.tensor_reduce(amax, wc3, AX.X, Alu.max,
                                                       apply_absolute_value=True),
               reads=[f"qw{slot}"], writes=["qBK"])
        p.emit("v", lambda e: e.tensor_single_scalar(amax, amax, 1e-8, Alu.max),
               reads=["qBK"], writes=["qBK"])
        p.emit("v", lambda e: e.tensor_single_scalar(base_, amax, g["r15"], Alu.mult),
               reads=["qBK", "CST"], writes=["qBK"])
        p.emit("g", lambda e: e.memset(bscore, 3.0e38), reads=["qBK"], writes=["qBK"])
        yield
        for si in range(16):
            sS = SH_S[:, si:si + 1]
            s2 = SH_S2[:, si:si + 1]
            p.emit("v", lambda e, sS=sS: e.tensor_single_scalar(scl, base_, sS, Alu.mult),
                   reads=["qBK", "SHR"], writes=["qBK"])
            p.emit("v", lambda e: e.reciprocal(isc, scl), reads=["qBK"], writes=["qBK"])
            ib = isc.unsqueeze(2).broadcast_to([128, NB, 64])
            p.emit("v", lambda e, wc3=wc3, ib=ib: e.tensor_tensor(
                qB[:].rearrange("p (nb b) -> p nb b", b=64), wc3, ib, Alu.mult),
                reads=[f"qw{slot}", "qBK"], writes=["qB"])
            p.emit("s", lambda e: e.activation(qI[:], qB[:], AFT.Copy),
                   reads=["qB"], writes=["qI"])
            p.emit("v", lambda e: e.tensor_scalar(qA[:], qI[:], 15.0, -15.0, Alu.min, Alu.max),
                   reads=["qI"], writes=["qA"])
            p.emit("v", lambda e: e.tensor_tensor(qA[:], qA[:], qB[:], Alu.subtract),
                   reads=["qA", "qB"], writes=["qA"])
            p.emit("s", lambda e: e.activation(qA[:], qA[:], AFT.Square),
                   reads=["qA"], writes=["qA"])
            p.emit("v", lambda e: e.tensor_reduce(
                tmpb, qA[:].rearrange("p (nb b) -> p nb b", b=64), AX.X, Alu.add),
                reads=["qA"], writes=["qBK"])
            p.emit("v", lambda e, s2=s2: e.tensor_single_scalar(tmpb, tmpb, s2, Alu.mult),
                   reads=["qBK", "SHR"], writes=["qBK"])
            if si == 0:
                p.emit("v", lambda e, sS=sS: e.tensor_copy(sbest, sS.broadcast_to([128, NB])),
                       reads=["SHR", "qBK"], writes=["qBK"])
                p.emit("v", lambda e: e.tensor_copy(bscore, tmpb),
                       reads=["qBK"], writes=["qBK"])
            else:
                p.emit("v", lambda e: e.tensor_tensor(g["QUPD"][:], tmpb, bscore, Alu.is_lt),
                       reads=["qBK"], writes=["QUPD"])
                p.emit("v", lambda e, sS=sS: e.select(sbest, g["QUPD"][:],
                                                      sS.broadcast_to([128, NB]), sbest),
                       reads=["qBK", "SHR", "QUPD"], writes=["qBK"])
                p.emit("v", lambda e: e.tensor_tensor(bscore, tmpb, bscore, Alu.min),
                       reads=["qBK"], writes=["qBK"])
            yield
        p.emit("v", lambda e: e.tensor_tensor(bsc2, sbest, base_, Alu.mult),
               reads=["qBK"], writes=["qBK"])
        p.emit("v", lambda e: e.reciprocal(isc, bsc2), reads=["qBK"], writes=["qBK"])
        ib = isc.unsqueeze(2).broadcast_to([128, NB, 64])
        sbb = bsc2.unsqueeze(2).broadcast_to([128, NB, 64])
        p.emit("v", lambda e, wc3=wc3, ib=ib: e.tensor_tensor(
            qB[:].rearrange("p (nb b) -> p nb b", b=64), wc3, ib, Alu.mult),
            reads=[f"qw{slot}", "qBK"], writes=["qB"])
        qa3 = qA[:].rearrange("p (nb b) -> p nb b", b=64)
        qb3 = qB[:].rearrange("p (nb b) -> p nb b", b=64)
        p.emit("v", lambda e, sbb=sbb: e.tensor_tensor(qa3, qb3, sbb, Alu.mult),
               reads=["qB", "qBK"], writes=["qA"])
        p.emit("v", lambda e, wc3=wc3: e.tensor_tensor(qa3, wc3, qa3, Alu.subtract),
               reads=["qA", f"qw{slot}"], writes=["qA"])
        p.emit("v", lambda e, ib=ib: e.tensor_tensor(qa3, qa3, ib, Alu.mult),
               reads=["qA", "qBK"], writes=["qA"])
        p.emit("v", lambda e: e.tensor_tensor(qB[:], qB[:], qA[:], Alu.add),
               reads=["qA", "qB"], writes=["qB"])
        p.emit("s", lambda e: e.activation(qI[:], qB[:], AFT.Copy), reads=["qB"], writes=["qI"])
        p.emit("v", lambda e: e.tensor_scalar(qA[:], qI[:], 15.0, -15.0, Alu.min, Alu.max),
               reads=["qI"], writes=["qA"])
        sb = bsc2.unsqueeze(2).broadcast_to([128, NB, 64])
        p.emit("v", lambda e, sb=sb: e.tensor_tensor(
            qdq[:].rearrange("p (nb b) -> p nb b", b=64),
            qA[:].rearrange("p (nb b) -> p nb b", b=64), sb, Alu.mult),
            reads=["qA", "qBK"], writes=["qdq"])
        yield
        for i in range(8):
            p.emit("pe", lambda e, i=i: e.transpose(ps_t[:, ts(i, 128)], qdq[:, ts(i, 128)],
                                                    g["IDE"][:]),
                   reads=["qdq", "IDE"], writes=["ps_t"])
        p.emit("v", lambda e: e.tensor_copy(qT[:], ps_t[:, 0:1024]),
               reads=["ps_t"], writes=["qT"])
        dst = qcin_flat[off:off + 128 * cw].rearrange("(p f) -> p f", p=128)[
            :, part * 1024:(part + 1) * 1024]
        p.dma("g", dst, qT[:], reads=["qT"], writes=["q_cin"])
        yield


def quant_allgather(nc, p, g):
    p.emit("g", lambda e: e.collective_compute(
        "AllGather", Alu.bypass, replica_groups=[[0, 1, 2, 3, 4, 5, 6, 7]],
        ins=[g["q_cin"][:]], outs=[g["q_cout"][:]]),
        reads=["q_cin"], writes=["q_cout"], sem="ccc", inc=1)


def ffn(nc, p, g, call):
    AB1, AB2, AB3, STR = g["AB1"], g["AB2"], g["AB3"], g["STR"]
    ps_a, ps_b = g["ps_a"], g["ps_b"]
    tagln = "ln2" if call == 0 else "ln4"
    if call == 0:
        p.fence_all()
    QSH_FC = 8 * 2048 * 128
    qcof = [g["q_cout"][r].rearrange("a b -> (a b)") for r in range(8)]
    for sub in range(T // TSUB2):
        sl = slice(sub * TSUB2, (sub + 1) * TSUB2)
        for mh in range(64):
            r, cih = mh // 8, mh % 8
            slot = mh % 2
            slab = STR[:, slot * 2048:slot * 2048 + 2048]
            src = qcof[r][cih * 2048 * 128:(cih + 1) * 2048 * 128]
            p.dma("sp", slab, src.rearrange("(p f) -> p f", p=128),
                  reads=["q_cout"], writes=[f"STR{slot}"])
            stiles = [slab[:, ts(_k, 128)] for _k in range(16)]
            pp, ppn = (ps_a, "ps_a") if mh % 2 == 0 else (ps_b, "ps_b")
            for kc in range(NCH):
                mov = AB1[:, ts(kc, T)][:, sl]
                p.emit("pe", lambda e, stiles=stiles, kc=kc, mov=mov, pp=pp: e.matmul(
                    pp[:, 0:TSUB2], stiles[kc], mov, start=(kc == 0), stop=(kc == NCH - 1)),
                    reads=[f"STR{slot}", f"{tagln}_{kc}", f"AB1:{kc}"], writes=[ppn])
            harena, hnm = (AB2, "AB2") if mh < 32 else (AB3, "AB3")
            hslot = mh % 32
            hdst = harena[:, hslot * TSUB2:(hslot + 1) * TSUB2]
            hf32 = g["SMALL"][:, 0:TSUB2] if mh % 2 == 0 else g["RSC"][:, 0:TSUB2]
            hfn = "SMALL" if mh % 2 == 0 else "RSC"
            p.emit("s", lambda e, hf32=hf32, pp=pp: e.activation(hf32, pp[:, 0:TSUB2], AFT.Relu),
                   reads=[ppn], writes=[hfn])
            p.emit("s", lambda e, hdst=hdst, hf32=hf32: e.activation(hdst, hf32, AFT.Square),
                   reads=[hfn], writes=[f"h{mh}", f"{hnm}:{hslot // 2}"])
        if KDBG == "hid0" and call == 0 and sub == 0:
            for sl2 in range(16):
                p.dma("g", g["y"][ts(sl2, 128), 0:1024],
                      (g["AB2"] if sl2 < 8 else g["AB3"])[:, ts(sl2 % 8, 2048)][:, 0:1024],
                      reads=[f"h{sl2 * 4}", f"h{sl2 * 4 + 1}", f"h{sl2 * 4 + 2}", f"h{sl2 * 4 + 3}"])
        for mo in range(NCH):
            r, cio = mo // 2, mo % 2
            for quarter in range(4):
                slot = quarter % 2
                slab = STR[:, slot * 2048:slot * 2048 + 2048]
                base_el = QSH_FC + cio * 8192 * 128
                src = qcof[r][base_el:base_el + 8192 * 128].rearrange(
                    "(p f) -> p f", p=128)[:, quarter * 2048:(quarter + 1) * 2048]
                p.dma("sp", slab, src, reads=["q_cout"], writes=[f"STR{slot}"])
                stiles = [slab[:, ts(_k, 128)] for _k in range(16)]
                pp, ppn = (ps_a, "ps_a") if mo % 2 == 0 else (ps_b, "ps_b")
                for kq in range(16):
                    kh = quarter * 16 + kq
                    harena = AB2 if kh < 32 else AB3
                    hs = kh % 32
                    mov = harena[:, hs * TSUB2:(hs + 1) * TSUB2]
                    p.emit("pe", lambda e, stiles=stiles, kq=kq, mov=mov, kh=kh, pp=pp: e.matmul(
                        pp[:, 0:TSUB2], stiles[kq], mov, start=(kh == 0), stop=(kh == 63)),
                        reads=[f"STR{slot}", f"h{kh}"], writes=[ppn])
            xc = g["A_x"][:, ts(mo, T)][:, sl]
            p.emit("v", lambda e, xc=xc, pp=pp: e.tensor_tensor(xc, xc, pp[:, 0:TSUB2], Alu.add),
                   reads=[ppn, f"x{mo}"], writes=[f"x{mo}"])


def attn_b(nc, p, g):
    AB1, AB2, AB3, KVB, VTM, STR = (g["AB1"], g["AB2"], g["AB3"], g["KVB"],
                                    g["VTM"], g["STR"])
    ps_a, ps_b, ps_t = g["ps_a"], g["ps_b"], g["ps_t"]
    KPART = NCH * 128 * T
    kvin_flat = g["kv_cin"].rearrange("a b -> (a b)")
    for W, bias, kindnm in (("wqT", "bqT", "q"), ("wkT", "bkT", "k"), ("wvT", "bvT", "v")):
        for mo in range(NCH):
            stiles = stream_colblock(nc, p, g, g[W], W, D, mo, mo % 2)
            pq, pqn = (ps_a, "ps_a") if mo % 2 == 0 else (ps_b, "ps_b")
            for half in range(2):
                sl = slice(half * 512, half * 512 + 512)
                for kc in range(NCH):
                    mov = AB1[:, ts(kc, T)][:, sl]
                    p.emit("pe", lambda e, stiles=stiles, kc=kc, mov=mov, sl=sl, pq=pq: e.matmul(
                        pq[:, sl], stiles[kc], mov, start=(kc == 0), stop=(kc == NCH - 1)),
                        reads=[f"STR{mo % 2}", f"ln3_{kc}", f"AB1:{kc}"], writes=[pqn])
            bcol = g[bias + "_s"][:, mo:mo + 1]
            if kindnm == "q":
                dst = AB2[:, ts(mo, T)]
                p.emit("v", lambda e, dst=dst, bcol=bcol, pq=pq: e.tensor_single_scalar(
                    dst, pq[:, 0:T], bcol, Alu.add),
                    reads=[pqn], writes=[f"q{mo}", f"AB2:{mo}"])
            elif kindnm == "k":
                kb = VTM[:, 0:T]
                p.emit("v", lambda e, kb=kb, bcol=bcol, pq=pq: e.tensor_single_scalar(
                    kb, pq[:, 0:T], bcol, Alu.add), reads=[pqn], writes=["VTM"])
                dst = kvin_flat[mo * 128 * T:(mo + 1) * 128 * T].rearrange("(p f) -> p f", p=128)
                p.dma("g", dst, kb, reads=["VTM"], writes=["kv_cin"])
            else:
                vb = VTM[:, 0:T]
                p.emit("v", lambda e, vb=vb, bcol=bcol, pq=pq: e.tensor_single_scalar(
                    vb, pq[:, 0:T], bcol, Alu.add), reads=[pqn], writes=["VTM"])
                for i in range(8):
                    p.emit("pe", lambda e, i=i, vb=vb: e.transpose(
                        ps_t[:, ts(i, 128)], vb[:, ts(i, 128)], g["IDE"][:]),
                        reads=["VTM", "IDE"], writes=["ps_t"])
                vtb = g["SC2"][:, 0:T]
                p.emit("v", lambda e, vtb=vtb: e.tensor_copy(vtb, ps_t[:, 0:T]),
                       reads=["ps_t"], writes=["SC2"])
                dstap = kvin_flat[KPART + mo * 1024 * 128:
                                  KPART + (mo + 1) * 1024 * 128].rearrange(
                    "(p f) -> p f", p=128)
                p.dma("g", dstap, vtb, reads=["SC2"], writes=["kv_cin"])
    p.emit("g", lambda e: e.collective_compute(
        "AllGather", Alu.bypass, replica_groups=[[0, 1], [2, 3], [4, 5], [6, 7]],
        ins=[g["kv_cin"][:]], outs=[g["kv_cout"][:]]),
        reads=["kv_cin"], writes=["kv_cout"], sem="ccc", inc=1)

    kvof = [g["kv_cout"][r].rearrange("a b -> (a b)") for r in range(2)]
    for qp in range(4):
        qts = (2 * qp, 2 * qp + 1)
        for khalf in range(2):
            for kc in range(NCH):
                slot = kc % 2
                slab = KVB[:, slot * 2048:slot * 2048 + 1024]
                src = kvof[khalf][kc * 128 * T:(kc + 1) * 128 * T]
                p.dma("sp", slab, src.rearrange("(p f) -> p f", p=128),
                      reads=["kv_cout"], writes=[f"KVB{slot}"])
                for j, qt in enumerate(qts):
                    pp = ps_a if j == 0 else ps_b
                    ppn = "ps_a" if j == 0 else "ps_b"
                    stat = AB2[:, ts(kc, T)][:, ts(qt, 128)]
                    for nh in range(2):
                        p.emit("pe", lambda e, pp=pp, stat=stat, slab=slab, nh=nh, kc=kc:
                               e.matmul(pp[:, nh * 512:(nh + 1) * 512], stat,
                                        slab[:, nh * 512:(nh + 1) * 512],
                                        start=(kc == 0), stop=(kc == NCH - 1)),
                               reads=[f"KVB{slot}", f"q{kc}", f"AB2:{kc}"], writes=[ppn])
            for j, qt in enumerate(qts):
                pp = ps_a if j == 0 else ps_b
                ppn = "ps_a" if j == 0 else "ps_b"
                nmax = g["CST"][:, 92 + j:93 + j]
                ssum = g["CST"][:, 94 + j:95 + j]
                Pj = g["SC1"] if j == 0 else g["SC2"]
                Pjn = "SC1" if j == 0 else "SC2"
                P2 = KVB[:, 1024:2048] if j == 0 else KVB[:, 3072:4096]
                P2n = "KVBp0" if j == 0 else "KVBp1"
                if khalf == 0:
                    p.emit("v", lambda e, pp=pp, nmax=nmax: e.tensor_reduce(
                        nmax, pp[:, 0:T], AX.X, Alu.max, negate=True),
                        reads=[ppn], writes=[f"nmx{j}"])
                    p.emit("s", lambda e, pp=pp, Pj=Pj, nmax=nmax, ssum=ssum: e.activation(
                        Pj[:, 0:T], pp[:, 0:T], AFT.Exp, bias=nmax, scale=1.0,
                        accum_out=ssum),
                        reads=[ppn, f"nmx{j}"], writes=[Pjn, f"ssm{j}"])
                else:
                    nm2 = g["CST"][:, 96 + j:97 + j]
                    fac = g["CST"][:, 98 + j:99 + j]
                    s2 = g["CST"][:, 100 + j:101 + j]
                    p.emit("v", lambda e, pp=pp, nm2=nm2: e.tensor_reduce(
                        nm2, pp[:, 0:T], AX.X, Alu.max, negate=True),
                        reads=[ppn], writes=[f"nm2{j}"])
                    p.emit("v", lambda e, fac=fac, nmax=nmax, nm2=nm2: e.tensor_tensor(
                        fac, nmax, nm2, Alu.max),
                        reads=[f"nmx{j}", f"nm2{j}"], writes=[f"tmx{j}"])
                    p.emit("v", lambda e, fac=fac, nmax=nmax, nm2=nm2: e.tensor_tensor(
                        fac, nmax, nm2, Alu.min),
                        reads=[f"nmx{j}", f"nm2{j}"], writes=[f"tmx{j}"])
                    dfc = g["CST"][:, 102 + j:103 + j]
                    p.emit("v", lambda e, dfc=dfc, fac=fac, nmax=nmax: e.tensor_tensor(
                        dfc, fac, nmax, Alu.subtract),
                        reads=[f"tmx{j}", f"nmx{j}"], writes=[f"dfc{j}"])
                    p.emit("s", lambda e, dfc=dfc: e.activation(dfc, dfc, AFT.Exp),
                           reads=[f"dfc{j}"], writes=[f"dfc{j}"])
                    p.emit("v", lambda e, Pj=Pj, dfc=dfc: e.tensor_single_scalar(
                        Pj[:, 0:T], Pj[:, 0:T], dfc, Alu.mult),
                        reads=[Pjn, f"dfc{j}"], writes=[Pjn])
                    p.emit("v", lambda e, ssum=ssum, dfc=dfc: e.tensor_tensor(
                        ssum, ssum, dfc, Alu.mult),
                        reads=[f"ssm{j}", f"dfc{j}"], writes=[f"ssm{j}"])
                    p.emit("s", lambda e, pp=pp, P2=P2, fac=fac, s2=s2: e.activation(
                        P2, pp[:, 0:T], AFT.Exp, bias=fac, scale=1.0, accum_out=s2),
                        reads=[ppn, f"tmx{j}"], writes=[P2n, f"s2{j}"])
                    p.emit("v", lambda e, ssum=ssum, s2=s2: e.tensor_tensor(
                        ssum, ssum, s2, Alu.add),
                        reads=[f"ssm{j}", f"s2{j}"], writes=[f"ssm{j}"])
                    p.emit("v", lambda e, ssum=ssum: e.reciprocal(ssum, ssum),
                           reads=[f"ssm{j}"], writes=[f"ssm{j}"])
                    p.emit("v", lambda e, Pj=Pj, ssum=ssum: e.tensor_single_scalar(
                        Pj[:, 0:T], Pj[:, 0:T], ssum, Alu.mult),
                        reads=[Pjn, f"ssm{j}"], writes=[Pjn])
                    p.emit("v", lambda e, P2=P2, ssum=ssum: e.tensor_single_scalar(
                        P2, P2, ssum, Alu.mult),
                        reads=[P2n, f"ssm{j}"], writes=[P2n])
                    for h2 in range(2):
                        srcP = Pj[:, 0:T] if h2 == 0 else P2
                        srcn = Pjn if h2 == 0 else P2n
                        for i in range(8):
                            p.emit("pe", lambda e, srcP=srcP, i=i: e.transpose(
                                ps_t[:, ts(i, 128)], srcP[:, ts(i, 128)], g["IDE"][:]),
                                reads=[srcn, "IDE"], writes=["ps_t"])
                        for i in range(8):
                            kt = h2 * 8 + i
                            d = AB3[:, kt * 1024 + qt * 128: kt * 1024 + qt * 128 + 128]
                            p.emit("v", lambda e, d=d, i=i: e.tensor_copy(d, ps_t[:, ts(i, 128)]),
                                   reads=["ps_t"], writes=[f"Pt{kt}", f"AB3:{kt}"])
    for mo in range(NCH):
        slot = mo % 2
        slab = STR[:, slot * 2048:(slot + 1) * 2048]
        for r in range(2):
            srcap = kvof[r][KPART + mo * 1024 * 128:
                            KPART + (mo + 1) * 1024 * 128].rearrange("(p f) -> p f", p=128)
            p.dma("sp", slab[:, r * 1024:(r + 1) * 1024], srcap,
                  reads=["kv_cout"], writes=[f"STR{slot}"])
        stiles = [slab[:, ts(_k, 128)] for _k in range(16)]
        for kt in range(16):
            mov = AB3[:, kt * 1024:(kt + 1) * 1024]
            for nh in range(2):
                p.emit("pe", lambda e, stiles=stiles, kt=kt, mov=mov, nh=nh: e.matmul(
                    ps_a[:, nh * 512:(nh + 1) * 512], stiles[kt],
                    mov[:, nh * 512:(nh + 1) * 512], start=(kt == 0), stop=(kt == 15)),
                    reads=[f"STR{slot}", f"Pt{kt}", f"AB3:{kt}"], writes=["ps_a"])
        xc = g["A_x"][:, ts(mo, T)]
        p.emit("v", lambda e, xc=xc: e.tensor_tensor(xc, xc, ps_a[:, 0:T], Alu.add),
               reads=["ps_a", f"x{mo}"], writes=[f"x{mo}"])


def write_out(nc, p, g):
    if KDBG == "ln1":
        for ch in range(NCH):
            p.dma("g", g["y"][ts(ch, 128), :], g["AB1"][:, ts(ch, T)],
                  reads=[f"ln1_{ch}", f"AB1:{ch}"])
    elif KDBG == "q":
        for ch in range(NCH):
            p.dma("g", g["y"][ts(ch, 128), :], g["AB2"][:, ts(ch, T)],
                  reads=[f"q{ch}"])
    elif KDBG == "deq":
        for cch in range(8):
            srcq = g["q_cin"].rearrange("a b -> (a b)")[cch * 2048 * 128:(cch + 1) * 2048 * 128]
            p.dma("g", g["y"][ts(cch * 2, 128) if False else slice(0, 0), :] if False else
                  g["y"].rearrange("a b -> (a b)")[cch * 2048 * 128:(cch + 1) * 2048 * 128].rearrange("(p f) -> p f", p=128),
                  srcq.rearrange("(p f) -> p f", p=128), reads=["q_cin"])
    elif KDBG == "deqp":
        for cch in range(2):
            srcq = g["q_cin"].rearrange("a b -> (a b)")[
                8 * 2048 * 128 + cch * 8192 * 128: 8 * 2048 * 128 + (cch + 1) * 8192 * 128]
            p.dma("g", g["y"].rearrange("a b -> (a b)")[cch * 8192 * 128:(cch + 1) * 8192 * 128].rearrange("(p f) -> p f", p=128),
                  srcq.rearrange("(p f) -> p f", p=128), reads=["q_cin"])
    elif KDBG == "ag":
        for cch in range(8):
            srcq = g["q_cout"][3].rearrange("a b -> (a b)")[cch * 2048 * 128:(cch + 1) * 2048 * 128]
            p.dma("g", g["y"].rearrange("a b -> (a b)")[cch * 2048 * 128:(cch + 1) * 2048 * 128].rearrange("(p f) -> p f", p=128),
                  srcq.rearrange("(p f) -> p f", p=128), reads=["q_cout"])
    elif KDBG == "hid":
        for sl2 in range(16):
            p.dma("g", g["y"][ts(sl2, 128), :], g["AB2"][:, ts(sl2, 1024)],
                  reads=[f"h{sl2 * 2}", f"h{sl2 * 2 + 1}"])
    elif KDBG == "kv":
        p.dma("g", g["y"][0:128, 0:1024], g["KVB"][:, 0:1024], reads=["KVB"])
        p.dma("g", g["y"][128:256, 0:512], g["KVB"][:, 1024:1536], reads=["KVB"])
        p.dma("g", g["y"][256:384, 0:1024], g["VTM"][:, 0:1024], reads=["VTM"])
        p.dma("g", g["y"][384:512, 0:512], g["VTM"][:, 1024:1536], reads=["VTM"])
    elif KDBG == "ao":
        for ch in range(NCH):
            p.dma("g", g["y"][ts(ch, 128), :], g["AB1"][:, ts(ch, T)],
                  reads=[f"ao{ch}", f"AB1:{ch}"])
    elif KDBG == "hid0":
        pass
    else:
        for ch in range(NCH):
            p.dma("sp", g["y"][ts(ch, 128), :], g["A_x"][:, ts(ch, T)], reads=[f"x{ch}"])
    p.fence("sp")
    p.fence("g")


def kernel(**inputs):
    xs = np.asarray(inputs["L0_attn_a_h"], dtype=np.float32)
    nc = build_nc()
    in_maps = []
    f32 = np.float32
    for c in range(8):
        b, h = c // 2, c % 2
        xo = np.ascontiguousarray(xs[b, h * T:(h + 1) * T, :].T)
        if h == 1:
            xh = np.ascontiguousarray(xs[b, 512:1024, :].T)
        else:
            xh = np.zeros((D, HALO), f32)
        k_off = h * T - HALO
        tm = np.zeros((NQT, 8), f32)
        if h == 0:
            for j in range(NQT):
                tm[j, 0:max(0, 4 - j)] = -1e30
        m = {
            "x_own": xo, "x_halo": xh,
            "posmeta": np.full((128, 1), k_off, f32),
            "tilemask": np.ascontiguousarray(
                np.broadcast_to(tm.reshape(1, -1), (128, NQT * 8))),
            "wq_comp": inputs["L0_attn_a_wq_comp"], "wq_up": inputs["L0_attn_a_wq_up"],
            "wk": inputs["L0_attn_a_wk"], "wv": inputs["L0_attn_a_wv"],
            "q_norm_w": np.asarray(inputs["L0_attn_a_q_norm_w"]).reshape(1, -1),
            "k_norm_w": np.asarray(inputs["L0_attn_a_k_norm_w"]).reshape(1, -1),
            "sinkrep": np.ascontiguousarray(np.broadcast_to(
                np.asarray(inputs["L0_attn_a_sink_logit"]).reshape(1, -1), (128, NHEAD))),
            "out_w1": inputs["L0_attn_a_out_w1"], "out_w2": inputs["L0_attn_a_out_w2"],
            "wqT": np.ascontiguousarray(np.asarray(inputs["L0_attn_b_Wq"]).T),
            "wkT": np.ascontiguousarray(np.asarray(inputs["L0_attn_b_Wk"]).T),
            "wvT": np.ascontiguousarray(np.asarray(inputs["L0_attn_b_Wv"]).T),
            "bqT": np.ascontiguousarray(np.asarray(inputs["L0_attn_b_bq"]).reshape(NCH, 128).T),
            "bkT": np.ascontiguousarray(np.asarray(inputs["L0_attn_b_bk"]).reshape(NCH, 128).T),
            "bvT": np.ascontiguousarray(np.asarray(inputs["L0_attn_b_bv"]).reshape(NCH, 128).T),
            "wfc_shard": np.ascontiguousarray(
                np.asarray(inputs["L0_ffn_w_fc"])[c * 1024:(c + 1) * 1024, :]),
            "wproj_shard": np.ascontiguousarray(
                np.asarray(inputs["L0_ffn_w_proj"])[c * 256:(c + 1) * 256, :]),
        }
        m = {k: np.asarray(v, dtype=f32) for k, v in m.items()}
        in_maps.append(m)
    res = run_bass_kernel_spmd(nc, in_maps, core_ids=list(range(8)))
    out = np.empty((4, 2048, 2048), np.float32)
    for c in range(8):
        b, h = c // 2, c % 2
        out[b, h * T:(h + 1) * T, :] = np.asarray(res.results[c]["y"]).T
    return out


# version 05
# speedup vs baseline: 1.0096x; 1.0096x over previous
import contextlib
import os
import numpy as np

import concourse.bass as bass
import concourse.mybir as mybir
from concourse.bass_utils import run_bass_kernel_spmd

F32 = mybir.dt.float32
BF16 = mybir.dt.bfloat16
I32 = mybir.dt.int32
Alu = mybir.AluOpType
AFT = mybir.ActivationFunctionType
AX = mybir.AxisListType

D = 2048
T = 1024
NCH = 16
HALO = 512
NQT = 8
NHEAD = 16
QC = 512
INTER = 8192

STAGE = int(os.environ.get("KSTAGE", "99"))
KDBG = os.environ.get("KDBG", "")
TSUB1 = 256
TSUB2 = 512

TWO_PI = 6.283185307179586
RC1 = float(np.float32(TWO_PI))
RC2 = float(np.float32(TWO_PI - RC1))
RC3 = float(np.float32(TWO_PI - RC1 - RC2))
INV2PI = float(np.float32(1.0 / TWO_PI))
PIO2 = float(np.float32(TWO_PI / 4.0))
PI_F = float(np.float32(TWO_PI / 2.0))


class P:

    ENGS = ("sp", "pe", "v", "s", "g")

    def __init__(self, nc):
        self.nc = nc
        self.q = {e: [] for e in self.ENGS}
        self.sem_count = {}
        self.sem_handles = {}
        self.last_w = {}
        self.readers = {}
        self.waited = {e: {} for e in self.ENGS}
        self._stack = contextlib.ExitStack()
        for name in ("dsp", "dg", "dsc", "cpe", "cv", "cs", "cg", "ccc"):
            self.sem_handles[name] = self._stack.enter_context(nc.semaphore(name))
            self.sem_count[name] = 0

    def close(self):
        self._stack.close()

    def _wait(self, eng, sem, val):
        if val <= 0 or self.waited[eng].get(sem, 0) >= val:
            return
        self.waited[eng][sem] = val
        self.q[eng].append(("w", self.sem_handles[sem], val))

    def emit(self, eng, builder, reads=(), writes=(), sem=None, inc=None):
        for r in reads:
            lw = self.last_w.get(r)
            if lw:
                self._wait(eng, lw[0], lw[1])
        for r in writes:
            lw = self.last_w.get(r)
            if lw:
                self._wait(eng, lw[0], lw[1])
            for s_, v_ in self.readers.get(r, {}).items():
                self._wait(eng, s_, v_)
        if sem is None:
            sem = {"sp": "dsp", "pe": "cpe", "v": "cv", "s": "cs", "g": "cg"}[eng]
            inc_ = 16 if sem in ("dsp", "dg", "dsc") else 1
        else:
            inc_ = 1
        if inc is not None:
            inc_ = inc
        self.sem_count[sem] += inc_
        val = self.sem_count[sem]
        self.q[eng].append(("i", builder, self.sem_handles[sem], inc_))
        for r in writes:
            self.last_w[r] = (sem, val)
            self.readers[r] = {}
        for r in reads:
            d = self.readers.setdefault(r, {})
            d[sem] = max(d.get(sem, 0), val)
        return sem, val

    def dma(self, eng, out, in_, reads=(), writes=(), **kw):
        sem = {"sp": "dsp", "g": "dg", "s": "dsc"}[eng]
        return self.emit(eng, lambda e: e.dma_start(out=out, in_=in_, **kw),
                         reads=reads, writes=writes, sem=sem, inc=16)

    def fence(self, eng):
        for sem, cnt in self.sem_count.items():
            self._wait(eng, sem, cnt)

    def fence_all(self):
        for e in self.ENGS:
            self.fence(e)

    def finalize(self, block):
        def runner(lst):
            def f(e):
                for item in lst:
                    if item[0] == "w":
                        e.wait_ge(item[1], item[2])
                    else:
                        _, builder, h, inc_ = item
                        inst = builder(e)
                        if inc_:
                            inst.then_inc(h, inc_)
            return f
        block.sync(runner(self.q["sp"]))
        block.tensor(runner(self.q["pe"]))
        block.vector(runner(self.q["v"]))
        block.scalar(runner(self.q["s"]))
        block.gpsimd(runner(self.q["g"]))


def ts(i, n):
    return bass.ts(i, n)


def build_nc():
    nc = bass.Bass("TRN2", target_bir_lowering=False, num_devices=8)
    g = {"nc": nc}

    def inp(name, shape, dtype=F32):
        g[name] = nc.dram_tensor(name, shape, dtype, kind="ExternalInput").ap()

    inp("x_own", [D, T])
    inp("x_halo", [D, HALO])
    inp("posmeta", [128, 1])
    inp("hkx", [128, 1024])
    inp("wq_comp", [D, QC])
    inp("wq_up", [QC, D])
    inp("wk", [D, 128])
    inp("wv", [D, 128])
    inp("q_norm_w", [1, 128])
    inp("k_norm_w", [1, 128])
    inp("sinkrep", [128, NHEAD])
    inp("out_w1", [4, QC, D])
    inp("out_w2", [INTER, D])
    inp("wqT", [D, D])
    inp("wkT", [D, D])
    inp("wvT", [D, D])
    inp("bqT", [128, NCH])
    inp("bkT", [128, NCH])
    inp("bvT", [128, NCH])
    inp("wfc_shard", [INTER // 8, D])
    inp("wproj_shard", [D // 8, INTER])
    g["y"] = nc.dram_tensor("y", [D, T], F32, kind="ExternalOutput").ap()

    g["w1bf"] = nc.dram_tensor("w1bf", [4, QC, D], BF16, kind="Internal").ap()
    g["w2bf"] = nc.dram_tensor("w2bf", [NCH, INTER, 128], BF16, kind="Internal").ap()
    QSH = 8 * 2048 * 128 + 2 * 8192 * 128
    g["q_cin"] = nc.dram_tensor("q_cin", [QSH // 512, 512], BF16, kind="Internal").ap()
    g["q_cout"] = nc.dram_tensor("q_cout", [8, QSH // 512, 512], BF16, kind="Internal").ap()
    KVSH = NCH * 128 * T + 8 * 128 * D
    g["kv_cin"] = nc.dram_tensor("kv_cin", [KVSH // 512, 512], BF16, kind="Internal").ap()
    g["kv_cout"] = nc.dram_tensor("kv_cout", [2, KVSH // 512, 512], BF16, kind="Internal").ap()

    def sball(name, shape, dtype):
        g[name] = nc.alloc_sbuf_tensor(name, shape, dtype).ap()

    sball("A_x", [128, NCH * T], F32)
    sball("AB1", [128, 16384], BF16)
    sball("AB2", [128, 16384], BF16)
    start, _ = nc.bump_sbuf(32768)
    g["AB3"] = nc.alloc_sbuf_tensor_at("AB3", [128, 16384], BF16, offset=start).ap()
    g["qw"] = nc.alloc_sbuf_tensor_at("qw", [128, 2048], F32, offset=start).ap()
    g["qu"] = nc.alloc_sbuf_tensor_at("qu", [128, 1024], F32, offset=start + 8192).ap()
    g["qA"] = nc.alloc_sbuf_tensor_at("qA", [128, 1024], F32, offset=start + 12288).ap()
    g["qB"] = nc.alloc_sbuf_tensor_at("qB", [128, 1024], F32, offset=start + 16384).ap()
    g["qI"] = nc.alloc_sbuf_tensor_at("qI", [128, 1024], I32, offset=start + 20480).ap()
    g["qdq"] = nc.alloc_sbuf_tensor_at("qdq", [128, 1024], BF16, offset=start + 24576).ap()
    g["qT"] = nc.alloc_sbuf_tensor_at("qT", [128, 1024], BF16, offset=start + 26624).ap()
    g["qBK"] = nc.alloc_sbuf_tensor_at("qBK", [128, 160], F32, offset=start + 28672).ap()

    sball("STR", [128, 4096], BF16)
    sball("KVB", [128, 4096], BF16)
    sball("VTM", [128, 1536], BF16)
    sball("SC1", [128, 1024], BF16)
    sball("SC2", [128, 1024], BF16)
    sball("SMALL", [128, 1024], F32)
    sball("RSC", [128, 1024], F32)
    sball("ROPE", [64, 1024], F32)
    sball("KKI", [32, 1024], I32)
    sball("CST", [128, 512], F32)
    sball("CSTB", [128, 768], BF16)
    sball("IDE", [128, 128], BF16)
    sball("MSKB", [128, 256], BF16)
    sball("SHR", [128, 48], F32)
    sball("ROWF", [1, 1024], F32)
    sball("QUPD", [128, 16], I32)
    sball("CEPS", [128, 2], F32)
    nc.const_aps.register(F32, 1e-5, g["CEPS"][:, 0:1])
    nc.const_aps.register(F32, 1e-6, g["CEPS"][:, 1:2])

    g["ps_a"] = nc.alloc_psum_tensor("ps_a", [128, 1024], F32).ap()
    g["ps_b"] = nc.alloc_psum_tensor("ps_b", [128, 1024], F32).ap()
    g["ps_c"] = nc.alloc_psum_tensor("ps_c", [128, 1024], F32).ap()
    g["ps_t"] = nc.alloc_psum_tensor("ps_t", [128, 1024], BF16).ap()
    g["ps_t2"] = nc.alloc_psum_tensor("ps_t2", [128, 1024], BF16).ap()

    assert nc.sbuf_bytes_remaining >= 0

    with nc.Block() as block:
        p = P(nc)
        build_program(nc, p, g)
        p.finalize(block)
        p.close()
    return nc


def build_program(nc, p, g):
    build_consts(nc, p, g)
    qgen = quant_generator(nc, p, g) if STAGE >= 5 else None

    def pump(n):
        if qgen is None:
            return
        for _ in range(n):
            try:
                next(qgen)
            except StopIteration:
                break

    load_x(nc, p, g)
    layer_norm(nc, p, g, dst=g["AB1"], tag="ln1")
    pump(30)
    if STAGE >= 2:
        attn_a_qkv(nc, p, g, pump)
    if STAGE >= 3:
        attn_a_scores(nc, p, g, pump)
    if STAGE >= 4:
        attn_a_out(nc, p, g, pump)
    if STAGE >= 5:
        pump(10 ** 9)
        quant_allgather(nc, p, g)
        layer_norm(nc, p, g, dst=g["AB1"], tag="ln2")
        ffn(nc, p, g, call=0)
    if STAGE >= 6:
        layer_norm(nc, p, g, dst=g["AB1"], tag="ln3")
        attn_b(nc, p, g)
    if STAGE >= 7:
        layer_norm(nc, p, g, dst=g["AB1"], tag="ln4")
        ffn(nc, p, g, call=1)
    write_out(nc, p, g)


def load_x(nc, p, g):
    for ch in range(NCH):
        p.dma("sp", g["A_x"][:, ts(ch, T)], g["x_own"][ts(ch, 128), :], writes=[f"x{ch}"])


def build_consts(nc, p, g):
    CST, CSTB, MSKB, SHR = g["CST"], g["CSTB"], g["MSKB"], g["SHR"]
    p.emit("g", lambda e: e.memset(g["CEPS"][:, 0:1], 1e-5), writes=["CEPS"])
    p.emit("g", lambda e: e.memset(g["CEPS"][:, 1:2], 1e-6), reads=["CEPS"], writes=["CEPS"])
    p.emit("g", lambda e: e.memset(g["IDE"][:], 0.0), writes=["IDE"])
    p.emit("g", lambda e: e.affine_select(g["IDE"][:], g["IDE"][:], pattern=[[-1, 128]],
                                          compare_op=Alu.not_equal, fill=1.0, base=0,
                                          channel_multiplier=1), reads=["IDE"], writes=["IDE"])
    M0 = MSKB[:, 0:128]
    M4 = MSKB[:, 128:256]
    p.emit("g", lambda e: e.memset(M0, 0.0), writes=["M0"])
    p.emit("g", lambda e: e.affine_select(M0, M0, pattern=[[-1, 128]], compare_op=Alu.is_ge,
                                          fill=-1e30, base=0, channel_multiplier=1),
           reads=["M0"], writes=["M0"])
    p.emit("g", lambda e: e.memset(M4, 0.0), writes=["M4"])
    p.emit("g", lambda e: e.affine_select(M4, M4, pattern=[[1, 128]], compare_op=Alu.is_ge,
                                          fill=-1e30, base=-1, channel_multiplier=-1),
           reads=["M4"], writes=["M4"])
    consts = {"c2": (0, 2.0), "c5": (1, 5.0), "c1": (2, 1.0), "c15": (3, 15.0),
              "c128": (4, 128.0), "cth": (5, 10000.0)}
    for nm, (i, vv) in consts.items():
        g[nm] = CST[:, i:i + 1]
        p.emit("g", lambda e, col=g[nm], vv=vv: e.memset(col, vv), writes=["CST"])
    c04 = CST[:, 8:9]; cstep = CST[:, 9:10]
    g["rs128"] = CST[:, 10:11]
    sq128 = CST[:, 11:12]
    r5 = CST[:, 12:13]
    g["r15"] = CST[:, 13:14]
    p.emit("v", lambda e: e.reciprocal(r5, g["c5"]), reads=["CST"], writes=["CST"])
    p.emit("v", lambda e: e.reciprocal(g["r15"], g["c15"]), reads=["CST"], writes=["CST"])
    p.emit("v", lambda e: e.tensor_tensor(c04, g["c2"], r5, Alu.mult),
           reads=["CST"], writes=["CST"])
    p.emit("v", lambda e: e.tensor_tensor(cstep, g["c1"], c04, Alu.subtract),
           reads=["CST"], writes=["CST"])
    p.emit("v", lambda e: e.tensor_tensor(cstep, cstep, g["r15"], Alu.mult),
           reads=["CST"], writes=["CST"])
    p.emit("s", lambda e: e.activation(sq128, g["c128"], AFT.Sqrt),
           reads=["CST"], writes=["CST"])
    p.emit("v", lambda e: e.reciprocal(g["rs128"], sq128), reads=["CST"], writes=["CST"])
    SH_S = SHR[:, 0:16]; SH_I = SHR[:, 16:32]; kio = SHR[:, 32:48]
    g["SH_S"], g["SH_I"] = SH_S, SH_I
    g["SH_S2"] = SHR[:, 16:32]
    p.emit("g", lambda e: e.iota(kio, pattern=[[1, 16]], base=0, channel_multiplier=0,
                                 allow_small_or_imprecise_dtypes=True), writes=["SHR"])
    p.emit("v", lambda e: e.tensor_scalar(SH_S, kio, cstep, c04, Alu.mult, Alu.add),
           reads=["SHR", "CST"], writes=["SHR"])
    p.emit("g", lambda e: e.memset(SH_S[:, 15:16], 1.0), reads=["SHR"], writes=["SHR"])
    p.emit("v", lambda e: e.tensor_tensor(g["SH_S2"], SH_S, SH_S, Alu.mult),
           reads=["SHR"], writes=["SHR"])
    g["nr15"] = CST[:, 15:16]
    p.emit("v", lambda e: e.tensor_single_scalar(g["nr15"], g["r15"], -1.0, Alu.mult),
           reads=["CST"], writes=["CST"])
    lnth32 = CST[0:32, 16:17]
    p.emit("s", lambda e: e.activation(lnth32, g["cth"][0:32, :], AFT.Ln),
           reads=["CST"], writes=["CST"])
    icol = CST[0:32, 20:21]
    p.emit("g", lambda e: e.iota(icol, pattern=[[0, 1]], base=0, channel_multiplier=1,
                                 allow_small_or_imprecise_dtypes=True),
           reads=["CST"], writes=["CST"])
    p.emit("v", lambda e: e.tensor_single_scalar(icol, icol, -1.0 / 32.0, Alu.mult),
           reads=["CST"], writes=["CST"])
    p.emit("v", lambda e: e.tensor_tensor(icol, icol, lnth32, Alu.mult),
           reads=["CST"], writes=["CST"])
    p.emit("s", lambda e: e.activation(icol, icol, AFT.Exp), reads=["CST"], writes=["CST"])
    g["invcol"] = icol
    g["CSTinv_dummy"] = None
    p.emit("v", lambda e: e.tensor_copy(CST[0:32, 22:23], icol), reads=["CST"], writes=["CSTinv"])
    g["invcol"] = CST[0:32, 22:23]
    g["koff"] = CST[:, 14:15]
    p.dma("sp", g["koff"], g["posmeta"][:], reads=["CST"], writes=["CSTko"])
    g["onescol_b"] = CSTB[0:128, 0:1]
    p.emit("g", lambda e: e.memset(g["onescol_b"], 1.0), writes=["CSTB"])
    g["onesrow_b"] = CSTB[0:1, 4:132]
    p.emit("g", lambda e: e.memset(g["onesrow_b"], 1.0), reads=["CSTB"], writes=["CSTB"])
    g["sink_s"] = CST[:, 168:184]
    p.dma("sp", g["sink_s"], g["sinkrep"], writes=["CSTsink"])
    g["HKB"] = g["KVB"][:, 2816:3840]
    g["bqT_s"] = CST[:, 184:200]
    g["bkT_s"] = CST[:, 200:216]
    g["bvT_s"] = CST[:, 216:232]
    p.dma("sp", g["bqT_s"], g["bqT"], writes=["CSTbias"])
    p.dma("sp", g["bkT_s"], g["bkT"], writes=["CSTbias"])
    p.dma("sp", g["bvT_s"], g["bvT"], writes=["CSTbias"])
    g["negsink"] = CST[:, 64:80]
    p.emit("v", lambda e: e.tensor_single_scalar(g["negsink"], g["sink_s"], -1.0, Alu.mult),
           reads=["CSTsink"], writes=["CST"])
    qnwr = CST[0:1, 240:368]; knwr = CST[0:1, 368:496]; tmpr = g["SMALL"][0:1, 0:128]
    p.dma("sp", qnwr, g["q_norm_w"], reads=["CST"], writes=["CSTnw"])
    p.dma("sp", knwr, g["k_norm_w"], reads=["CSTnw"], writes=["CSTnw"])
    p.emit("v", lambda e: e.tensor_single_scalar(qnwr, qnwr, g["rs128"][0:1, 0:1], Alu.mult),
           reads=["CSTnw", "CST"], writes=["CSTnw"])
    CB = g["CSTB"]
    g["qnw_hi"], g["qnw_lo"] = CB[0:1, 132:260], CB[0:1, 260:388]
    g["knw_hi"], g["knw_lo"] = CB[0:1, 388:516], CB[0:1, 516:644]
    for srcr, hi, lo in ((qnwr, g["qnw_hi"], g["qnw_lo"]),
                         (knwr, g["knw_hi"], g["knw_lo"])):
        p.emit("v", lambda e, srcr=srcr, hi=hi: e.tensor_copy(hi, srcr),
               reads=["CSTnw"], writes=["QNWr"])
        p.emit("v", lambda e, srcr=srcr, hi=hi: e.tensor_tensor(
            tmpr, srcr, hi, Alu.subtract),
               reads=["CSTnw", "QNWr"], writes=["SMALL"])
        p.emit("v", lambda e, lo=lo: e.tensor_copy(lo, tmpr), reads=["SMALL"], writes=["QNWr"])
    if STAGE >= 4:
        for gi in range(4):
            p.dma("g", g["w1bf"][gi], g["out_w1"][gi], writes=[f"w1bf{gi}"])
        for mo in range(NCH):
            p.dma("g", g["w2bf"][mo], g["out_w2"][:, ts(mo, 128)], writes=[f"w2bf{mo}"])


def rope_tables(nc, p, g, pos_lo, n):
    RSC = g["RSC"]
    ang = RSC[0:32, 0:n]
    kkf = g["SMALL"][0:32, 0:n]
    kki = g["KKI"][0:32, 0:n]
    p.emit("g", lambda e: e.iota(ang, pattern=[[1, n]], base=0, channel_multiplier=0,
                                 allow_small_or_imprecise_dtypes=True), writes=["RSC"])
    p.emit("v", lambda e: e.tensor_single_scalar(ang, ang, float(pos_lo), Alu.add),
           reads=["RSC"], writes=["RSC"])
    p.emit("v", lambda e: e.tensor_single_scalar(ang, ang, g["koff"][0:32, 0:1], Alu.add),
           reads=["RSC", "CSTko"], writes=["RSC"])
    p.emit("v", lambda e: e.tensor_single_scalar(ang, ang, g["invcol"], Alu.mult),
           reads=["RSC", "CSTinv"], writes=["RSC"])
    p.emit("s", lambda e: e.activation(kki, ang, AFT.Copy, bias=0.0, scale=INV2PI),
           reads=["RSC"], writes=["KKI"])
    p.emit("v", lambda e: e.tensor_copy(kkf, kki), reads=["KKI"], writes=["SMALL"])
    for c in (RC1, RC2, RC3):
        p.emit("v", lambda e, c=c: e.scalar_tensor_tensor(ang, kkf, -c, ang, Alu.mult, Alu.add),
               reads=["RSC", "SMALL"], writes=["RSC"])
    p.emit("s", lambda e: e.activation(g["ROPE"][32:64, 0:n], ang, AFT.Sin),
           reads=["RSC"], writes=["rope_sin"])
    p.emit("v", lambda e: e.tensor_single_scalar(ang, ang, PIO2, Alu.add),
           reads=["RSC"], writes=["RSC"])
    p.emit("v", lambda e: e.tensor_single_scalar(kkf, ang, PI_F, Alu.is_gt),
           reads=["RSC"], writes=["SMALL"])
    p.emit("v", lambda e: e.scalar_tensor_tensor(ang, kkf, -RC1, ang, Alu.mult, Alu.add),
           reads=["RSC", "SMALL"], writes=["RSC"])
    p.emit("s", lambda e: e.activation(g["ROPE"][0:32, 0:n], ang, AFT.Sin),
           reads=["RSC"], writes=["rope_cos"])


def ln_stats_bcast(nc, p, g, n, sum_ps, sum_nm, sq_ps, sq_nm, m_ps, m_nm, r_ps, r_nm):
    mrow = g["RSC"][0:1, 0:n]
    vrow = g["SMALL"][0:1, 0:n]
    m2row = g["ROWF"][0:1, 0:n]
    p.emit("v", lambda e: e.tensor_single_scalar(mrow, sum_ps[0:1, 0:n], 1.0 / D, Alu.mult),
           reads=[sum_nm, sq_nm], writes=["RSC"])
    p.emit("v", lambda e: e.tensor_single_scalar(vrow, sq_ps[0:1, 0:n], 1.0 / D, Alu.mult),
           reads=[sum_nm, sq_nm], writes=["SMALL"])
    p.emit("s", lambda e: e.activation(m2row, mrow, AFT.Square),
           reads=["RSC"], writes=["ROWF"])
    p.emit("v", lambda e: e.tensor_tensor(vrow, vrow, m2row, Alu.subtract),
           reads=["SMALL", "ROWF"], writes=["SMALL"])
    p.emit("s", lambda e: e.activation(vrow, vrow, AFT.Sqrt, bias=1e-5, scale=1.0),
           reads=["SMALL", "CEPS"], writes=["SMALL"])
    p.emit("v", lambda e: e.reciprocal(vrow, vrow), reads=["SMALL"], writes=["SMALL"])
    bcast_row(nc, p, g, mrow, "RSC", m_ps, m_nm, n)
    bcast_row(nc, p, g, vrow, "SMALL", r_ps, r_nm, n)


def bcast_row(nc, p, g, row, row_nm, dest_ps, dest_nm, n):
    onesb = g["onesrow_b"][0:1, 0:128]
    bh = g["SC1"][0:1, 0:n]
    bl = g["SC2"][0:1, 0:n]
    p.emit("v", lambda e: e.tensor_copy(bh, row), reads=[row_nm], writes=["SC1"])
    p.emit("v", lambda e: e.tensor_tensor(bl, row, bh, Alu.subtract),
           reads=[row_nm, "SC1"], writes=["SC2"])
    for half in range((n + 511) // 512):
        sl = slice(half * 512, min(n, half * 512 + 512))
        p.emit("pe", lambda e, sl=sl: e.matmul(dest_ps[:, sl], onesb, bh[:, sl],
                                               start=True, stop=False),
               reads=["SC1", "CSTB"], writes=[dest_nm])
        p.emit("pe", lambda e, sl=sl: e.matmul(dest_ps[:, sl], onesb, bl[:, sl],
                                               start=False, stop=True),
               reads=["SC2", "CSTB"], writes=[dest_nm])


def layer_norm(nc, p, g, dst, tag):
    A_x, ps_a, ps_b, ps_c = g["A_x"], g["ps_a"], g["ps_b"], g["ps_c"]
    onescol = g["onescol_b"]
    for ch in range(NCH):
        xc = A_x[:, ts(ch, T)]
        xb = g["SC1"][:, 0:T]
        p.emit("v", lambda e, xc=xc, xb=xb: e.tensor_copy(xb, xc),
               reads=[f"x{ch}"], writes=["SC1"])
        for hf in range(2):
            shf = slice(hf * 512, hf * 512 + 512)
            p.emit("pe", lambda e, xb=xb, ch=ch, shf=shf: e.matmul(
                ps_c[0:1, shf], onescol, xb[:, shf],
                start=(ch == 0), stop=(ch == NCH - 1)),
                reads=["SC1", "CSTB"], writes=["ps_c"])
        x2 = g["SC2"][:, 0:T]
        p.emit("s", lambda e, xc=xc, x2=x2: e.activation(x2, xc, AFT.Square),
               reads=[f"x{ch}"], writes=["SC2"])
        for hf in range(2):
            shf = slice(hf * 512, hf * 512 + 512)
            p.emit("pe", lambda e, x2=x2, ch=ch, shf=shf: e.matmul(
                ps_b[0:1, shf], onescol, x2[:, shf],
                start=(ch == 0), stop=(ch == NCH - 1)),
                reads=["SC2", "CSTB"], writes=["ps_b"])
    ln_stats_bcast(nc, p, g, T, ps_c, "ps_c", ps_b, "ps_b", ps_a, "ps_a", ps_c, "ps_c")
    for ch in range(NCH):
        xc = A_x[:, ts(ch, T)]
        oc = dst[:, ts(ch, T)]
        tmpf = g["RSC"][:, 0:T]
        p.emit("v", lambda e, xc=xc: e.tensor_tensor(tmpf, xc, ps_a[:, 0:T], Alu.subtract),
               reads=[f"x{ch}", "ps_a"], writes=["RSC"])
        p.emit("v", lambda e, oc=oc: e.tensor_tensor(oc, tmpf, ps_c[:, 0:T], Alu.mult),
               reads=["RSC", "ps_c"], writes=[f"{tag}_{ch}", f"AB1:{ch}"])


def stream_colblock(nc, p, g, w_ap, wname, kdim, m, slot):
    STR = g["STR"]
    nk = kdim // 128
    slab = STR[:, slot * 2048: slot * 2048 + nk * 128]
    dst3 = slab.rearrange("p (k m) -> p k m", k=nk)
    src = w_ap[:, ts(m, 128)].rearrange("(k p) m -> p k m", p=128)
    p.dma("g", dst3, src, reads=[wname], writes=[f"STR{slot}"])
    return [slab[:, ts(k, 128)] for k in range(nk)]


def rms_head(nc, p, g, src, src_nm, dst, dst_nm, whi, wlo, n, stat_ps, stat_nm, bc_ps, bc_nm):
    sq = g["SC1"][:, 0:n]
    p.emit("s", lambda e: e.activation(sq, src[:, 0:n], AFT.Square),
           reads=[src_nm], writes=["SC1"])
    for half in range((n + 511) // 512):
        sl = slice(half * 512, min(n, half * 512 + 512))
        p.emit("pe", lambda e, sl=sl: e.matmul(stat_ps[0:1, sl], g["onescol_b"], sq[:, sl],
                                               start=True, stop=True),
               reads=["SC1", "CSTB"], writes=[stat_nm])
    rrow = g["ROWF"][0:1, 0:n]
    p.emit("s", lambda e: e.activation(rrow, stat_ps[0:1, 0:n], AFT.Sqrt, bias=1e-6,
                                       scale=1.0 / 128.0),
           reads=[stat_nm, "CEPS"], writes=["ROWF"])
    p.emit("v", lambda e: e.reciprocal(rrow, rrow), reads=["ROWF"], writes=["ROWF"])
    bh = g["SC1"][0:1, 0:n]
    bl = g["SC2"][0:1, 0:n]
    p.emit("v", lambda e: e.tensor_copy(bh, rrow), reads=["ROWF"], writes=["SC1"])
    p.emit("v", lambda e: e.tensor_tensor(bl, rrow, bh, Alu.subtract),
           reads=["ROWF", "SC1"], writes=["SC2"])
    for half in range((n + 511) // 512):
        sl = slice(half * 512, min(n, half * 512 + 512))
        p.emit("pe", lambda e, sl=sl: e.matmul(bc_ps[:, sl], whi, bh[:, sl],
                                               start=True, stop=False),
               reads=["SC1", "QNWr"], writes=[bc_nm])
        p.emit("pe", lambda e, sl=sl: e.matmul(bc_ps[:, sl], whi, bl[:, sl],
                                               start=False, stop=False),
               reads=["SC2", "QNWr"], writes=[bc_nm])
        p.emit("pe", lambda e, sl=sl: e.matmul(bc_ps[:, sl], wlo, bh[:, sl],
                                               start=False, stop=True),
               reads=["SC1", "QNWr"], writes=[bc_nm])
    p.emit("v", lambda e: e.tensor_tensor(dst[:, 0:n], src[:, 0:n], bc_ps[:, 0:n], Alu.mult),
           reads=[src_nm, bc_nm], writes=[dst_nm])


def rope_apply(nc, p, g, ps, ps_nm, out, out_nm, n):
    ROPE, RSC = g["ROPE"], g["RSC"]
    cos = ROPE[0:32, 0:n]
    sin = ROPE[32:64, 0:n]
    p.emit("v", lambda e: e.tensor_tensor(out[0:32, 0:n], ps[0:32, 0:n], cos, Alu.mult),
           reads=[ps_nm, "rope_cos"], writes=[out_nm])
    p.emit("v", lambda e: e.tensor_tensor(RSC[0:32, 0:n], ps[32:64, 0:n], sin, Alu.mult),
           reads=[ps_nm, "rope_sin"], writes=["RSC"])
    p.emit("v", lambda e: e.tensor_tensor(out[0:32, 0:n], out[0:32, 0:n], RSC[0:32, 0:n],
                                          Alu.subtract),
           reads=[out_nm, "RSC"], writes=[out_nm])
    p.emit("v", lambda e: e.tensor_tensor(out[32:64, 0:n], ps[0:32, 0:n], sin, Alu.mult),
           reads=[ps_nm, "rope_sin"], writes=[out_nm])
    p.emit("v", lambda e: e.tensor_tensor(RSC[32:64, 0:n], ps[32:64, 0:n], cos, Alu.mult),
           reads=[ps_nm, "rope_cos"], writes=["RSC"])
    p.emit("v", lambda e: e.tensor_tensor(out[32:64, 0:n], out[32:64, 0:n], RSC[32:64, 0:n],
                                          Alu.add),
           reads=[out_nm, "RSC"], writes=[out_nm])
    p.emit("v", lambda e: e.tensor_copy(out[64:128, 0:n], ps[64:128, 0:n]),
           reads=[ps_nm], writes=[out_nm])


def attn_a_qkv(nc, p, g, pump):
    AB1, AB2, KVB = g["AB1"], g["AB2"], g["KVB"]
    ps_a = g["ps_a"]
    for m in range(4):
        stiles = stream_colblock(nc, p, g, g["wq_comp"], "wq_comp", D, m, m % 2)
        pq, pqn = (ps_a, "ps_a") if m % 2 == 0 else (g["ps_b"], "ps_b")
        for half in range(2):
            sl = slice(half * 512, half * 512 + 512)
            for kc in range(NCH):
                mov = AB1[:, ts(kc, T)][:, sl]
                p.emit("pe", lambda e, stiles=stiles, kc=kc, mov=mov, sl=sl, pq=pq: e.matmul(
                    pq[:, sl], stiles[kc], mov, start=(kc == 0), stop=(kc == NCH - 1)),
                    reads=[f"STR{m % 2}", f"ln1_{kc}", f"AB1:{kc}"], writes=[pqn])
        qc = KVB[:, ts(m, T)]
        p.emit("v", lambda e, qc=qc, pq=pq: e.tensor_copy(qc, pq[:, 0:T]),
               reads=[pqn], writes=[f"qc{m}", "KVB"])
        pump(4)
    rope_tables(nc, p, g, 512, 1024)
    for h in range(NHEAD):
        stiles = stream_colblock(nc, p, g, g["wq_up"], "wq_up", QC, h, h % 2)
        pq, pqn = (ps_a, "ps_a") if h % 2 == 0 else (g["ps_b"], "ps_b")
        for half in range(2):
            sl = slice(half * 512, half * 512 + 512)
            for kq in range(4):
                p.emit("pe", lambda e, stiles=stiles, kq=kq, sl=sl, pq=pq: e.matmul(
                    pq[:, sl], stiles[kq], KVB[:, ts(kq, T)][:, sl],
                    start=(kq == 0), stop=(kq == 3)),
                    reads=[f"STR{h % 2}", f"qc{kq}"], writes=[pqn])
        rope_apply(nc, p, g, pq, pqn, g["SMALL"], "SMALL", T)
        rms_head(nc, p, g, g["SMALL"], "SMALL", AB2[:, ts(h, T)], f"q{h}",
                 g["qnw_hi"], g["qnw_lo"], T, g["ps_c"], "ps_c", g["ps_c"], "ps_c")
        pump(6)


def attn_a_scores(nc, p, g, pump):
    AB1, AB2, KVB, VTM = g["AB1"], g["AB2"], g["KVB"], g["VTM"]
    ps_a, ps_b, ps_c, ps_t = g["ps_a"], g["ps_b"], g["ps_c"], g["ps_t"]
    RSC = g["RSC"]
    p.dma("g", g["HKB"], g["hkx"], writes=["HKB", "qc2", "qc3"])
    kslab = stream_colblock(nc, p, g, g["wk"], "wk", D, 0, 0)
    vslab = stream_colblock(nc, p, g, g["wv"], "wv", D, 0, 1)
    for ch in range(NCH):
        slot = ch % 2
        xh = RSC[:, slot * 512:(slot + 1) * 512]
        p.dma("sp", xh, g["x_halo"][ts(ch, 128), :], writes=["RSC"])
        xb = g["SC1"][:, 0:512]
        p.emit("v", lambda e, xh=xh, xb=xb: e.tensor_copy(xb, xh),
               reads=["RSC"], writes=["SC1"])
        p.emit("pe", lambda e, xb=xb, ch=ch: e.matmul(ps_c[0:1, 0:512], g["onescol_b"], xb,
                                                      start=(ch == 0), stop=(ch == NCH - 1)),
               reads=["SC1", "CSTB"], writes=["ps_c"])
        x2 = g["SC2"][:, 0:512]
        p.emit("s", lambda e, xh=xh, x2=x2: e.activation(x2, xh, AFT.Square),
               reads=["RSC"], writes=["SC2"])
        p.emit("pe", lambda e, x2=x2, ch=ch: e.matmul(ps_b[0:1, 0:512], g["onescol_b"], x2,
                                                      start=(ch == 0), stop=(ch == NCH - 1)),
               reads=["SC2", "CSTB"], writes=["ps_b"])
    ln_stats_bcast(nc, p, g, 512, ps_c, "ps_c", ps_b, "ps_b", ps_a, "ps_a", ps_b, "ps_b")
    for blk in range(3):
        rope_tables(nc, p, g, blk * 512, 512)
        for kc in range(NCH):
            if blk == 0:
                slot = kc % 2
                xh = RSC[:, slot * 512:(slot + 1) * 512]
                p.dma("sp", xh, g["x_halo"][ts(kc, 128), :], writes=["RSC"])
                hl = g["SC1"][:, 0:512] if slot == 0 else g["SC1"][:, 512:1024]
                tmq = g["SMALL"][:, 0:512]
                p.emit("v", lambda e, xh=xh, tmq=tmq: e.tensor_tensor(
                    tmq, xh, ps_a[:, 0:512], Alu.subtract),
                    reads=["RSC", "ps_a"], writes=["SMALL"])
                p.emit("v", lambda e, hl=hl, tmq=tmq: e.tensor_tensor(
                    hl, tmq, ps_b[:, 0:512], Alu.mult),
                    reads=["SMALL", "ps_b"], writes=["SC1"])
                mov, mnm = hl, ["SC1"]
            else:
                off = (blk - 1) * 512
                mov = AB1[:, ts(kc, T)][:, off:off + 512]
                mnm = [f"ln1_{kc}", f"AB1:{kc}"]
            p.emit("pe", lambda e, mov=mov, kc=kc: e.matmul(
                ps_c[:, 0:512], kslab[kc], mov, start=(kc == 0), stop=(kc == NCH - 1)),
                reads=mnm + ["STR0"], writes=["ps_c"])
            p.emit("pe", lambda e, mov=mov, kc=kc: e.matmul(
                ps_c[:, 512:1024], vslab[kc], mov, start=(kc == 0), stop=(kc == NCH - 1)),
                reads=mnm + ["STR1"], writes=["ps_c"])
        rope_apply(nc, p, g, ps_c[:, 0:512], "ps_c", g["SMALL"], "SMALL", 512)
        vb = g["SC2"][:, 0:512]
        p.emit("v", lambda e: e.tensor_copy(vb, ps_c[:, 512:1024]),
               reads=["ps_c"], writes=["SC2"])
        for i in range(4):
            p.emit("pe", lambda e, i=i: e.transpose(ps_t[:, ts(i, 128)], vb[:, ts(i, 128)],
                                                    g["IDE"][:]),
                   reads=["SC2", "IDE"], writes=["ps_t"])
        p.emit("v", lambda e, blk=blk: e.tensor_copy(VTM[:, blk * 512:(blk + 1) * 512],
                                                     ps_t[:, 0:512]),
               reads=["ps_t"], writes=["VTM"])
        rms_head(nc, p, g, g["SMALL"], "SMALL", KVB[:, blk * 512:(blk + 1) * 512],
                 "KVB", g["knw_hi"], g["knw_lo"], 512,
                 g["ps_c"], "ps_c", g["ps_b"], "ps_b")
        pump(6)
    for h in range(NHEAD):
        outb = ps_b if h % 2 else ps_a
        outbnm = "ps_b" if h % 2 else "ps_a"
        altb = ps_a if h % 2 else ps_b
        altbnm = "ps_a" if h % 2 else "ps_b"
        for qt in range(NQT):
            klo = qt * 128
            if qt % 2 == 0:
                sc_ps, sc_nm = ps_c, "ps_c"
                Pr, Prn = g["SC1"][:, 0:640], "SC1"
                Pt2, Ptn = g["SC2"][:, 0:640], "SC2"
                tslot, tnm = ps_t[:, 0:640], "ps_t"
                nmax = g["CST"][:, 88:89]; nmn = "CSTn0"
                srow = g["CST"][:, 89:90]; srn = "CSTs0"
                esk = g["CST"][:, 90:91]; esn = "CSTe0"
            else:
                sc_ps, sc_nm = altb, altbnm
                Pr, Prn = KVB[:, 1536:2176], "KVBp"
                Pt2, Ptn = KVB[:, 2176:2816], "KVBq"
                tslot, tnm = g["ps_t2"][:, 0:640], "ps_t2"
                nmax = g["CST"][:, 91:92]; nmn = "CSTn1"
                srow = g["CST"][:, 92:93]; srn = "CSTs1"
                esk = g["CST"][:, 93:94]; esn = "CSTe1"
            stat = AB2[:, ts(h, T)][:, ts(qt, 128)]
            p.emit("pe", lambda e, stat=stat, klo=klo, sc_ps=sc_ps: e.matmul(
                sc_ps[:, 0:512], stat, KVB[:, klo:klo + 512], start=True, stop=True),
                reads=[f"q{h}", "KVB"], writes=[sc_nm])
            p.emit("pe", lambda e, stat=stat, klo=klo, sc_ps=sc_ps: e.matmul(
                sc_ps[:, 512:640], stat, KVB[:, klo + 512:klo + 640], start=True, stop=True),
                reads=[f"q{h}", "KVB"], writes=[sc_nm])
            if qt < 4:
                hk = g["HKB"][:, qt * 128:qt * 128 + 512]
                p.emit("v", lambda e, sc_ps=sc_ps, hk=hk: e.tensor_tensor(
                    sc_ps[:, 0:512], sc_ps[:, 0:512], hk, Alu.add),
                    reads=[sc_nm, "HKB"], writes=[sc_nm])
            p.emit("v", lambda e, sc_ps=sc_ps: e.tensor_tensor(
                sc_ps[:, 0:128], sc_ps[:, 0:128], g["MSKB"][:, 128:256], Alu.add),
                reads=[sc_nm, "M4"], writes=[sc_nm])
            p.emit("v", lambda e, sc_ps=sc_ps: e.tensor_tensor(
                sc_ps[:, 512:640], sc_ps[:, 512:640], g["MSKB"][:, 0:128], Alu.add),
                reads=[sc_nm, "M0"], writes=[sc_nm])
            p.emit("v", lambda e, sc_ps=sc_ps, nmax=nmax: e.tensor_reduce(
                nmax, sc_ps[:, 0:640], AX.X, Alu.max, negate=True),
                reads=[sc_nm], writes=[nmn])
            nsk = g["negsink"][:, h:h + 1]
            p.emit("v", lambda e, nsk=nsk, nmax=nmax: e.tensor_single_scalar(
                nmax, nmax, nsk, Alu.min),
                reads=[nmn, "CST"], writes=[nmn])
            p.emit("s", lambda e, Pr=Pr, sc_ps=sc_ps, nmax=nmax, srow=srow: e.activation(
                Pr, sc_ps[:, 0:640], AFT.Exp, bias=nmax, scale=1.0, accum_out=srow),
                reads=[sc_nm, nmn], writes=[Prn, srn])
            snk = g["sink_s"][:, h:h + 1]
            p.emit("s", lambda e, snk=snk, esk=esk, nmax=nmax: e.activation(
                esk, snk, AFT.Exp, bias=nmax, scale=1.0),
                reads=[nmn], writes=[esn])
            p.emit("v", lambda e, srow=srow, esk=esk: e.tensor_tensor(srow, srow, esk, Alu.add),
                   reads=[srn, esn], writes=[srn])
            p.emit("v", lambda e, srow=srow: e.reciprocal(srow, srow), reads=[srn], writes=[srn])
            p.emit("v", lambda e, Pr=Pr, srow=srow: e.tensor_single_scalar(
                Pr, Pr, srow, Alu.mult),
                reads=[Prn, srn], writes=[Prn])
            for i in range(5):
                p.emit("pe", lambda e, i=i, Pr=Pr, tslot=tslot: e.transpose(
                    tslot[:, ts(i, 128)], Pr[:, ts(i, 128)], g["IDE"][:]),
                    reads=[Prn, "IDE"], writes=[tnm])
            p.emit("v", lambda e, Pt2=Pt2, tslot=tslot: e.tensor_copy(Pt2, tslot[:, 0:640]),
                   reads=[tnm], writes=[Ptn])
            for i in range(5):
                kt = qt + i
                p.emit("pe", lambda e, i=i, kt=kt, outb=outb, qt=qt, Pt2=Pt2: e.matmul(
                    outb[:, ts(qt, 128)], VTM[:, ts(kt, 128)], Pt2[:, ts(i, 128)],
                    start=(i == 0), stop=(i == 4)),
                    reads=[Ptn, "VTM"], writes=[outbnm])
        ao = AB1[:, ts(h, T)]
        p.emit("v", lambda e, ao=ao, outb=outb: e.tensor_copy(ao, outb[:, 0:T]),
               reads=[outbnm], writes=[f"ao{h}", f"AB1:{h}"])
        pump(8)


def attn_a_out(nc, p, g, pump):
    AB1, AB2, STR = g["AB1"], g["AB2"], g["STR"]
    ps_a, ps_b = g["ps_a"], g["ps_b"]
    for sub in range(T // TSUB1):
        sl = slice(sub * TSUB1, (sub + 1) * TSUB1)
        for gi in range(4):
            for mi in range(NCH):
                slot = (gi * NCH + mi) % 2
                slab = STR[:, slot * 2048:slot * 2048 + 512]
                dst3 = slab.rearrange("p (k m) -> p k m", k=4)
                src = g["w1bf"][gi][:, ts(mi, 128)].rearrange("(k p) m -> p k m", p=128)
                p.dma("sp", dst3, src, reads=[f"w1bf{gi}"], writes=[f"STR{slot}"])
                stiles = [slab[:, ts(_k, 128)] for _k in range(4)]
                ic = gi * NCH + mi
                pp, ppn = (ps_a, "ps_a") if ic % 2 == 0 else (g["ps_c"], "ps_c")
                for kg in range(4):
                    hh = 4 * gi + kg
                    mov = AB1[:, ts(hh, T)][:, sl]
                    p.emit("pe", lambda e, stiles=stiles, kg=kg, mov=mov, pp=pp: e.matmul(
                        pp[:, 0:TSUB1], stiles[kg], mov, start=(kg == 0), stop=(kg == 3)),
                        reads=[f"STR{slot}", f"ao{hh}", f"AB1:{hh}"], writes=[ppn])
                dst = AB2[:, ic * TSUB1:(ic + 1) * TSUB1]
                p.emit("v", lambda e, dst=dst, pp=pp: e.tensor_copy(dst, pp[:, 0:TSUB1]),
                       reads=[ppn], writes=[f"ic{ic}", f"AB2:{ic // 4}"])
            pump(2)
        for mo in range(NCH):
            for quarter in range(4):
                slot = quarter % 2
                slab = STR[:, slot * 2048:slot * 2048 + 2048]
                dst3 = slab.rearrange("p (k m) -> p k m", k=16)
                src = g["w2bf"][mo][quarter * 2048:(quarter + 1) * 2048, :].rearrange(
                    "(k p) m -> p k m", p=128)
                p.dma("sp", dst3, src, reads=[f"w2bf{mo}"], writes=[f"STR{slot}"])
                stiles = [slab[:, ts(_k, 128)] for _k in range(16)]
                pp, ppn = (ps_b, "ps_b") if mo % 2 == 0 else (g["ps_c"], "ps_c")
                for kq in range(16):
                    ki = quarter * 16 + kq
                    mov = AB2[:, ki * TSUB1:(ki + 1) * TSUB1]
                    p.emit("pe", lambda e, stiles=stiles, kq=kq, mov=mov, ki=ki, pp=pp: e.matmul(
                        pp[:, 0:TSUB1], stiles[kq], mov, start=(ki == 0), stop=(ki == 63)),
                        reads=[f"STR{slot}", f"ic{ki}", f"AB2:{ki // 4}"], writes=[ppn])
            xc = g["A_x"][:, ts(mo, T)][:, sl]
            p.emit("v", lambda e, xc=xc, pp=pp: e.tensor_tensor(xc, xc, pp[:, 0:TSUB1], Alu.add),
                   reads=[ppn, f"x{mo}"], writes=[f"x{mo}"])
            pump(2)


def quant_generator(nc, p, g):
    qw, qA, qB, qI, qdq, qT, qBK = (g["qw"], g["qA"], g["qB"],
                                    g["qI"], g["qdq"], g["qT"], g["qBK"])
    SH_S, SH_S2 = g["SH_S"], g["SH_S2"]
    ps_t = g["ps_t"]
    chunk_srcs = []
    for cch in range(8):
        for half in range(2):
            chunk_srcs.append((g["wfc_shard"][ts(cch, 128), half * 1024:(half + 1) * 1024],
                               cch * 2048 * 128, 2048, half))
    for cch in range(2):
        for part in range(8):
            chunk_srcs.append((g["wproj_shard"][ts(cch, 128), part * 1024:(part + 1) * 1024],
                               8 * 2048 * 128 + cch * 8192 * 128, 8192, part))
    NB = 16
    amax = qBK[:, 0:NB]; base_ = qBK[:, NB:2 * NB]; scl = qBK[:, 2 * NB:3 * NB]
    isc = qBK[:, 3 * NB:4 * NB]; bscore = qBK[:, 4 * NB:5 * NB]
    tmpb = qBK[:, 5 * NB:6 * NB]; nres = qBK[:, 6 * NB:7 * NB]; sbest = qBK[:, 7 * NB:8 * NB]
    bsc2 = qBK[:, 8 * NB:9 * NB]
    qcin_flat = g["q_cin"].rearrange("a b -> (a b)")
    for ci, (src, off, cw, part) in enumerate(chunk_srcs):
        slot = ci % 2
        wc = qw[:, slot * 1024:(slot + 1) * 1024]
        wc3 = wc.rearrange("p (nb b) -> p nb b", b=64)
        p.dma("sp", wc, src, writes=[f"qw{slot}"])
        yield
        p.emit("v", lambda e, wc3=wc3: e.tensor_reduce(amax, wc3, AX.X, Alu.max,
                                                       apply_absolute_value=True),
               reads=[f"qw{slot}"], writes=["qBK"])
        p.emit("v", lambda e: e.tensor_single_scalar(amax, amax, 1e-8, Alu.max),
               reads=["qBK"], writes=["qBK"])
        p.emit("v", lambda e: e.tensor_single_scalar(base_, amax, g["r15"], Alu.mult),
               reads=["qBK", "CST"], writes=["qBK"])
        p.emit("g", lambda e: e.memset(bscore, 3.0e38), reads=["qBK"], writes=["qBK"])
        yield
        for si in range(16):
            sS = SH_S[:, si:si + 1]
            s2 = SH_S2[:, si:si + 1]
            p.emit("v", lambda e, sS=sS: e.tensor_single_scalar(scl, base_, sS, Alu.mult),
                   reads=["qBK", "SHR"], writes=["qBK"])
            p.emit("v", lambda e: e.reciprocal(isc, scl), reads=["qBK"], writes=["qBK"])
            ib = isc.unsqueeze(2).broadcast_to([128, NB, 64])
            p.emit("v", lambda e, wc3=wc3, ib=ib: e.tensor_tensor(
                qB[:].rearrange("p (nb b) -> p nb b", b=64), wc3, ib, Alu.mult),
                reads=[f"qw{slot}", "qBK"], writes=["qB"])
            p.emit("s", lambda e: e.activation(qI[:], qB[:], AFT.Copy),
                   reads=["qB"], writes=["qI"])
            p.emit("v", lambda e: e.tensor_scalar(qA[:], qI[:], 15.0, -15.0, Alu.min, Alu.max),
                   reads=["qI"], writes=["qA"])
            p.emit("v", lambda e: e.tensor_tensor(qA[:], qA[:], qB[:], Alu.subtract),
                   reads=["qA", "qB"], writes=["qA"])
            p.emit("s", lambda e: e.activation(qA[:], qA[:], AFT.Square),
                   reads=["qA"], writes=["qA"])
            p.emit("v", lambda e: e.tensor_reduce(
                tmpb, qA[:].rearrange("p (nb b) -> p nb b", b=64), AX.X, Alu.add),
                reads=["qA"], writes=["qBK"])
            p.emit("v", lambda e, s2=s2: e.tensor_single_scalar(tmpb, tmpb, s2, Alu.mult),
                   reads=["qBK", "SHR"], writes=["qBK"])
            if si == 0:
                p.emit("v", lambda e, sS=sS: e.tensor_copy(sbest, sS.broadcast_to([128, NB])),
                       reads=["SHR", "qBK"], writes=["qBK"])
                p.emit("v", lambda e: e.tensor_copy(bscore, tmpb),
                       reads=["qBK"], writes=["qBK"])
            else:
                p.emit("v", lambda e: e.tensor_tensor(g["QUPD"][:], tmpb, bscore, Alu.is_lt),
                       reads=["qBK"], writes=["QUPD"])
                p.emit("v", lambda e, sS=sS: e.select(sbest, g["QUPD"][:],
                                                      sS.broadcast_to([128, NB]), sbest),
                       reads=["qBK", "SHR", "QUPD"], writes=["qBK"])
                p.emit("v", lambda e: e.tensor_tensor(bscore, tmpb, bscore, Alu.min),
                       reads=["qBK"], writes=["qBK"])
            yield
        p.emit("v", lambda e: e.tensor_tensor(bsc2, sbest, base_, Alu.mult),
               reads=["qBK"], writes=["qBK"])
        p.emit("v", lambda e: e.reciprocal(isc, bsc2), reads=["qBK"], writes=["qBK"])
        ib = isc.unsqueeze(2).broadcast_to([128, NB, 64])
        sbb = bsc2.unsqueeze(2).broadcast_to([128, NB, 64])
        p.emit("v", lambda e, wc3=wc3, ib=ib: e.tensor_tensor(
            qB[:].rearrange("p (nb b) -> p nb b", b=64), wc3, ib, Alu.mult),
            reads=[f"qw{slot}", "qBK"], writes=["qB"])
        qa3 = qA[:].rearrange("p (nb b) -> p nb b", b=64)
        qb3 = qB[:].rearrange("p (nb b) -> p nb b", b=64)
        p.emit("v", lambda e, sbb=sbb: e.tensor_tensor(qa3, qb3, sbb, Alu.mult),
               reads=["qB", "qBK"], writes=["qA"])
        p.emit("v", lambda e, wc3=wc3: e.tensor_tensor(qa3, wc3, qa3, Alu.subtract),
               reads=["qA", f"qw{slot}"], writes=["qA"])
        p.emit("v", lambda e, ib=ib: e.tensor_tensor(qa3, qa3, ib, Alu.mult),
               reads=["qA", "qBK"], writes=["qA"])
        p.emit("v", lambda e: e.tensor_tensor(qB[:], qB[:], qA[:], Alu.add),
               reads=["qA", "qB"], writes=["qB"])
        p.emit("s", lambda e: e.activation(qI[:], qB[:], AFT.Copy), reads=["qB"], writes=["qI"])
        p.emit("v", lambda e: e.tensor_scalar(qA[:], qI[:], 15.0, -15.0, Alu.min, Alu.max),
               reads=["qI"], writes=["qA"])
        sb = bsc2.unsqueeze(2).broadcast_to([128, NB, 64])
        p.emit("v", lambda e, sb=sb: e.tensor_tensor(
            qdq[:].rearrange("p (nb b) -> p nb b", b=64),
            qA[:].rearrange("p (nb b) -> p nb b", b=64), sb, Alu.mult),
            reads=["qA", "qBK"], writes=["qdq"])
        yield
        for i in range(8):
            p.emit("pe", lambda e, i=i: e.transpose(ps_t[:, ts(i, 128)], qdq[:, ts(i, 128)],
                                                    g["IDE"][:]),
                   reads=["qdq", "IDE"], writes=["ps_t"])
        p.emit("v", lambda e: e.tensor_copy(qT[:], ps_t[:, 0:1024]),
               reads=["ps_t"], writes=["qT"])
        dst = qcin_flat[off:off + 128 * cw].rearrange("(p f) -> p f", p=128)[
            :, part * 1024:(part + 1) * 1024]
        p.dma("g", dst, qT[:], reads=["qT"], writes=["q_cin"])
        yield


def quant_allgather(nc, p, g):
    p.emit("g", lambda e: e.collective_compute(
        "AllGather", Alu.bypass, replica_groups=[[0, 1, 2, 3, 4, 5, 6, 7]],
        ins=[g["q_cin"][:]], outs=[g["q_cout"][:]]),
        reads=["q_cin"], writes=["q_cout"], sem="ccc", inc=1)


def ffn(nc, p, g, call):
    AB1, AB2, AB3, STR = g["AB1"], g["AB2"], g["AB3"], g["STR"]
    ps_a, ps_b = g["ps_a"], g["ps_b"]
    tagln = "ln2" if call == 0 else "ln4"
    if call == 0:
        p.fence_all()
    QSH_FC = 8 * 2048 * 128
    qcof = [g["q_cout"][r].rearrange("a b -> (a b)") for r in range(8)]
    for sub in range(T // TSUB2):
        sl = slice(sub * TSUB2, (sub + 1) * TSUB2)
        for mh in range(64):
            r, cih = mh // 8, mh % 8
            slot = mh % 2
            slab = STR[:, slot * 2048:slot * 2048 + 2048]
            src = qcof[r][cih * 2048 * 128:(cih + 1) * 2048 * 128]
            p.dma("sp", slab, src.rearrange("(p f) -> p f", p=128),
                  reads=["q_cout"], writes=[f"STR{slot}"])
            stiles = [slab[:, ts(_k, 128)] for _k in range(16)]
            pp, ppn = (ps_a, "ps_a") if mh % 2 == 0 else (ps_b, "ps_b")
            for kc in range(NCH):
                mov = AB1[:, ts(kc, T)][:, sl]
                p.emit("pe", lambda e, stiles=stiles, kc=kc, mov=mov, pp=pp: e.matmul(
                    pp[:, 0:TSUB2], stiles[kc], mov, start=(kc == 0), stop=(kc == NCH - 1)),
                    reads=[f"STR{slot}", f"{tagln}_{kc}", f"AB1:{kc}"], writes=[ppn])
            harena, hnm = (AB2, "AB2") if mh < 32 else (AB3, "AB3")
            hslot = mh % 32
            hdst = harena[:, hslot * TSUB2:(hslot + 1) * TSUB2]
            hf32 = g["SMALL"][:, 0:TSUB2] if mh % 2 == 0 else g["RSC"][:, 0:TSUB2]
            hfn = "SMALL" if mh % 2 == 0 else "RSC"
            p.emit("s", lambda e, hf32=hf32, pp=pp: e.activation(hf32, pp[:, 0:TSUB2], AFT.Relu),
                   reads=[ppn], writes=[hfn])
            p.emit("s", lambda e, hdst=hdst, hf32=hf32: e.activation(hdst, hf32, AFT.Square),
                   reads=[hfn], writes=[f"h{mh}", f"{hnm}:{hslot // 2}"])
        if KDBG == "hid0" and call == 0 and sub == 0:
            for sl2 in range(16):
                p.dma("g", g["y"][ts(sl2, 128), 0:1024],
                      (g["AB2"] if sl2 < 8 else g["AB3"])[:, ts(sl2 % 8, 2048)][:, 0:1024],
                      reads=[f"h{sl2 * 4}", f"h{sl2 * 4 + 1}", f"h{sl2 * 4 + 2}", f"h{sl2 * 4 + 3}"])
        for mo in range(NCH):
            r, cio = mo // 2, mo % 2
            for quarter in range(4):
                slot = quarter % 2
                slab = STR[:, slot * 2048:slot * 2048 + 2048]
                base_el = QSH_FC + cio * 8192 * 128
                src = qcof[r][base_el:base_el + 8192 * 128].rearrange(
                    "(p f) -> p f", p=128)[:, quarter * 2048:(quarter + 1) * 2048]
                p.dma("sp", slab, src, reads=["q_cout"], writes=[f"STR{slot}"])
                stiles = [slab[:, ts(_k, 128)] for _k in range(16)]
                pp, ppn = (ps_a, "ps_a") if mo % 2 == 0 else (ps_b, "ps_b")
                for kq in range(16):
                    kh = quarter * 16 + kq
                    harena = AB2 if kh < 32 else AB3
                    hs = kh % 32
                    mov = harena[:, hs * TSUB2:(hs + 1) * TSUB2]
                    p.emit("pe", lambda e, stiles=stiles, kq=kq, mov=mov, kh=kh, pp=pp: e.matmul(
                        pp[:, 0:TSUB2], stiles[kq], mov, start=(kh == 0), stop=(kh == 63)),
                        reads=[f"STR{slot}", f"h{kh}"], writes=[ppn])
            xc = g["A_x"][:, ts(mo, T)][:, sl]
            p.emit("v", lambda e, xc=xc, pp=pp: e.tensor_tensor(xc, xc, pp[:, 0:TSUB2], Alu.add),
                   reads=[ppn, f"x{mo}"], writes=[f"x{mo}"])


def attn_b(nc, p, g):
    AB1, AB2, AB3, KVB, VTM, STR = (g["AB1"], g["AB2"], g["AB3"], g["KVB"],
                                    g["VTM"], g["STR"])
    ps_a, ps_b, ps_t = g["ps_a"], g["ps_b"], g["ps_t"]
    KPART = NCH * 128 * T
    kvin_flat = g["kv_cin"].rearrange("a b -> (a b)")
    for W, bias, kindnm in (("wqT", "bqT", "q"), ("wkT", "bkT", "k"), ("wvT", "bvT", "v")):
        for mo in range(NCH):
            stiles = stream_colblock(nc, p, g, g[W], W, D, mo, mo % 2)
            pq, pqn = (ps_a, "ps_a") if mo % 2 == 0 else (ps_b, "ps_b")
            for half in range(2):
                sl = slice(half * 512, half * 512 + 512)
                for kc in range(NCH):
                    mov = AB1[:, ts(kc, T)][:, sl]
                    p.emit("pe", lambda e, stiles=stiles, kc=kc, mov=mov, sl=sl, pq=pq: e.matmul(
                        pq[:, sl], stiles[kc], mov, start=(kc == 0), stop=(kc == NCH - 1)),
                        reads=[f"STR{mo % 2}", f"ln3_{kc}", f"AB1:{kc}"], writes=[pqn])
            bcol = g[bias + "_s"][:, mo:mo + 1]
            if kindnm == "q":
                dst = AB2[:, ts(mo, T)]
                p.emit("v", lambda e, dst=dst, bcol=bcol, pq=pq: e.tensor_single_scalar(
                    dst, pq[:, 0:T], bcol, Alu.add),
                    reads=[pqn], writes=[f"q{mo}", f"AB2:{mo}"])
            elif kindnm == "k":
                kb = VTM[:, 0:T]
                p.emit("v", lambda e, kb=kb, bcol=bcol, pq=pq: e.tensor_single_scalar(
                    kb, pq[:, 0:T], bcol, Alu.add), reads=[pqn], writes=["VTM"])
                dst = kvin_flat[mo * 128 * T:(mo + 1) * 128 * T].rearrange("(p f) -> p f", p=128)
                p.dma("g", dst, kb, reads=["VTM"], writes=["kv_cin"])
            else:
                vb = VTM[:, 0:T]
                p.emit("v", lambda e, vb=vb, bcol=bcol, pq=pq: e.tensor_single_scalar(
                    vb, pq[:, 0:T], bcol, Alu.add), reads=[pqn], writes=["VTM"])
                for i in range(8):
                    p.emit("pe", lambda e, i=i, vb=vb: e.transpose(
                        ps_t[:, ts(i, 128)], vb[:, ts(i, 128)], g["IDE"][:]),
                        reads=["VTM", "IDE"], writes=["ps_t"])
                vtb = g["SC2"][:, 0:T]
                p.emit("v", lambda e, vtb=vtb: e.tensor_copy(vtb, ps_t[:, 0:T]),
                       reads=["ps_t"], writes=["SC2"])
                dstap = kvin_flat[KPART + mo * 1024 * 128:
                                  KPART + (mo + 1) * 1024 * 128].rearrange(
                    "(p f) -> p f", p=128)
                p.dma("g", dstap, vtb, reads=["SC2"], writes=["kv_cin"])
    p.emit("g", lambda e: e.collective_compute(
        "AllGather", Alu.bypass, replica_groups=[[0, 1], [2, 3], [4, 5], [6, 7]],
        ins=[g["kv_cin"][:]], outs=[g["kv_cout"][:]]),
        reads=["kv_cin"], writes=["kv_cout"], sem="ccc", inc=1)

    kvof = [g["kv_cout"][r].rearrange("a b -> (a b)") for r in range(2)]
    for qp in range(4):
        qts = (2 * qp, 2 * qp + 1)
        for khalf in range(2):
            for kc in range(NCH):
                slot = kc % 2
                slab = KVB[:, slot * 2048:slot * 2048 + 1024]
                src = kvof[khalf][kc * 128 * T:(kc + 1) * 128 * T]
                p.dma("sp", slab, src.rearrange("(p f) -> p f", p=128),
                      reads=["kv_cout"], writes=[f"KVB{slot}"])
                for j, qt in enumerate(qts):
                    pp = ps_a if j == 0 else ps_b
                    ppn = "ps_a" if j == 0 else "ps_b"
                    stat = AB2[:, ts(kc, T)][:, ts(qt, 128)]
                    for nh in range(2):
                        p.emit("pe", lambda e, pp=pp, stat=stat, slab=slab, nh=nh, kc=kc:
                               e.matmul(pp[:, nh * 512:(nh + 1) * 512], stat,
                                        slab[:, nh * 512:(nh + 1) * 512],
                                        start=(kc == 0), stop=(kc == NCH - 1)),
                               reads=[f"KVB{slot}", f"q{kc}", f"AB2:{kc}"], writes=[ppn])
            for j, qt in enumerate(qts):
                pp = ps_a if j == 0 else ps_b
                ppn = "ps_a" if j == 0 else "ps_b"
                nmax = g["CST"][:, 92 + j:93 + j]
                ssum = g["CST"][:, 94 + j:95 + j]
                Pj = g["SC1"] if j == 0 else g["SC2"]
                Pjn = "SC1" if j == 0 else "SC2"
                P2 = KVB[:, 1024:2048] if j == 0 else KVB[:, 3072:4096]
                P2n = "KVBp0" if j == 0 else "KVBp1"
                if khalf == 0:
                    p.emit("v", lambda e, pp=pp, nmax=nmax: e.tensor_reduce(
                        nmax, pp[:, 0:T], AX.X, Alu.max, negate=True),
                        reads=[ppn], writes=[f"nmx{j}"])
                    p.emit("s", lambda e, pp=pp, Pj=Pj, nmax=nmax, ssum=ssum: e.activation(
                        Pj[:, 0:T], pp[:, 0:T], AFT.Exp, bias=nmax, scale=1.0,
                        accum_out=ssum),
                        reads=[ppn, f"nmx{j}"], writes=[Pjn, f"ssm{j}"])
                else:
                    nm2 = g["CST"][:, 96 + j:97 + j]
                    fac = g["CST"][:, 98 + j:99 + j]
                    s2 = g["CST"][:, 100 + j:101 + j]
                    p.emit("v", lambda e, pp=pp, nm2=nm2: e.tensor_reduce(
                        nm2, pp[:, 0:T], AX.X, Alu.max, negate=True),
                        reads=[ppn], writes=[f"nm2{j}"])
                    p.emit("v", lambda e, fac=fac, nmax=nmax, nm2=nm2: e.tensor_tensor(
                        fac, nmax, nm2, Alu.max),
                        reads=[f"nmx{j}", f"nm2{j}"], writes=[f"tmx{j}"])
                    p.emit("v", lambda e, fac=fac, nmax=nmax, nm2=nm2: e.tensor_tensor(
                        fac, nmax, nm2, Alu.min),
                        reads=[f"nmx{j}", f"nm2{j}"], writes=[f"tmx{j}"])
                    dfc = g["CST"][:, 102 + j:103 + j]
                    p.emit("v", lambda e, dfc=dfc, fac=fac, nmax=nmax: e.tensor_tensor(
                        dfc, fac, nmax, Alu.subtract),
                        reads=[f"tmx{j}", f"nmx{j}"], writes=[f"dfc{j}"])
                    p.emit("s", lambda e, dfc=dfc: e.activation(dfc, dfc, AFT.Exp),
                           reads=[f"dfc{j}"], writes=[f"dfc{j}"])
                    p.emit("v", lambda e, Pj=Pj, dfc=dfc: e.tensor_single_scalar(
                        Pj[:, 0:T], Pj[:, 0:T], dfc, Alu.mult),
                        reads=[Pjn, f"dfc{j}"], writes=[Pjn])
                    p.emit("v", lambda e, ssum=ssum, dfc=dfc: e.tensor_tensor(
                        ssum, ssum, dfc, Alu.mult),
                        reads=[f"ssm{j}", f"dfc{j}"], writes=[f"ssm{j}"])
                    p.emit("s", lambda e, pp=pp, P2=P2, fac=fac, s2=s2: e.activation(
                        P2, pp[:, 0:T], AFT.Exp, bias=fac, scale=1.0, accum_out=s2),
                        reads=[ppn, f"tmx{j}"], writes=[P2n, f"s2{j}"])
                    p.emit("v", lambda e, ssum=ssum, s2=s2: e.tensor_tensor(
                        ssum, ssum, s2, Alu.add),
                        reads=[f"ssm{j}", f"s2{j}"], writes=[f"ssm{j}"])
                    p.emit("v", lambda e, ssum=ssum: e.reciprocal(ssum, ssum),
                           reads=[f"ssm{j}"], writes=[f"ssm{j}"])
                    p.emit("v", lambda e, Pj=Pj, ssum=ssum: e.tensor_single_scalar(
                        Pj[:, 0:T], Pj[:, 0:T], ssum, Alu.mult),
                        reads=[Pjn, f"ssm{j}"], writes=[Pjn])
                    p.emit("v", lambda e, P2=P2, ssum=ssum: e.tensor_single_scalar(
                        P2, P2, ssum, Alu.mult),
                        reads=[P2n, f"ssm{j}"], writes=[P2n])
                    for h2 in range(2):
                        srcP = Pj[:, 0:T] if h2 == 0 else P2
                        srcn = Pjn if h2 == 0 else P2n
                        for i in range(8):
                            p.emit("pe", lambda e, srcP=srcP, i=i: e.transpose(
                                ps_t[:, ts(i, 128)], srcP[:, ts(i, 128)], g["IDE"][:]),
                                reads=[srcn, "IDE"], writes=["ps_t"])
                        for i in range(8):
                            kt = h2 * 8 + i
                            d = AB3[:, kt * 1024 + qt * 128: kt * 1024 + qt * 128 + 128]
                            p.emit("v", lambda e, d=d, i=i: e.tensor_copy(d, ps_t[:, ts(i, 128)]),
                                   reads=["ps_t"], writes=[f"Pt{kt}", f"AB3:{kt}"])
    for mo in range(NCH):
        slot = mo % 2
        slab = STR[:, slot * 2048:(slot + 1) * 2048]
        for r in range(2):
            srcap = kvof[r][KPART + mo * 1024 * 128:
                            KPART + (mo + 1) * 1024 * 128].rearrange("(p f) -> p f", p=128)
            p.dma("sp", slab[:, r * 1024:(r + 1) * 1024], srcap,
                  reads=["kv_cout"], writes=[f"STR{slot}"])
        stiles = [slab[:, ts(_k, 128)] for _k in range(16)]
        for kt in range(16):
            mov = AB3[:, kt * 1024:(kt + 1) * 1024]
            for nh in range(2):
                p.emit("pe", lambda e, stiles=stiles, kt=kt, mov=mov, nh=nh: e.matmul(
                    ps_a[:, nh * 512:(nh + 1) * 512], stiles[kt],
                    mov[:, nh * 512:(nh + 1) * 512], start=(kt == 0), stop=(kt == 15)),
                    reads=[f"STR{slot}", f"Pt{kt}", f"AB3:{kt}"], writes=["ps_a"])
        xc = g["A_x"][:, ts(mo, T)]
        p.emit("v", lambda e, xc=xc: e.tensor_tensor(xc, xc, ps_a[:, 0:T], Alu.add),
               reads=["ps_a", f"x{mo}"], writes=[f"x{mo}"])


def write_out(nc, p, g):
    if KDBG == "ln1":
        for ch in range(NCH):
            p.dma("g", g["y"][ts(ch, 128), :], g["AB1"][:, ts(ch, T)],
                  reads=[f"ln1_{ch}", f"AB1:{ch}"])
    elif KDBG == "q":
        for ch in range(NCH):
            p.dma("g", g["y"][ts(ch, 128), :], g["AB2"][:, ts(ch, T)],
                  reads=[f"q{ch}"])
    elif KDBG == "deq":
        for cch in range(8):
            srcq = g["q_cin"].rearrange("a b -> (a b)")[cch * 2048 * 128:(cch + 1) * 2048 * 128]
            p.dma("g", g["y"][ts(cch * 2, 128) if False else slice(0, 0), :] if False else
                  g["y"].rearrange("a b -> (a b)")[cch * 2048 * 128:(cch + 1) * 2048 * 128].rearrange("(p f) -> p f", p=128),
                  srcq.rearrange("(p f) -> p f", p=128), reads=["q_cin"])
    elif KDBG == "deqp":
        for cch in range(2):
            srcq = g["q_cin"].rearrange("a b -> (a b)")[
                8 * 2048 * 128 + cch * 8192 * 128: 8 * 2048 * 128 + (cch + 1) * 8192 * 128]
            p.dma("g", g["y"].rearrange("a b -> (a b)")[cch * 8192 * 128:(cch + 1) * 8192 * 128].rearrange("(p f) -> p f", p=128),
                  srcq.rearrange("(p f) -> p f", p=128), reads=["q_cin"])
    elif KDBG == "ag":
        for cch in range(8):
            srcq = g["q_cout"][3].rearrange("a b -> (a b)")[cch * 2048 * 128:(cch + 1) * 2048 * 128]
            p.dma("g", g["y"].rearrange("a b -> (a b)")[cch * 2048 * 128:(cch + 1) * 2048 * 128].rearrange("(p f) -> p f", p=128),
                  srcq.rearrange("(p f) -> p f", p=128), reads=["q_cout"])
    elif KDBG == "hid":
        for sl2 in range(16):
            p.dma("g", g["y"][ts(sl2, 128), :], g["AB2"][:, ts(sl2, 1024)],
                  reads=[f"h{sl2 * 2}", f"h{sl2 * 2 + 1}"])
    elif KDBG == "kv":
        p.dma("g", g["y"][0:128, 0:1024], g["KVB"][:, 0:1024], reads=["KVB"])
        p.dma("g", g["y"][128:256, 0:512], g["KVB"][:, 1024:1536], reads=["KVB"])
        p.dma("g", g["y"][256:384, 0:1024], g["VTM"][:, 0:1024], reads=["VTM"])
        p.dma("g", g["y"][384:512, 0:512], g["VTM"][:, 1024:1536], reads=["VTM"])
    elif KDBG == "ao":
        for ch in range(NCH):
            p.dma("g", g["y"][ts(ch, 128), :], g["AB1"][:, ts(ch, T)],
                  reads=[f"ao{ch}", f"AB1:{ch}"])
    elif KDBG == "hkb":
        p.dma("g", g["y"][0:128, 0:1024], g["HKB"], reads=["HKB"])
    elif KDBG == "hid0":
        pass
    else:
        for ch in range(NCH):
            p.dma("sp", g["y"][ts(ch, 128), :], g["A_x"][:, ts(ch, T)], reads=[f"x{ch}"])
    p.fence("sp")
    p.fence("g")


def kernel(**inputs):
    xs = np.asarray(inputs["L0_attn_a_h"], dtype=np.float32)
    nc = build_nc()
    in_maps = []
    f32 = np.float32
    for c in range(8):
        b, h = c // 2, c % 2
        xo = np.ascontiguousarray(xs[b, h * T:(h + 1) * T, :].T)
        if h == 1:
            xh = np.ascontiguousarray(xs[b, 512:1024, :].T)
        else:
            xh = np.zeros((D, HALO), f32)
        k_off = h * T - HALO
        hkx = np.zeros((128, 1024), f32)
        if h == 0:
            hkx[:, 0:512] = -1e30
        m = {
            "x_own": xo, "x_halo": xh,
            "posmeta": np.full((128, 1), k_off, f32),
            "hkx": hkx,
            "wq_comp": inputs["L0_attn_a_wq_comp"], "wq_up": inputs["L0_attn_a_wq_up"],
            "wk": inputs["L0_attn_a_wk"], "wv": inputs["L0_attn_a_wv"],
            "q_norm_w": np.asarray(inputs["L0_attn_a_q_norm_w"]).reshape(1, -1),
            "k_norm_w": np.asarray(inputs["L0_attn_a_k_norm_w"]).reshape(1, -1),
            "sinkrep": np.ascontiguousarray(np.broadcast_to(
                np.asarray(inputs["L0_attn_a_sink_logit"]).reshape(1, -1), (128, NHEAD))),
            "out_w1": inputs["L0_attn_a_out_w1"], "out_w2": inputs["L0_attn_a_out_w2"],
            "wqT": np.ascontiguousarray(np.asarray(inputs["L0_attn_b_Wq"]).T),
            "wkT": np.ascontiguousarray(np.asarray(inputs["L0_attn_b_Wk"]).T),
            "wvT": np.ascontiguousarray(np.asarray(inputs["L0_attn_b_Wv"]).T),
            "bqT": np.ascontiguousarray(np.asarray(inputs["L0_attn_b_bq"]).reshape(NCH, 128).T),
            "bkT": np.ascontiguousarray(np.asarray(inputs["L0_attn_b_bk"]).reshape(NCH, 128).T),
            "bvT": np.ascontiguousarray(np.asarray(inputs["L0_attn_b_bv"]).reshape(NCH, 128).T),
            "wfc_shard": np.ascontiguousarray(
                np.asarray(inputs["L0_ffn_w_fc"])[c * 1024:(c + 1) * 1024, :]),
            "wproj_shard": np.ascontiguousarray(
                np.asarray(inputs["L0_ffn_w_proj"])[c * 256:(c + 1) * 256, :]),
        }
        m = {k: np.asarray(v, dtype=f32) for k, v in m.items()}
        in_maps.append(m)
    res = run_bass_kernel_spmd(nc, in_maps, core_ids=list(range(8)))
    out = np.empty((4, 2048, 2048), np.float32)
    for c in range(8):
        b, h = c // 2, c % 2
        out[b, h * T:(h + 1) * T, :] = np.asarray(res.results[c]["y"]).T
    return out


# version 06 (final)
# speedup vs baseline: 1.0823x; 1.0720x over previous
import contextlib
import os
import numpy as np

import concourse.bass as bass
import concourse.mybir as mybir
from concourse.bass_utils import run_bass_kernel_spmd

F32 = mybir.dt.float32
BF16 = mybir.dt.bfloat16
I32 = mybir.dt.int32
Alu = mybir.AluOpType
AFT = mybir.ActivationFunctionType
AX = mybir.AxisListType

D = 2048
T = 1024
NCH = 16
HALO = 512
NQT = 8
NHEAD = 16
QC = 512
INTER = 8192

STAGE = int(os.environ.get("KSTAGE", "99"))
KDBG = os.environ.get("KDBG", "")
TSUB1 = 256
TSUB2 = 512

TWO_PI = 6.283185307179586
RC1 = float(np.float32(TWO_PI))
RC2 = float(np.float32(TWO_PI - RC1))
RC3 = float(np.float32(TWO_PI - RC1 - RC2))
INV2PI = float(np.float32(1.0 / TWO_PI))
PIO2 = float(np.float32(TWO_PI / 4.0))
PI_F = float(np.float32(TWO_PI / 2.0))


class P:

    ENGS = ("sp", "pe", "v", "s", "g")

    def __init__(self, nc):
        self.nc = nc
        self.q = {e: [] for e in self.ENGS}
        self.sem_count = {}
        self.sem_handles = {}
        self.last_w = {}
        self.readers = {}
        self.waited = {e: {} for e in self.ENGS}
        self._stack = contextlib.ExitStack()
        for name in ("dsp", "dg", "dsc", "cpe", "cv", "cs", "cg", "ccc"):
            self.sem_handles[name] = self._stack.enter_context(nc.semaphore(name))
            self.sem_count[name] = 0

    def close(self):
        self._stack.close()

    def _wait(self, eng, sem, val):
        if val <= 0 or self.waited[eng].get(sem, 0) >= val:
            return
        self.waited[eng][sem] = val
        self.q[eng].append(("w", self.sem_handles[sem], val))

    def emit(self, eng, builder, reads=(), writes=(), sem=None, inc=None):
        for r in reads:
            lw = self.last_w.get(r)
            if lw:
                self._wait(eng, lw[0], lw[1])
        for r in writes:
            lw = self.last_w.get(r)
            if lw:
                self._wait(eng, lw[0], lw[1])
            for s_, v_ in self.readers.get(r, {}).items():
                self._wait(eng, s_, v_)
        if sem is None:
            sem = {"sp": "dsp", "pe": "cpe", "v": "cv", "s": "cs", "g": "cg"}[eng]
            inc_ = 16 if sem in ("dsp", "dg", "dsc") else 1
        else:
            inc_ = 1
        if inc is not None:
            inc_ = inc
        self.sem_count[sem] += inc_
        val = self.sem_count[sem]
        self.q[eng].append(("i", builder, self.sem_handles[sem], inc_))
        for r in writes:
            self.last_w[r] = (sem, val)
            self.readers[r] = {}
        for r in reads:
            d = self.readers.setdefault(r, {})
            d[sem] = max(d.get(sem, 0), val)
        return sem, val

    def dma(self, eng, out, in_, reads=(), writes=(), **kw):
        sem = {"sp": "dsp", "g": "dg", "s": "dsc"}[eng]
        return self.emit(eng, lambda e: e.dma_start(out=out, in_=in_, **kw),
                         reads=reads, writes=writes, sem=sem, inc=16)

    def fence(self, eng):
        for sem, cnt in self.sem_count.items():
            self._wait(eng, sem, cnt)

    def fence_all(self):
        for e in self.ENGS:
            self.fence(e)

    def finalize(self, block):
        def runner(lst):
            def f(e):
                for item in lst:
                    if item[0] == "w":
                        e.wait_ge(item[1], item[2])
                    else:
                        _, builder, h, inc_ = item
                        inst = builder(e)
                        if inc_:
                            inst.then_inc(h, inc_)
            return f
        block.sync(runner(self.q["sp"]))
        block.tensor(runner(self.q["pe"]))
        block.vector(runner(self.q["v"]))
        block.scalar(runner(self.q["s"]))
        block.gpsimd(runner(self.q["g"]))


def ts(i, n):
    return bass.ts(i, n)


def build_nc():
    nc = bass.Bass("TRN2", target_bir_lowering=False, num_devices=8)
    g = {"nc": nc}

    def inp(name, shape, dtype=F32):
        g[name] = nc.dram_tensor(name, shape, dtype, kind="ExternalInput").ap()

    inp("x_own", [D, T])
    inp("x_halo", [D, HALO])
    inp("posmeta", [128, 1])
    inp("hkx", [128, 1024])
    inp("wq_comp", [D, QC])
    inp("wq_up", [QC, D])
    inp("wk", [D, 128])
    inp("wv", [D, 128])
    inp("q_norm_w", [1, 128])
    inp("k_norm_w", [1, 128])
    inp("sinkrep", [128, NHEAD])
    inp("out_w1", [4, QC, D])
    inp("out_w2", [INTER, D])
    inp("wqT", [D, D])
    inp("wkT", [D, D])
    inp("wvT", [D, D])
    inp("bqT", [128, NCH])
    inp("bkT", [128, NCH])
    inp("bvT", [128, NCH])
    inp("wfc_shard", [INTER // 8, D])
    inp("wproj_shard", [D // 8, INTER])
    g["y"] = nc.dram_tensor("y", [D, T], F32, kind="ExternalOutput").ap()

    g["w1bf"] = nc.dram_tensor("w1bf", [4, QC, D], BF16, kind="Internal").ap()
    g["w2bf"] = nc.dram_tensor("w2bf", [NCH, INTER, 128], BF16, kind="Internal").ap()
    QSH = 8 * 2048 * 128 + 2 * 8192 * 128
    g["q_cin"] = nc.dram_tensor("q_cin", [QSH // 512, 512], BF16, kind="Internal").ap()
    g["q_cout"] = nc.dram_tensor("q_cout", [8, QSH // 512, 512], BF16, kind="Internal").ap()
    KVSH = NCH * 128 * T + 8 * 128 * D
    g["kv_cin"] = nc.dram_tensor("kv_cin", [KVSH // 512, 512], BF16, kind="Internal").ap()
    g["kv_cout"] = nc.dram_tensor("kv_cout", [2, KVSH // 512, 512], BF16, kind="Internal").ap()

    def sball(name, shape, dtype):
        g[name] = nc.alloc_sbuf_tensor(name, shape, dtype).ap()

    sball("A_x", [128, NCH * T], F32)
    sball("AB1", [128, 16384], BF16)
    sball("AB2", [128, 16384], BF16)
    start, _ = nc.bump_sbuf(32768)
    g["AB3"] = nc.alloc_sbuf_tensor_at("AB3", [128, 16384], BF16, offset=start).ap()
    g["qw"] = nc.alloc_sbuf_tensor_at("qw", [128, 2048], F32, offset=start).ap()
    g["qu"] = nc.alloc_sbuf_tensor_at("qu", [128, 1024], F32, offset=start + 8192).ap()
    g["qA"] = nc.alloc_sbuf_tensor_at("qA", [128, 1024], F32, offset=start + 12288).ap()
    g["qB"] = nc.alloc_sbuf_tensor_at("qB", [128, 1024], F32, offset=start + 16384).ap()
    g["qI"] = nc.alloc_sbuf_tensor_at("qI", [128, 1024], I32, offset=start + 20480).ap()
    g["qdq"] = nc.alloc_sbuf_tensor_at("qdq", [128, 1024], BF16, offset=start + 24576).ap()
    g["qT"] = nc.alloc_sbuf_tensor_at("qT", [128, 1024], BF16, offset=start + 26624).ap()
    g["qBK"] = nc.alloc_sbuf_tensor_at("qBK", [128, 160], F32, offset=start + 28672).ap()

    sball("STR", [128, 4096], BF16)
    sball("KVB", [128, 4096], BF16)
    sball("VTM", [128, 1536], BF16)
    sball("SC1", [128, 1024], BF16)
    sball("SC2", [128, 1024], BF16)
    sball("SMALL", [128, 1024], F32)
    sball("RSC", [128, 1024], F32)
    sball("ROPE", [64, 1024], F32)
    sball("KKI", [32, 1024], I32)
    sball("CST", [128, 512], F32)
    sball("CSTB", [128, 768], BF16)
    sball("IDE", [128, 128], BF16)
    sball("MSKB", [128, 256], BF16)
    sball("SHR", [128, 48], F32)
    sball("ROWF", [1, 1024], F32)
    sball("QUPD", [128, 16], I32)
    sball("CEPS", [128, 2], F32)
    nc.const_aps.register(F32, 1e-5, g["CEPS"][:, 0:1])
    nc.const_aps.register(F32, 1e-6, g["CEPS"][:, 1:2])

    g["ps_a"] = nc.alloc_psum_tensor("ps_a", [128, 1024], F32).ap()
    g["ps_b"] = nc.alloc_psum_tensor("ps_b", [128, 1024], F32).ap()
    g["ps_c"] = nc.alloc_psum_tensor("ps_c", [128, 1024], F32).ap()
    g["ps_t"] = nc.alloc_psum_tensor("ps_t", [128, 1024], BF16).ap()
    g["ps_t2"] = nc.alloc_psum_tensor("ps_t2", [128, 1024], BF16).ap()

    assert nc.sbuf_bytes_remaining >= 0

    with nc.Block() as block:
        p = P(nc)
        build_program(nc, p, g)
        p.finalize(block)
        p.close()
    return nc


def build_program(nc, p, g):
    build_consts(nc, p, g)
    qgen = quant_generator(nc, p, g) if STAGE >= 5 else None

    def pump(n):
        if qgen is None:
            return
        for _ in range(n):
            try:
                next(qgen)
            except StopIteration:
                break

    load_x(nc, p, g)
    layer_norm(nc, p, g, dst=g["AB1"], tag="ln1")
    pump(30)
    if STAGE >= 2:
        attn_a_qkv(nc, p, g, pump)
    if STAGE >= 3:
        attn_a_scores(nc, p, g, pump)
    if STAGE >= 4:
        attn_a_out(nc, p, g, pump)
    if STAGE >= 5:
        pump(10 ** 9)
        quant_allgather(nc, p, g)
        layer_norm(nc, p, g, dst=g["AB1"], tag="ln2")
        ffn(nc, p, g, call=0)
    if STAGE >= 6:
        layer_norm(nc, p, g, dst=g["AB1"], tag="ln3")
        attn_b(nc, p, g)
    if STAGE >= 7:
        layer_norm(nc, p, g, dst=g["AB1"], tag="ln4")
        ffn(nc, p, g, call=1)
    write_out(nc, p, g)


def load_x(nc, p, g):
    for ch in range(NCH):
        p.dma("sp", g["A_x"][:, ts(ch, T)], g["x_own"][ts(ch, 128), :], writes=[f"x{ch}"])


def build_consts(nc, p, g):
    CST, CSTB, MSKB, SHR = g["CST"], g["CSTB"], g["MSKB"], g["SHR"]
    p.emit("g", lambda e: e.memset(g["CEPS"][:, 0:1], 1e-5), writes=["CEPS"])
    p.emit("g", lambda e: e.memset(g["CEPS"][:, 1:2], 1e-6), reads=["CEPS"], writes=["CEPS"])
    p.emit("g", lambda e: e.memset(g["IDE"][:], 0.0), writes=["IDE"])
    p.emit("g", lambda e: e.affine_select(g["IDE"][:], g["IDE"][:], pattern=[[-1, 128]],
                                          compare_op=Alu.not_equal, fill=1.0, base=0,
                                          channel_multiplier=1), reads=["IDE"], writes=["IDE"])
    M0 = MSKB[:, 0:128]
    M4 = MSKB[:, 128:256]
    p.emit("g", lambda e: e.memset(M0, 0.0), writes=["M0"])
    p.emit("g", lambda e: e.affine_select(M0, M0, pattern=[[-1, 128]], compare_op=Alu.is_ge,
                                          fill=-1e30, base=0, channel_multiplier=1),
           reads=["M0"], writes=["M0"])
    p.emit("g", lambda e: e.memset(M4, 0.0), writes=["M4"])
    p.emit("g", lambda e: e.affine_select(M4, M4, pattern=[[1, 128]], compare_op=Alu.is_ge,
                                          fill=-1e30, base=-1, channel_multiplier=-1),
           reads=["M4"], writes=["M4"])
    consts = {"c2": (0, 2.0), "c5": (1, 5.0), "c1": (2, 1.0), "c15": (3, 15.0),
              "c128": (4, 128.0), "cth": (5, 10000.0)}
    for nm, (i, vv) in consts.items():
        g[nm] = CST[:, i:i + 1]
        p.emit("g", lambda e, col=g[nm], vv=vv: e.memset(col, vv), writes=["CST"])
    c04 = CST[:, 8:9]; cstep = CST[:, 9:10]
    g["rs128"] = CST[:, 10:11]
    sq128 = CST[:, 11:12]
    r5 = CST[:, 12:13]
    g["r15"] = CST[:, 13:14]
    p.emit("v", lambda e: e.reciprocal(r5, g["c5"]), reads=["CST"], writes=["CST"])
    p.emit("v", lambda e: e.reciprocal(g["r15"], g["c15"]), reads=["CST"], writes=["CST"])
    p.emit("v", lambda e: e.tensor_tensor(c04, g["c2"], r5, Alu.mult),
           reads=["CST"], writes=["CST"])
    p.emit("v", lambda e: e.tensor_tensor(cstep, g["c1"], c04, Alu.subtract),
           reads=["CST"], writes=["CST"])
    p.emit("v", lambda e: e.tensor_tensor(cstep, cstep, g["r15"], Alu.mult),
           reads=["CST"], writes=["CST"])
    p.emit("s", lambda e: e.activation(sq128, g["c128"], AFT.Sqrt),
           reads=["CST"], writes=["CST"])
    p.emit("v", lambda e: e.reciprocal(g["rs128"], sq128), reads=["CST"], writes=["CST"])
    SH_S = SHR[:, 0:16]; SH_I = SHR[:, 16:32]; kio = SHR[:, 32:48]
    g["SH_S"], g["SH_I"] = SH_S, SH_I
    g["SH_S2"] = SHR[:, 16:32]
    p.emit("g", lambda e: e.iota(kio, pattern=[[1, 16]], base=0, channel_multiplier=0,
                                 allow_small_or_imprecise_dtypes=True), writes=["SHR"])
    p.emit("v", lambda e: e.tensor_scalar(SH_S, kio, cstep, c04, Alu.mult, Alu.add),
           reads=["SHR", "CST"], writes=["SHR"])
    p.emit("g", lambda e: e.memset(SH_S[:, 15:16], 1.0), reads=["SHR"], writes=["SHR"])
    p.emit("v", lambda e: e.tensor_tensor(g["SH_S2"], SH_S, SH_S, Alu.mult),
           reads=["SHR"], writes=["SHR"])
    g["nr15"] = CST[:, 15:16]
    p.emit("v", lambda e: e.tensor_single_scalar(g["nr15"], g["r15"], -1.0, Alu.mult),
           reads=["CST"], writes=["CST"])
    lnth32 = CST[0:32, 16:17]
    p.emit("s", lambda e: e.activation(lnth32, g["cth"][0:32, :], AFT.Ln),
           reads=["CST"], writes=["CST"])
    icol = CST[0:32, 20:21]
    p.emit("g", lambda e: e.iota(icol, pattern=[[0, 1]], base=0, channel_multiplier=1,
                                 allow_small_or_imprecise_dtypes=True),
           reads=["CST"], writes=["CST"])
    p.emit("v", lambda e: e.tensor_single_scalar(icol, icol, -1.0 / 32.0, Alu.mult),
           reads=["CST"], writes=["CST"])
    p.emit("v", lambda e: e.tensor_tensor(icol, icol, lnth32, Alu.mult),
           reads=["CST"], writes=["CST"])
    p.emit("s", lambda e: e.activation(icol, icol, AFT.Exp), reads=["CST"], writes=["CST"])
    g["invcol"] = icol
    g["CSTinv_dummy"] = None
    p.emit("v", lambda e: e.tensor_copy(CST[0:32, 22:23], icol), reads=["CST"], writes=["CSTinv"])
    g["invcol"] = CST[0:32, 22:23]
    g["koff"] = CST[:, 14:15]
    p.dma("sp", g["koff"], g["posmeta"][:], reads=["CST"], writes=["CSTko"])
    g["onescol_b"] = CSTB[0:128, 0:1]
    p.emit("g", lambda e: e.memset(g["onescol_b"], 1.0), writes=["CSTB"])
    g["onesrow_b"] = CSTB[0:1, 4:132]
    p.emit("g", lambda e: e.memset(g["onesrow_b"], 1.0), reads=["CSTB"], writes=["CSTB"])
    g["sink_s"] = CST[:, 168:184]
    p.dma("sp", g["sink_s"], g["sinkrep"], writes=["CSTsink"])
    g["HKB"] = g["KVB"][:, 2816:3840]
    g["bqT_s"] = CST[:, 184:200]
    g["bkT_s"] = CST[:, 200:216]
    g["bvT_s"] = CST[:, 216:232]
    p.dma("sp", g["bqT_s"], g["bqT"], writes=["CSTbias"])
    p.dma("sp", g["bkT_s"], g["bkT"], writes=["CSTbias"])
    p.dma("sp", g["bvT_s"], g["bvT"], writes=["CSTbias"])
    g["negsink"] = CST[:, 64:80]
    p.emit("v", lambda e: e.tensor_single_scalar(g["negsink"], g["sink_s"], -1.0, Alu.mult),
           reads=["CSTsink"], writes=["CST"])
    qnwr = CST[0:1, 240:368]; knwr = CST[0:1, 368:496]; tmpr = g["SMALL"][0:1, 0:128]
    p.dma("sp", qnwr, g["q_norm_w"], reads=["CST"], writes=["CSTnw"])
    p.dma("sp", knwr, g["k_norm_w"], reads=["CSTnw"], writes=["CSTnw"])
    p.emit("v", lambda e: e.tensor_single_scalar(qnwr, qnwr, g["rs128"][0:1, 0:1], Alu.mult),
           reads=["CSTnw", "CST"], writes=["CSTnw"])
    CB = g["CSTB"]
    g["qnw_hi"], g["qnw_lo"] = CB[0:1, 132:260], CB[0:1, 260:388]
    g["knw_hi"], g["knw_lo"] = CB[0:1, 388:516], CB[0:1, 516:644]
    for srcr, hi, lo in ((qnwr, g["qnw_hi"], g["qnw_lo"]),
                         (knwr, g["knw_hi"], g["knw_lo"])):
        p.emit("v", lambda e, srcr=srcr, hi=hi: e.tensor_copy(hi, srcr),
               reads=["CSTnw"], writes=["QNWr"])
        p.emit("v", lambda e, srcr=srcr, hi=hi: e.tensor_tensor(
            tmpr, srcr, hi, Alu.subtract),
               reads=["CSTnw", "QNWr"], writes=["SMALL"])
        p.emit("v", lambda e, lo=lo: e.tensor_copy(lo, tmpr), reads=["SMALL"], writes=["QNWr"])
    if STAGE >= 4:
        for gi in range(4):
            p.dma("g", g["w1bf"][gi], g["out_w1"][gi], writes=[f"w1bf{gi}"])
        for mo in range(NCH):
            p.dma("g", g["w2bf"][mo], g["out_w2"][:, ts(mo, 128)], writes=[f"w2bf{mo}"])


def rope_tables(nc, p, g, pos_lo, n):
    RSC = g["RSC"]
    ang = RSC[0:32, 0:n]
    kkf = g["SMALL"][0:32, 0:n]
    kki = g["KKI"][0:32, 0:n]
    p.emit("g", lambda e: e.iota(ang, pattern=[[1, n]], base=0, channel_multiplier=0,
                                 allow_small_or_imprecise_dtypes=True), writes=["RSC"])
    p.emit("v", lambda e: e.tensor_single_scalar(ang, ang, float(pos_lo), Alu.add),
           reads=["RSC"], writes=["RSC"])
    p.emit("v", lambda e: e.tensor_single_scalar(ang, ang, g["koff"][0:32, 0:1], Alu.add),
           reads=["RSC", "CSTko"], writes=["RSC"])
    p.emit("v", lambda e: e.tensor_single_scalar(ang, ang, g["invcol"], Alu.mult),
           reads=["RSC", "CSTinv"], writes=["RSC"])
    p.emit("s", lambda e: e.activation(kki, ang, AFT.Copy, bias=0.0, scale=INV2PI),
           reads=["RSC"], writes=["KKI"])
    p.emit("v", lambda e: e.tensor_copy(kkf, kki), reads=["KKI"], writes=["SMALL"])
    for c in (RC1, RC2, RC3):
        p.emit("v", lambda e, c=c: e.scalar_tensor_tensor(ang, kkf, -c, ang, Alu.mult, Alu.add),
               reads=["RSC", "SMALL"], writes=["RSC"])
    p.emit("s", lambda e: e.activation(g["ROPE"][32:64, 0:n], ang, AFT.Sin),
           reads=["RSC"], writes=["rope_sin"])
    p.emit("v", lambda e: e.tensor_single_scalar(ang, ang, PIO2, Alu.add),
           reads=["RSC"], writes=["RSC"])
    p.emit("v", lambda e: e.tensor_single_scalar(kkf, ang, PI_F, Alu.is_gt),
           reads=["RSC"], writes=["SMALL"])
    p.emit("v", lambda e: e.scalar_tensor_tensor(ang, kkf, -RC1, ang, Alu.mult, Alu.add),
           reads=["RSC", "SMALL"], writes=["RSC"])
    p.emit("s", lambda e: e.activation(g["ROPE"][0:32, 0:n], ang, AFT.Sin),
           reads=["RSC"], writes=["rope_cos"])


def ln_stats_bcast(nc, p, g, n, sum_ps, sum_nm, sq_ps, sq_nm, m_ps, m_nm, r_ps, r_nm):
    mrow = g["RSC"][0:1, 0:n]
    vrow = g["SMALL"][0:1, 0:n]
    m2row = g["ROWF"][0:1, 0:n]
    p.emit("v", lambda e: e.tensor_single_scalar(mrow, sum_ps[0:1, 0:n], 1.0 / D, Alu.mult),
           reads=[sum_nm, sq_nm], writes=["RSC"])
    p.emit("v", lambda e: e.tensor_single_scalar(vrow, sq_ps[0:1, 0:n], 1.0 / D, Alu.mult),
           reads=[sum_nm, sq_nm], writes=["SMALL"])
    p.emit("s", lambda e: e.activation(m2row, mrow, AFT.Square),
           reads=["RSC"], writes=["ROWF"])
    p.emit("v", lambda e: e.tensor_tensor(vrow, vrow, m2row, Alu.subtract),
           reads=["SMALL", "ROWF"], writes=["SMALL"])
    p.emit("s", lambda e: e.activation(vrow, vrow, AFT.Sqrt, bias=1e-5, scale=1.0),
           reads=["SMALL", "CEPS"], writes=["SMALL"])
    p.emit("v", lambda e: e.reciprocal(vrow, vrow), reads=["SMALL"], writes=["SMALL"])
    bcast_row(nc, p, g, mrow, "RSC", m_ps, m_nm, n)
    bcast_row(nc, p, g, vrow, "SMALL", r_ps, r_nm, n)


def bcast_row(nc, p, g, row, row_nm, dest_ps, dest_nm, n):
    onesb = g["onesrow_b"][0:1, 0:128]
    bh = g["SC1"][0:1, 0:n]
    bl = g["SC2"][0:1, 0:n]
    p.emit("v", lambda e: e.tensor_copy(bh, row), reads=[row_nm], writes=["SC1"])
    p.emit("v", lambda e: e.tensor_tensor(bl, row, bh, Alu.subtract),
           reads=[row_nm, "SC1"], writes=["SC2"])
    for half in range((n + 511) // 512):
        sl = slice(half * 512, min(n, half * 512 + 512))
        p.emit("pe", lambda e, sl=sl: e.matmul(dest_ps[:, sl], onesb, bh[:, sl],
                                               start=True, stop=False),
               reads=["SC1", "CSTB"], writes=[dest_nm])
        p.emit("pe", lambda e, sl=sl: e.matmul(dest_ps[:, sl], onesb, bl[:, sl],
                                               start=False, stop=True),
               reads=["SC2", "CSTB"], writes=[dest_nm])


def layer_norm(nc, p, g, dst, tag):
    A_x, ps_a, ps_b, ps_c = g["A_x"], g["ps_a"], g["ps_b"], g["ps_c"]
    onescol = g["onescol_b"]
    for ch in range(NCH):
        xc = A_x[:, ts(ch, T)]
        xb = g["SC1"][:, 0:T]
        p.emit("v", lambda e, xc=xc, xb=xb: e.tensor_copy(xb, xc),
               reads=[f"x{ch}"], writes=["SC1"])
        for hf in range(2):
            shf = slice(hf * 512, hf * 512 + 512)
            p.emit("pe", lambda e, xb=xb, ch=ch, shf=shf: e.matmul(
                ps_c[0:1, shf], onescol, xb[:, shf],
                start=(ch == 0), stop=(ch == NCH - 1)),
                reads=["SC1", "CSTB"], writes=["ps_c"])
        x2 = g["SC2"][:, 0:T]
        p.emit("s", lambda e, xc=xc, x2=x2: e.activation(x2, xc, AFT.Square),
               reads=[f"x{ch}"], writes=["SC2"])
        for hf in range(2):
            shf = slice(hf * 512, hf * 512 + 512)
            p.emit("pe", lambda e, x2=x2, ch=ch, shf=shf: e.matmul(
                ps_b[0:1, shf], onescol, x2[:, shf],
                start=(ch == 0), stop=(ch == NCH - 1)),
                reads=["SC2", "CSTB"], writes=["ps_b"])
    ln_stats_bcast(nc, p, g, T, ps_c, "ps_c", ps_b, "ps_b", ps_a, "ps_a", ps_c, "ps_c")
    for ch in range(NCH):
        xc = A_x[:, ts(ch, T)]
        oc = dst[:, ts(ch, T)]
        tmpf = g["RSC"][:, 0:T]
        p.emit("v", lambda e, xc=xc: e.tensor_tensor(tmpf, xc, ps_a[:, 0:T], Alu.subtract),
               reads=[f"x{ch}", "ps_a"], writes=["RSC"])
        p.emit("v", lambda e, oc=oc: e.tensor_tensor(oc, tmpf, ps_c[:, 0:T], Alu.mult),
               reads=["RSC", "ps_c"], writes=[f"{tag}_{ch}", f"AB1:{ch}"])


def stream_colblock(nc, p, g, w_ap, wname, kdim, m, slot):
    STR = g["STR"]
    nk = kdim // 128
    slab = STR[:, slot * 2048: slot * 2048 + nk * 128]
    dst3 = slab.rearrange("p (k m) -> p k m", k=nk)
    src = w_ap[:, ts(m, 128)].rearrange("(k p) m -> p k m", p=128)
    p.dma("g", dst3, src, reads=[wname], writes=[f"STR{slot}"])
    return [slab[:, ts(k, 128)] for k in range(nk)]


def rms_head(nc, p, g, src, src_nm, dst, dst_nm, whi, wlo, n, stat_ps, stat_nm, bc_ps, bc_nm):
    sq = g["SC1"][:, 0:n]
    p.emit("s", lambda e: e.activation(sq, src[:, 0:n], AFT.Square),
           reads=[src_nm], writes=["SC1"])
    for half in range((n + 511) // 512):
        sl = slice(half * 512, min(n, half * 512 + 512))
        p.emit("pe", lambda e, sl=sl: e.matmul(stat_ps[0:1, sl], g["onescol_b"], sq[:, sl],
                                               start=True, stop=True),
               reads=["SC1", "CSTB"], writes=[stat_nm])
    rrow = g["ROWF"][0:1, 0:n]
    p.emit("s", lambda e: e.activation(rrow, stat_ps[0:1, 0:n], AFT.Sqrt, bias=1e-6,
                                       scale=1.0 / 128.0),
           reads=[stat_nm, "CEPS"], writes=["ROWF"])
    p.emit("v", lambda e: e.reciprocal(rrow, rrow), reads=["ROWF"], writes=["ROWF"])
    bh = g["SC1"][0:1, 0:n]
    bl = g["SC2"][0:1, 0:n]
    p.emit("v", lambda e: e.tensor_copy(bh, rrow), reads=["ROWF"], writes=["SC1"])
    p.emit("v", lambda e: e.tensor_tensor(bl, rrow, bh, Alu.subtract),
           reads=["ROWF", "SC1"], writes=["SC2"])
    for half in range((n + 511) // 512):
        sl = slice(half * 512, min(n, half * 512 + 512))
        p.emit("pe", lambda e, sl=sl: e.matmul(bc_ps[:, sl], whi, bh[:, sl],
                                               start=True, stop=False),
               reads=["SC1", "QNWr"], writes=[bc_nm])
        p.emit("pe", lambda e, sl=sl: e.matmul(bc_ps[:, sl], whi, bl[:, sl],
                                               start=False, stop=False),
               reads=["SC2", "QNWr"], writes=[bc_nm])
        p.emit("pe", lambda e, sl=sl: e.matmul(bc_ps[:, sl], wlo, bh[:, sl],
                                               start=False, stop=True),
               reads=["SC1", "QNWr"], writes=[bc_nm])
    p.emit("v", lambda e: e.tensor_tensor(dst[:, 0:n], src[:, 0:n], bc_ps[:, 0:n], Alu.mult),
           reads=[src_nm, bc_nm], writes=[dst_nm])


def rope_apply(nc, p, g, ps, ps_nm, out, out_nm, n):
    ROPE, RSC = g["ROPE"], g["RSC"]
    cos = ROPE[0:32, 0:n]
    sin = ROPE[32:64, 0:n]
    p.emit("v", lambda e: e.tensor_tensor(out[0:32, 0:n], ps[0:32, 0:n], cos, Alu.mult),
           reads=[ps_nm, "rope_cos"], writes=[out_nm])
    p.emit("v", lambda e: e.tensor_tensor(RSC[0:32, 0:n], ps[32:64, 0:n], sin, Alu.mult),
           reads=[ps_nm, "rope_sin"], writes=["RSC"])
    p.emit("v", lambda e: e.tensor_tensor(out[0:32, 0:n], out[0:32, 0:n], RSC[0:32, 0:n],
                                          Alu.subtract),
           reads=[out_nm, "RSC"], writes=[out_nm])
    p.emit("v", lambda e: e.tensor_tensor(out[32:64, 0:n], ps[0:32, 0:n], sin, Alu.mult),
           reads=[ps_nm, "rope_sin"], writes=[out_nm])
    p.emit("v", lambda e: e.tensor_tensor(RSC[32:64, 0:n], ps[32:64, 0:n], cos, Alu.mult),
           reads=[ps_nm, "rope_cos"], writes=["RSC"])
    p.emit("v", lambda e: e.tensor_tensor(out[32:64, 0:n], out[32:64, 0:n], RSC[32:64, 0:n],
                                          Alu.add),
           reads=[out_nm, "RSC"], writes=[out_nm])
    p.emit("v", lambda e: e.tensor_copy(out[64:128, 0:n], ps[64:128, 0:n]),
           reads=[ps_nm], writes=[out_nm])


def attn_a_qkv(nc, p, g, pump):
    AB1, AB2, KVB = g["AB1"], g["AB2"], g["KVB"]
    ps_a = g["ps_a"]
    for m in range(4):
        stiles = stream_colblock(nc, p, g, g["wq_comp"], "wq_comp", D, m, m % 2)
        pq, pqn = (ps_a, "ps_a") if m % 2 == 0 else (g["ps_b"], "ps_b")
        for half in range(2):
            sl = slice(half * 512, half * 512 + 512)
            for kc in range(NCH):
                mov = AB1[:, ts(kc, T)][:, sl]
                p.emit("pe", lambda e, stiles=stiles, kc=kc, mov=mov, sl=sl, pq=pq: e.matmul(
                    pq[:, sl], stiles[kc], mov, start=(kc == 0), stop=(kc == NCH - 1)),
                    reads=[f"STR{m % 2}", f"ln1_{kc}", f"AB1:{kc}"], writes=[pqn])
        qc = KVB[:, ts(m, T)]
        p.emit("v", lambda e, qc=qc, pq=pq: e.tensor_copy(qc, pq[:, 0:T]),
               reads=[pqn], writes=[f"qc{m}", "KVB"])
        pump(4)
    rope_tables(nc, p, g, 512, 1024)
    for h in range(NHEAD):
        stiles = stream_colblock(nc, p, g, g["wq_up"], "wq_up", QC, h, h % 2)
        pq, pqn = (ps_a, "ps_a") if h % 2 == 0 else (g["ps_b"], "ps_b")
        for half in range(2):
            sl = slice(half * 512, half * 512 + 512)
            for kq in range(4):
                p.emit("pe", lambda e, stiles=stiles, kq=kq, sl=sl, pq=pq: e.matmul(
                    pq[:, sl], stiles[kq], KVB[:, ts(kq, T)][:, sl],
                    start=(kq == 0), stop=(kq == 3)),
                    reads=[f"STR{h % 2}", f"qc{kq}"], writes=[pqn])
        rope_apply(nc, p, g, pq, pqn, g["SMALL"], "SMALL", T)
        rms_head(nc, p, g, g["SMALL"], "SMALL", AB2[:, ts(h, T)], f"q{h}",
                 g["qnw_hi"], g["qnw_lo"], T, g["ps_c"], "ps_c", g["ps_c"], "ps_c")
        pump(6)


def attn_a_scores(nc, p, g, pump):
    AB1, AB2, KVB, VTM = g["AB1"], g["AB2"], g["KVB"], g["VTM"]
    ps_a, ps_b, ps_c, ps_t = g["ps_a"], g["ps_b"], g["ps_c"], g["ps_t"]
    RSC = g["RSC"]
    p.dma("g", g["HKB"], g["hkx"], writes=["HKB", "qc2", "qc3"])
    kslab = stream_colblock(nc, p, g, g["wk"], "wk", D, 0, 0)
    vslab = stream_colblock(nc, p, g, g["wv"], "wv", D, 0, 1)
    for ch in range(NCH):
        slot = ch % 2
        xh = RSC[:, slot * 512:(slot + 1) * 512]
        p.dma("sp", xh, g["x_halo"][ts(ch, 128), :], writes=["RSC"])
        xb = g["SC1"][:, 0:512]
        p.emit("v", lambda e, xh=xh, xb=xb: e.tensor_copy(xb, xh),
               reads=["RSC"], writes=["SC1"])
        p.emit("pe", lambda e, xb=xb, ch=ch: e.matmul(ps_c[0:1, 0:512], g["onescol_b"], xb,
                                                      start=(ch == 0), stop=(ch == NCH - 1)),
               reads=["SC1", "CSTB"], writes=["ps_c"])
        x2 = g["SC2"][:, 0:512]
        p.emit("s", lambda e, xh=xh, x2=x2: e.activation(x2, xh, AFT.Square),
               reads=["RSC"], writes=["SC2"])
        p.emit("pe", lambda e, x2=x2, ch=ch: e.matmul(ps_b[0:1, 0:512], g["onescol_b"], x2,
                                                      start=(ch == 0), stop=(ch == NCH - 1)),
               reads=["SC2", "CSTB"], writes=["ps_b"])
    ln_stats_bcast(nc, p, g, 512, ps_c, "ps_c", ps_b, "ps_b", ps_a, "ps_a", ps_b, "ps_b")
    for blk in range(3):
        rope_tables(nc, p, g, blk * 512, 512)
        for kc in range(NCH):
            if blk == 0:
                slot = kc % 2
                xh = RSC[:, slot * 512:(slot + 1) * 512]
                p.dma("sp", xh, g["x_halo"][ts(kc, 128), :], writes=["RSC"])
                hl = g["SC1"][:, 0:512] if slot == 0 else g["SC1"][:, 512:1024]
                tmq = g["SMALL"][:, 0:512]
                p.emit("v", lambda e, xh=xh, tmq=tmq: e.tensor_tensor(
                    tmq, xh, ps_a[:, 0:512], Alu.subtract),
                    reads=["RSC", "ps_a"], writes=["SMALL"])
                p.emit("v", lambda e, hl=hl, tmq=tmq: e.tensor_tensor(
                    hl, tmq, ps_b[:, 0:512], Alu.mult),
                    reads=["SMALL", "ps_b"], writes=["SC1"])
                mov, mnm = hl, ["SC1"]
            else:
                off = (blk - 1) * 512
                mov = AB1[:, ts(kc, T)][:, off:off + 512]
                mnm = [f"ln1_{kc}", f"AB1:{kc}"]
            p.emit("pe", lambda e, mov=mov, kc=kc: e.matmul(
                ps_c[:, 0:512], kslab[kc], mov, start=(kc == 0), stop=(kc == NCH - 1)),
                reads=mnm + ["STR0"], writes=["ps_c"])
            p.emit("pe", lambda e, mov=mov, kc=kc: e.matmul(
                ps_c[:, 512:1024], vslab[kc], mov, start=(kc == 0), stop=(kc == NCH - 1)),
                reads=mnm + ["STR1"], writes=["ps_c"])
        rope_apply(nc, p, g, ps_c[:, 0:512], "ps_c", g["SMALL"], "SMALL", 512)
        vb = g["SC2"][:, 0:512]
        p.emit("v", lambda e: e.tensor_copy(vb, ps_c[:, 512:1024]),
               reads=["ps_c"], writes=["SC2"])
        for i in range(4):
            p.emit("pe", lambda e, i=i: e.transpose(ps_t[:, ts(i, 128)], vb[:, ts(i, 128)],
                                                    g["IDE"][:]),
                   reads=["SC2", "IDE"], writes=["ps_t"])
        p.emit("v", lambda e, blk=blk: e.tensor_copy(VTM[:, blk * 512:(blk + 1) * 512],
                                                     ps_t[:, 0:512]),
               reads=["ps_t"], writes=["VTM"])
        rms_head(nc, p, g, g["SMALL"], "SMALL", KVB[:, blk * 512:(blk + 1) * 512],
                 "KVB", g["knw_hi"], g["knw_lo"], 512,
                 g["ps_c"], "ps_c", g["ps_b"], "ps_b")
        pump(6)
    for h in range(NHEAD):
        outb = ps_b if h % 2 else ps_a
        outbnm = "ps_b" if h % 2 else "ps_a"
        altb = ps_a if h % 2 else ps_b
        altbnm = "ps_a" if h % 2 else "ps_b"
        for qt in range(NQT):
            klo = qt * 128
            if qt % 2 == 0:
                sc_ps, sc_nm = ps_c, "ps_c"
                Pr, Prn = g["SC1"][:, 0:640], "SC1"
                Pt2, Ptn = g["SC2"][:, 0:640], "SC2"
                tslot, tnm = ps_t[:, 0:640], "ps_t"
                nmax = g["CST"][:, 88:89]; nmn = "CSTn0"
                srow = g["CST"][:, 89:90]; srn = "CSTs0"
                esk = g["CST"][:, 90:91]; esn = "CSTe0"
            else:
                sc_ps, sc_nm = altb, altbnm
                Pr, Prn = KVB[:, 1536:2176], "KVBp"
                Pt2, Ptn = KVB[:, 2176:2816], "KVBq"
                tslot, tnm = g["ps_t2"][:, 0:640], "ps_t2"
                nmax = g["CST"][:, 91:92]; nmn = "CSTn1"
                srow = g["CST"][:, 92:93]; srn = "CSTs1"
                esk = g["CST"][:, 93:94]; esn = "CSTe1"
            stat = AB2[:, ts(h, T)][:, ts(qt, 128)]
            p.emit("pe", lambda e, stat=stat, klo=klo, sc_ps=sc_ps: e.matmul(
                sc_ps[:, 0:512], stat, KVB[:, klo:klo + 512], start=True, stop=True),
                reads=[f"q{h}", "KVB"], writes=[sc_nm])
            p.emit("pe", lambda e, stat=stat, klo=klo, sc_ps=sc_ps: e.matmul(
                sc_ps[:, 512:640], stat, KVB[:, klo + 512:klo + 640], start=True, stop=True),
                reads=[f"q{h}", "KVB"], writes=[sc_nm])
            if qt < 4:
                hk = g["HKB"][:, qt * 128:qt * 128 + 512]
                p.emit("v", lambda e, sc_ps=sc_ps, hk=hk: e.tensor_tensor(
                    sc_ps[:, 0:512], sc_ps[:, 0:512], hk, Alu.add),
                    reads=[sc_nm, "HKB"], writes=[sc_nm])
            p.emit("v", lambda e, sc_ps=sc_ps: e.tensor_tensor(
                sc_ps[:, 0:128], sc_ps[:, 0:128], g["MSKB"][:, 128:256], Alu.add),
                reads=[sc_nm, "M4"], writes=[sc_nm])
            p.emit("v", lambda e, sc_ps=sc_ps: e.tensor_tensor(
                sc_ps[:, 512:640], sc_ps[:, 512:640], g["MSKB"][:, 0:128], Alu.add),
                reads=[sc_nm, "M0"], writes=[sc_nm])
            p.emit("v", lambda e, sc_ps=sc_ps, nmax=nmax: e.tensor_reduce(
                nmax, sc_ps[:, 0:640], AX.X, Alu.max, negate=True),
                reads=[sc_nm], writes=[nmn])
            nsk = g["negsink"][:, h:h + 1]
            p.emit("v", lambda e, nsk=nsk, nmax=nmax: e.tensor_single_scalar(
                nmax, nmax, nsk, Alu.min),
                reads=[nmn, "CST"], writes=[nmn])
            p.emit("s", lambda e, Pr=Pr, sc_ps=sc_ps, nmax=nmax, srow=srow: e.activation(
                Pr, sc_ps[:, 0:640], AFT.Exp, bias=nmax, scale=1.0, accum_out=srow),
                reads=[sc_nm, nmn], writes=[Prn, srn])
            snk = g["sink_s"][:, h:h + 1]
            p.emit("s", lambda e, snk=snk, esk=esk, nmax=nmax: e.activation(
                esk, snk, AFT.Exp, bias=nmax, scale=1.0),
                reads=[nmn], writes=[esn])
            p.emit("v", lambda e, srow=srow, esk=esk: e.tensor_tensor(srow, srow, esk, Alu.add),
                   reads=[srn, esn], writes=[srn])
            p.emit("v", lambda e, srow=srow: e.reciprocal(srow, srow), reads=[srn], writes=[srn])
            p.emit("v", lambda e, Pr=Pr, srow=srow: e.tensor_single_scalar(
                Pr, Pr, srow, Alu.mult),
                reads=[Prn, srn], writes=[Prn])
            for i in range(5):
                p.emit("pe", lambda e, i=i, Pr=Pr, tslot=tslot: e.transpose(
                    tslot[:, ts(i, 128)], Pr[:, ts(i, 128)], g["IDE"][:]),
                    reads=[Prn, "IDE"], writes=[tnm])
            p.emit("v", lambda e, Pt2=Pt2, tslot=tslot: e.tensor_copy(Pt2, tslot[:, 0:640]),
                   reads=[tnm], writes=[Ptn])
            for i in range(5):
                kt = qt + i
                p.emit("pe", lambda e, i=i, kt=kt, outb=outb, qt=qt, Pt2=Pt2: e.matmul(
                    outb[:, ts(qt, 128)], VTM[:, ts(kt, 128)], Pt2[:, ts(i, 128)],
                    start=(i == 0), stop=(i == 4)),
                    reads=[Ptn, "VTM"], writes=[outbnm])
        ao = AB1[:, ts(h, T)]
        p.emit("v", lambda e, ao=ao, outb=outb: e.tensor_copy(ao, outb[:, 0:T]),
               reads=[outbnm], writes=[f"ao{h}", f"AB1:{h}"])
        pump(8)


def attn_a_out(nc, p, g, pump):
    AB1, AB2, STR = g["AB1"], g["AB2"], g["STR"]
    ps_a, ps_b = g["ps_a"], g["ps_b"]
    for sub in range(T // TSUB1):
        sl = slice(sub * TSUB1, (sub + 1) * TSUB1)
        for gi in range(4):
            for mi in range(NCH):
                slot = (gi * NCH + mi) % 2
                slab = STR[:, slot * 2048:slot * 2048 + 512]
                dst3 = slab.rearrange("p (k m) -> p k m", k=4)
                src = g["w1bf"][gi][:, ts(mi, 128)].rearrange("(k p) m -> p k m", p=128)
                p.dma("sp", dst3, src, reads=[f"w1bf{gi}"], writes=[f"STR{slot}"])
                stiles = [slab[:, ts(_k, 128)] for _k in range(4)]
                ic = gi * NCH + mi
                pp, ppn = (ps_a, "ps_a") if ic % 2 == 0 else (g["ps_c"], "ps_c")
                for kg in range(4):
                    hh = 4 * gi + kg
                    mov = AB1[:, ts(hh, T)][:, sl]
                    p.emit("pe", lambda e, stiles=stiles, kg=kg, mov=mov, pp=pp: e.matmul(
                        pp[:, 0:TSUB1], stiles[kg], mov, start=(kg == 0), stop=(kg == 3)),
                        reads=[f"STR{slot}", f"ao{hh}", f"AB1:{hh}"], writes=[ppn])
                dst = AB2[:, ic * TSUB1:(ic + 1) * TSUB1]
                p.emit("v", lambda e, dst=dst, pp=pp: e.tensor_copy(dst, pp[:, 0:TSUB1]),
                       reads=[ppn], writes=[f"ic{ic}", f"AB2:{ic // 4}"])
            pump(2)
        for mo in range(NCH):
            for quarter in range(4):
                slot = quarter % 2
                slab = STR[:, slot * 2048:slot * 2048 + 2048]
                dst3 = slab.rearrange("p (k m) -> p k m", k=16)
                src = g["w2bf"][mo][quarter * 2048:(quarter + 1) * 2048, :].rearrange(
                    "(k p) m -> p k m", p=128)
                p.dma("sp", dst3, src, reads=[f"w2bf{mo}"], writes=[f"STR{slot}"])
                stiles = [slab[:, ts(_k, 128)] for _k in range(16)]
                pp, ppn = (ps_b, "ps_b") if mo % 2 == 0 else (g["ps_c"], "ps_c")
                for kq in range(16):
                    ki = quarter * 16 + kq
                    mov = AB2[:, ki * TSUB1:(ki + 1) * TSUB1]
                    p.emit("pe", lambda e, stiles=stiles, kq=kq, mov=mov, ki=ki, pp=pp: e.matmul(
                        pp[:, 0:TSUB1], stiles[kq], mov, start=(ki == 0), stop=(ki == 63)),
                        reads=[f"STR{slot}", f"ic{ki}", f"AB2:{ki // 4}"], writes=[ppn])
            xc = g["A_x"][:, ts(mo, T)][:, sl]
            p.emit("v", lambda e, xc=xc, pp=pp: e.tensor_tensor(xc, xc, pp[:, 0:TSUB1], Alu.add),
                   reads=[ppn, f"x{mo}"], writes=[f"x{mo}"])
            pump(2)


def quant_generator(nc, p, g):
    qw, qA, qB, qI, qdq, qT, qBK = (g["qw"], g["qA"], g["qB"],
                                    g["qI"], g["qdq"], g["qT"], g["qBK"])
    SH_S, SH_S2 = g["SH_S"], g["SH_S2"]
    ps_t = g["ps_t"]
    chunk_srcs = []
    for cch in range(8):
        for half in range(2):
            chunk_srcs.append((g["wfc_shard"][ts(cch, 128), half * 1024:(half + 1) * 1024],
                               cch * 2048 * 128, 2048, half))
    for cch in range(2):
        for part in range(8):
            chunk_srcs.append((g["wproj_shard"][ts(cch, 128), part * 1024:(part + 1) * 1024],
                               8 * 2048 * 128 + cch * 8192 * 128, 8192, part))
    NB = 16
    amax = qBK[:, 0:NB]; base_ = qBK[:, NB:2 * NB]; scl = qBK[:, 2 * NB:3 * NB]
    isc = qBK[:, 3 * NB:4 * NB]; bscore = qBK[:, 4 * NB:5 * NB]
    tmpb = qBK[:, 5 * NB:6 * NB]; nres = qBK[:, 6 * NB:7 * NB]; sbest = qBK[:, 7 * NB:8 * NB]
    bsc2 = qBK[:, 8 * NB:9 * NB]
    qcin_flat = g["q_cin"].rearrange("a b -> (a b)")
    for ci, (src, off, cw, part) in enumerate(chunk_srcs):
        slot = ci % 2
        wc = qw[:, slot * 1024:(slot + 1) * 1024]
        wc3 = wc.rearrange("p (nb b) -> p nb b", b=64)
        p.dma("sp", wc, src, writes=[f"qw{slot}"])
        yield
        p.emit("v", lambda e, wc3=wc3: e.tensor_reduce(amax, wc3, AX.X, Alu.max,
                                                       apply_absolute_value=True),
               reads=[f"qw{slot}"], writes=["qBK"])
        p.emit("v", lambda e: e.tensor_single_scalar(amax, amax, 1e-8, Alu.max),
               reads=["qBK"], writes=["qBK"])
        p.emit("v", lambda e: e.tensor_single_scalar(base_, amax, g["r15"], Alu.mult),
               reads=["qBK", "CST"], writes=["qBK"])
        yield
        tbufs = [(qB[:], "qB"), (g["qu"][:], "qu")]

        def emit_B(j):
            s2 = SH_S2[:, j:j + 1]
            sS = SH_S[:, j:j + 1]
            tb, tbn = tbufs[j % 2]
            tb3 = tb.rearrange("p (nb b) -> p nb b", b=64)
            p.emit("v", lambda e, tb3=tb3: e.tensor_reduce(tmpb, tb3, AX.X, Alu.add),
                   reads=[tbn], writes=["qBK"])
            p.emit("v", lambda e, s2=s2: e.tensor_single_scalar(tmpb, tmpb, s2, Alu.mult),
                   reads=["qBK", "SHR"], writes=["qBK"])
            if j == 0:
                p.emit("v", lambda e, sS=sS: e.tensor_copy(sbest, sS.broadcast_to([128, NB])),
                       reads=["SHR", "qBK"], writes=["qBK"])
                p.emit("v", lambda e: e.tensor_copy(bscore, tmpb),
                       reads=["qBK"], writes=["qBK"])
            else:
                p.emit("v", lambda e: e.tensor_tensor(g["QUPD"][:], tmpb, bscore, Alu.is_lt),
                       reads=["qBK"], writes=["QUPD"])
                p.emit("v", lambda e, sS=sS: e.select(sbest, g["QUPD"][:],
                                                      sS.broadcast_to([128, NB]), sbest),
                       reads=["qBK", "SHR", "QUPD"], writes=["qBK"])
                p.emit("v", lambda e: e.tensor_tensor(bscore, tmpb, bscore, Alu.min),
                       reads=["qBK"], writes=["qBK"])

        for si in range(18):
            if si < 16:
                sS = SH_S[:, si:si + 1]
                p.emit("v", lambda e, sS=sS: e.tensor_single_scalar(scl, base_, sS, Alu.mult),
                       reads=["qBK", "SHR"], writes=["qBK"])
                p.emit("v", lambda e: e.reciprocal(isc, scl), reads=["qBK"], writes=["qBK"])
            if si >= 2:
                emit_B(si - 2)
            if 0 <= si - 1 < 16:
                tb, tbn = tbufs[(si - 1) % 2]
                p.emit("v", lambda e: e.tensor_scalar(qA[:], qI[:], 15.0, -15.0,
                                                      Alu.min, Alu.max),
                       reads=["qI"], writes=["qA"])
                p.emit("v", lambda e, tb=tb: e.tensor_tensor(tb, qA[:], tb, Alu.subtract),
                       reads=["qA", tbn], writes=[tbn])
                p.emit("s", lambda e, tb=tb: e.activation(tb, tb, AFT.Square),
                       reads=[tbn], writes=[tbn])
            if si < 16:
                tb, tbn = tbufs[si % 2]
                tb3 = tb.rearrange("p (nb b) -> p nb b", b=64)
                ib = isc.unsqueeze(2).broadcast_to([128, NB, 64])
                p.emit("v", lambda e, wc3=wc3, ib=ib, tb3=tb3: e.tensor_tensor(
                    tb3, wc3, ib, Alu.mult),
                    reads=[f"qw{slot}", "qBK"], writes=[tbn])
                p.emit("s", lambda e, tb=tb: e.activation(qI[:], tb, AFT.Copy),
                       reads=[tbn], writes=["qI"])
            yield
        p.emit("v", lambda e: e.tensor_tensor(bsc2, sbest, base_, Alu.mult),
               reads=["qBK"], writes=["qBK"])
        p.emit("v", lambda e: e.reciprocal(isc, bsc2), reads=["qBK"], writes=["qBK"])
        ib = isc.unsqueeze(2).broadcast_to([128, NB, 64])
        sbb = bsc2.unsqueeze(2).broadcast_to([128, NB, 64])
        p.emit("v", lambda e, wc3=wc3, ib=ib: e.tensor_tensor(
            qB[:].rearrange("p (nb b) -> p nb b", b=64), wc3, ib, Alu.mult),
            reads=[f"qw{slot}", "qBK"], writes=["qB"])
        qa3 = qA[:].rearrange("p (nb b) -> p nb b", b=64)
        qb3 = qB[:].rearrange("p (nb b) -> p nb b", b=64)
        p.emit("v", lambda e, sbb=sbb: e.tensor_tensor(qa3, qb3, sbb, Alu.mult),
               reads=["qB", "qBK"], writes=["qA"])
        p.emit("v", lambda e, wc3=wc3: e.tensor_tensor(qa3, wc3, qa3, Alu.subtract),
               reads=["qA", f"qw{slot}"], writes=["qA"])
        p.emit("v", lambda e, ib=ib: e.tensor_tensor(qa3, qa3, ib, Alu.mult),
               reads=["qA", "qBK"], writes=["qA"])
        p.emit("v", lambda e: e.tensor_tensor(qB[:], qB[:], qA[:], Alu.add),
               reads=["qA", "qB"], writes=["qB"])
        p.emit("s", lambda e: e.activation(qI[:], qB[:], AFT.Copy), reads=["qB"], writes=["qI"])
        p.emit("v", lambda e: e.tensor_scalar(qA[:], qI[:], 15.0, -15.0, Alu.min, Alu.max),
               reads=["qI"], writes=["qA"])
        sb = bsc2.unsqueeze(2).broadcast_to([128, NB, 64])
        p.emit("v", lambda e, sb=sb: e.tensor_tensor(
            qdq[:].rearrange("p (nb b) -> p nb b", b=64),
            qA[:].rearrange("p (nb b) -> p nb b", b=64), sb, Alu.mult),
            reads=["qA", "qBK"], writes=["qdq"])
        yield
        for i in range(8):
            p.emit("pe", lambda e, i=i: e.transpose(ps_t[:, ts(i, 128)], qdq[:, ts(i, 128)],
                                                    g["IDE"][:]),
                   reads=["qdq", "IDE"], writes=["ps_t"])
        p.emit("v", lambda e: e.tensor_copy(qT[:], ps_t[:, 0:1024]),
               reads=["ps_t"], writes=["qT"])
        dst = qcin_flat[off:off + 128 * cw].rearrange("(p f) -> p f", p=128)[
            :, part * 1024:(part + 1) * 1024]
        p.dma("g", dst, qT[:], reads=["qT"], writes=["q_cin"])
        yield


def quant_allgather(nc, p, g):
    p.emit("g", lambda e: e.collective_compute(
        "AllGather", Alu.bypass, replica_groups=[[0, 1, 2, 3, 4, 5, 6, 7]],
        ins=[g["q_cin"][:]], outs=[g["q_cout"][:]]),
        reads=["q_cin"], writes=["q_cout"], sem="ccc", inc=1)


def ffn(nc, p, g, call):
    AB1, AB2, AB3, STR = g["AB1"], g["AB2"], g["AB3"], g["STR"]
    ps_a, ps_b = g["ps_a"], g["ps_b"]
    tagln = "ln2" if call == 0 else "ln4"
    if call == 0:
        p.fence_all()
    QSH_FC = 8 * 2048 * 128
    qcof = [g["q_cout"][r].rearrange("a b -> (a b)") for r in range(8)]
    for sub in range(T // TSUB2):
        sl = slice(sub * TSUB2, (sub + 1) * TSUB2)
        for mh in range(64):
            r, cih = mh // 8, mh % 8
            slot = mh % 2
            slab = STR[:, slot * 2048:slot * 2048 + 2048]
            src = qcof[r][cih * 2048 * 128:(cih + 1) * 2048 * 128]
            p.dma("sp", slab, src.rearrange("(p f) -> p f", p=128),
                  reads=["q_cout"], writes=[f"STR{slot}"])
            stiles = [slab[:, ts(_k, 128)] for _k in range(16)]
            pp, ppn = (ps_a, "ps_a") if mh % 2 == 0 else (ps_b, "ps_b")
            for kc in range(NCH):
                mov = AB1[:, ts(kc, T)][:, sl]
                p.emit("pe", lambda e, stiles=stiles, kc=kc, mov=mov, pp=pp: e.matmul(
                    pp[:, 0:TSUB2], stiles[kc], mov, start=(kc == 0), stop=(kc == NCH - 1)),
                    reads=[f"STR{slot}", f"{tagln}_{kc}", f"AB1:{kc}"], writes=[ppn])
            harena, hnm = (AB2, "AB2") if mh < 32 else (AB3, "AB3")
            hslot = mh % 32
            hdst = harena[:, hslot * TSUB2:(hslot + 1) * TSUB2]
            hf32 = g["SMALL"][:, 0:TSUB2] if mh % 2 == 0 else g["RSC"][:, 0:TSUB2]
            hfn = "SMALL" if mh % 2 == 0 else "RSC"
            p.emit("s", lambda e, hf32=hf32, pp=pp: e.activation(hf32, pp[:, 0:TSUB2], AFT.Relu),
                   reads=[ppn], writes=[hfn])
            p.emit("s", lambda e, hdst=hdst, hf32=hf32: e.activation(hdst, hf32, AFT.Square),
                   reads=[hfn], writes=[f"h{mh}", f"{hnm}:{hslot // 2}"])
        if KDBG == "hid0" and call == 0 and sub == 0:
            for sl2 in range(16):
                p.dma("g", g["y"][ts(sl2, 128), 0:1024],
                      (g["AB2"] if sl2 < 8 else g["AB3"])[:, ts(sl2 % 8, 2048)][:, 0:1024],
                      reads=[f"h{sl2 * 4}", f"h{sl2 * 4 + 1}", f"h{sl2 * 4 + 2}", f"h{sl2 * 4 + 3}"])
        for mo in range(NCH):
            r, cio = mo // 2, mo % 2
            for quarter in range(4):
                slot = quarter % 2
                slab = STR[:, slot * 2048:slot * 2048 + 2048]
                base_el = QSH_FC + cio * 8192 * 128
                src = qcof[r][base_el:base_el + 8192 * 128].rearrange(
                    "(p f) -> p f", p=128)[:, quarter * 2048:(quarter + 1) * 2048]
                p.dma("sp", slab, src, reads=["q_cout"], writes=[f"STR{slot}"])
                stiles = [slab[:, ts(_k, 128)] for _k in range(16)]
                pp, ppn = (ps_a, "ps_a") if mo % 2 == 0 else (ps_b, "ps_b")
                for kq in range(16):
                    kh = quarter * 16 + kq
                    harena = AB2 if kh < 32 else AB3
                    hs = kh % 32
                    mov = harena[:, hs * TSUB2:(hs + 1) * TSUB2]
                    p.emit("pe", lambda e, stiles=stiles, kq=kq, mov=mov, kh=kh, pp=pp: e.matmul(
                        pp[:, 0:TSUB2], stiles[kq], mov, start=(kh == 0), stop=(kh == 63)),
                        reads=[f"STR{slot}", f"h{kh}"], writes=[ppn])
            xc = g["A_x"][:, ts(mo, T)][:, sl]
            p.emit("v", lambda e, xc=xc, pp=pp: e.tensor_tensor(xc, xc, pp[:, 0:TSUB2], Alu.add),
                   reads=[ppn, f"x{mo}"], writes=[f"x{mo}"])


def attn_b(nc, p, g):
    AB1, AB2, AB3, KVB, VTM, STR = (g["AB1"], g["AB2"], g["AB3"], g["KVB"],
                                    g["VTM"], g["STR"])
    ps_a, ps_b, ps_t = g["ps_a"], g["ps_b"], g["ps_t"]
    KPART = NCH * 128 * T
    kvin_flat = g["kv_cin"].rearrange("a b -> (a b)")
    for W, bias, kindnm in (("wqT", "bqT", "q"), ("wkT", "bkT", "k"), ("wvT", "bvT", "v")):
        for mo in range(NCH):
            stiles = stream_colblock(nc, p, g, g[W], W, D, mo, mo % 2)
            pq, pqn = (ps_a, "ps_a") if mo % 2 == 0 else (ps_b, "ps_b")
            for half in range(2):
                sl = slice(half * 512, half * 512 + 512)
                for kc in range(NCH):
                    mov = AB1[:, ts(kc, T)][:, sl]
                    p.emit("pe", lambda e, stiles=stiles, kc=kc, mov=mov, sl=sl, pq=pq: e.matmul(
                        pq[:, sl], stiles[kc], mov, start=(kc == 0), stop=(kc == NCH - 1)),
                        reads=[f"STR{mo % 2}", f"ln3_{kc}", f"AB1:{kc}"], writes=[pqn])
            bcol = g[bias + "_s"][:, mo:mo + 1]
            if kindnm == "q":
                dst = AB2[:, ts(mo, T)]
                p.emit("v", lambda e, dst=dst, bcol=bcol, pq=pq: e.tensor_single_scalar(
                    dst, pq[:, 0:T], bcol, Alu.add),
                    reads=[pqn], writes=[f"q{mo}", f"AB2:{mo}"])
            elif kindnm == "k":
                kb = VTM[:, 0:T]
                p.emit("v", lambda e, kb=kb, bcol=bcol, pq=pq: e.tensor_single_scalar(
                    kb, pq[:, 0:T], bcol, Alu.add), reads=[pqn], writes=["VTM"])
                dst = kvin_flat[mo * 128 * T:(mo + 1) * 128 * T].rearrange("(p f) -> p f", p=128)
                p.dma("g", dst, kb, reads=["VTM"], writes=["kv_cin"])
            else:
                vb = VTM[:, 0:T]
                p.emit("v", lambda e, vb=vb, bcol=bcol, pq=pq: e.tensor_single_scalar(
                    vb, pq[:, 0:T], bcol, Alu.add), reads=[pqn], writes=["VTM"])
                for i in range(8):
                    p.emit("pe", lambda e, i=i, vb=vb: e.transpose(
                        ps_t[:, ts(i, 128)], vb[:, ts(i, 128)], g["IDE"][:]),
                        reads=["VTM", "IDE"], writes=["ps_t"])
                vtb = g["SC2"][:, 0:T]
                p.emit("v", lambda e, vtb=vtb: e.tensor_copy(vtb, ps_t[:, 0:T]),
                       reads=["ps_t"], writes=["SC2"])
                dstap = kvin_flat[KPART + mo * 1024 * 128:
                                  KPART + (mo + 1) * 1024 * 128].rearrange(
                    "(p f) -> p f", p=128)
                p.dma("g", dstap, vtb, reads=["SC2"], writes=["kv_cin"])
    p.emit("g", lambda e: e.collective_compute(
        "AllGather", Alu.bypass, replica_groups=[[0, 1], [2, 3], [4, 5], [6, 7]],
        ins=[g["kv_cin"][:]], outs=[g["kv_cout"][:]]),
        reads=["kv_cin"], writes=["kv_cout"], sem="ccc", inc=1)

    kvof = [g["kv_cout"][r].rearrange("a b -> (a b)") for r in range(2)]
    for qp in range(4):
        qts = (2 * qp, 2 * qp + 1)
        for khalf in range(2):
            for kc in range(NCH):
                slot = kc % 2
                slab = KVB[:, slot * 2048:slot * 2048 + 1024]
                src = kvof[khalf][kc * 128 * T:(kc + 1) * 128 * T]
                p.dma("sp", slab, src.rearrange("(p f) -> p f", p=128),
                      reads=["kv_cout"], writes=[f"KVB{slot}"])
                for j, qt in enumerate(qts):
                    pp = ps_a if j == 0 else ps_b
                    ppn = "ps_a" if j == 0 else "ps_b"
                    stat = AB2[:, ts(kc, T)][:, ts(qt, 128)]
                    for nh in range(2):
                        p.emit("pe", lambda e, pp=pp, stat=stat, slab=slab, nh=nh, kc=kc:
                               e.matmul(pp[:, nh * 512:(nh + 1) * 512], stat,
                                        slab[:, nh * 512:(nh + 1) * 512],
                                        start=(kc == 0), stop=(kc == NCH - 1)),
                               reads=[f"KVB{slot}", f"q{kc}", f"AB2:{kc}"], writes=[ppn])
            for j, qt in enumerate(qts):
                pp = ps_a if j == 0 else ps_b
                ppn = "ps_a" if j == 0 else "ps_b"
                nmax = g["CST"][:, 92 + j:93 + j]
                ssum = g["CST"][:, 94 + j:95 + j]
                Pj = g["SC1"] if j == 0 else g["SC2"]
                Pjn = "SC1" if j == 0 else "SC2"
                P2 = KVB[:, 1024:2048] if j == 0 else KVB[:, 3072:4096]
                P2n = "KVBp0" if j == 0 else "KVBp1"
                if khalf == 0:
                    p.emit("v", lambda e, pp=pp, nmax=nmax: e.tensor_reduce(
                        nmax, pp[:, 0:T], AX.X, Alu.max, negate=True),
                        reads=[ppn], writes=[f"nmx{j}"])
                    p.emit("s", lambda e, pp=pp, Pj=Pj, nmax=nmax, ssum=ssum: e.activation(
                        Pj[:, 0:T], pp[:, 0:T], AFT.Exp, bias=nmax, scale=1.0,
                        accum_out=ssum),
                        reads=[ppn, f"nmx{j}"], writes=[Pjn, f"ssm{j}"])
                else:
                    nm2 = g["CST"][:, 96 + j:97 + j]
                    fac = g["CST"][:, 98 + j:99 + j]
                    s2 = g["CST"][:, 100 + j:101 + j]
                    p.emit("v", lambda e, pp=pp, nm2=nm2: e.tensor_reduce(
                        nm2, pp[:, 0:T], AX.X, Alu.max, negate=True),
                        reads=[ppn], writes=[f"nm2{j}"])
                    p.emit("v", lambda e, fac=fac, nmax=nmax, nm2=nm2: e.tensor_tensor(
                        fac, nmax, nm2, Alu.max),
                        reads=[f"nmx{j}", f"nm2{j}"], writes=[f"tmx{j}"])
                    p.emit("v", lambda e, fac=fac, nmax=nmax, nm2=nm2: e.tensor_tensor(
                        fac, nmax, nm2, Alu.min),
                        reads=[f"nmx{j}", f"nm2{j}"], writes=[f"tmx{j}"])
                    dfc = g["CST"][:, 102 + j:103 + j]
                    p.emit("v", lambda e, dfc=dfc, fac=fac, nmax=nmax: e.tensor_tensor(
                        dfc, fac, nmax, Alu.subtract),
                        reads=[f"tmx{j}", f"nmx{j}"], writes=[f"dfc{j}"])
                    p.emit("s", lambda e, dfc=dfc: e.activation(dfc, dfc, AFT.Exp),
                           reads=[f"dfc{j}"], writes=[f"dfc{j}"])
                    p.emit("v", lambda e, Pj=Pj, dfc=dfc: e.tensor_single_scalar(
                        Pj[:, 0:T], Pj[:, 0:T], dfc, Alu.mult),
                        reads=[Pjn, f"dfc{j}"], writes=[Pjn])
                    p.emit("v", lambda e, ssum=ssum, dfc=dfc: e.tensor_tensor(
                        ssum, ssum, dfc, Alu.mult),
                        reads=[f"ssm{j}", f"dfc{j}"], writes=[f"ssm{j}"])
                    p.emit("s", lambda e, pp=pp, P2=P2, fac=fac, s2=s2: e.activation(
                        P2, pp[:, 0:T], AFT.Exp, bias=fac, scale=1.0, accum_out=s2),
                        reads=[ppn, f"tmx{j}"], writes=[P2n, f"s2{j}"])
                    p.emit("v", lambda e, ssum=ssum, s2=s2: e.tensor_tensor(
                        ssum, ssum, s2, Alu.add),
                        reads=[f"ssm{j}", f"s2{j}"], writes=[f"ssm{j}"])
                    p.emit("v", lambda e, ssum=ssum: e.reciprocal(ssum, ssum),
                           reads=[f"ssm{j}"], writes=[f"ssm{j}"])
                    p.emit("v", lambda e, Pj=Pj, ssum=ssum: e.tensor_single_scalar(
                        Pj[:, 0:T], Pj[:, 0:T], ssum, Alu.mult),
                        reads=[Pjn, f"ssm{j}"], writes=[Pjn])
                    p.emit("v", lambda e, P2=P2, ssum=ssum: e.tensor_single_scalar(
                        P2, P2, ssum, Alu.mult),
                        reads=[P2n, f"ssm{j}"], writes=[P2n])
                    for h2 in range(2):
                        srcP = Pj[:, 0:T] if h2 == 0 else P2
                        srcn = Pjn if h2 == 0 else P2n
                        for i in range(8):
                            p.emit("pe", lambda e, srcP=srcP, i=i: e.transpose(
                                ps_t[:, ts(i, 128)], srcP[:, ts(i, 128)], g["IDE"][:]),
                                reads=[srcn, "IDE"], writes=["ps_t"])
                        for i in range(8):
                            kt = h2 * 8 + i
                            d = AB3[:, kt * 1024 + qt * 128: kt * 1024 + qt * 128 + 128]
                            p.emit("v", lambda e, d=d, i=i: e.tensor_copy(d, ps_t[:, ts(i, 128)]),
                                   reads=["ps_t"], writes=[f"Pt{kt}", f"AB3:{kt}"])
    for mo in range(NCH):
        slot = mo % 2
        slab = STR[:, slot * 2048:(slot + 1) * 2048]
        for r in range(2):
            srcap = kvof[r][KPART + mo * 1024 * 128:
                            KPART + (mo + 1) * 1024 * 128].rearrange("(p f) -> p f", p=128)
            p.dma("sp", slab[:, r * 1024:(r + 1) * 1024], srcap,
                  reads=["kv_cout"], writes=[f"STR{slot}"])
        stiles = [slab[:, ts(_k, 128)] for _k in range(16)]
        for kt in range(16):
            mov = AB3[:, kt * 1024:(kt + 1) * 1024]
            for nh in range(2):
                p.emit("pe", lambda e, stiles=stiles, kt=kt, mov=mov, nh=nh: e.matmul(
                    ps_a[:, nh * 512:(nh + 1) * 512], stiles[kt],
                    mov[:, nh * 512:(nh + 1) * 512], start=(kt == 0), stop=(kt == 15)),
                    reads=[f"STR{slot}", f"Pt{kt}", f"AB3:{kt}"], writes=["ps_a"])
        xc = g["A_x"][:, ts(mo, T)]
        p.emit("v", lambda e, xc=xc: e.tensor_tensor(xc, xc, ps_a[:, 0:T], Alu.add),
               reads=["ps_a", f"x{mo}"], writes=[f"x{mo}"])


def write_out(nc, p, g):
    if KDBG == "ln1":
        for ch in range(NCH):
            p.dma("g", g["y"][ts(ch, 128), :], g["AB1"][:, ts(ch, T)],
                  reads=[f"ln1_{ch}", f"AB1:{ch}"])
    elif KDBG == "q":
        for ch in range(NCH):
            p.dma("g", g["y"][ts(ch, 128), :], g["AB2"][:, ts(ch, T)],
                  reads=[f"q{ch}"])
    elif KDBG == "deq":
        for cch in range(8):
            srcq = g["q_cin"].rearrange("a b -> (a b)")[cch * 2048 * 128:(cch + 1) * 2048 * 128]
            p.dma("g", g["y"][ts(cch * 2, 128) if False else slice(0, 0), :] if False else
                  g["y"].rearrange("a b -> (a b)")[cch * 2048 * 128:(cch + 1) * 2048 * 128].rearrange("(p f) -> p f", p=128),
                  srcq.rearrange("(p f) -> p f", p=128), reads=["q_cin"])
    elif KDBG == "deqp":
        for cch in range(2):
            srcq = g["q_cin"].rearrange("a b -> (a b)")[
                8 * 2048 * 128 + cch * 8192 * 128: 8 * 2048 * 128 + (cch + 1) * 8192 * 128]
            p.dma("g", g["y"].rearrange("a b -> (a b)")[cch * 8192 * 128:(cch + 1) * 8192 * 128].rearrange("(p f) -> p f", p=128),
                  srcq.rearrange("(p f) -> p f", p=128), reads=["q_cin"])
    elif KDBG == "ag":
        for cch in range(8):
            srcq = g["q_cout"][3].rearrange("a b -> (a b)")[cch * 2048 * 128:(cch + 1) * 2048 * 128]
            p.dma("g", g["y"].rearrange("a b -> (a b)")[cch * 2048 * 128:(cch + 1) * 2048 * 128].rearrange("(p f) -> p f", p=128),
                  srcq.rearrange("(p f) -> p f", p=128), reads=["q_cout"])
    elif KDBG == "hid":
        for sl2 in range(16):
            p.dma("g", g["y"][ts(sl2, 128), :], g["AB2"][:, ts(sl2, 1024)],
                  reads=[f"h{sl2 * 2}", f"h{sl2 * 2 + 1}"])
    elif KDBG == "kv":
        p.dma("g", g["y"][0:128, 0:1024], g["KVB"][:, 0:1024], reads=["KVB"])
        p.dma("g", g["y"][128:256, 0:512], g["KVB"][:, 1024:1536], reads=["KVB"])
        p.dma("g", g["y"][256:384, 0:1024], g["VTM"][:, 0:1024], reads=["VTM"])
        p.dma("g", g["y"][384:512, 0:512], g["VTM"][:, 1024:1536], reads=["VTM"])
    elif KDBG == "ao":
        for ch in range(NCH):
            p.dma("g", g["y"][ts(ch, 128), :], g["AB1"][:, ts(ch, T)],
                  reads=[f"ao{ch}", f"AB1:{ch}"])
    elif KDBG == "hkb":
        p.dma("g", g["y"][0:128, 0:1024], g["HKB"], reads=["HKB"])
    elif KDBG == "hid0":
        pass
    else:
        for ch in range(NCH):
            p.dma("sp", g["y"][ts(ch, 128), :], g["A_x"][:, ts(ch, T)], reads=[f"x{ch}"])
    p.fence("sp")
    p.fence("g")


def kernel(**inputs):
    xs = np.asarray(inputs["L0_attn_a_h"], dtype=np.float32)
    nc = build_nc()
    in_maps = []
    f32 = np.float32
    for c in range(8):
        b, h = c // 2, c % 2
        xo = np.ascontiguousarray(xs[b, h * T:(h + 1) * T, :].T)
        if h == 1:
            xh = np.ascontiguousarray(xs[b, 512:1024, :].T)
        else:
            xh = np.zeros((D, HALO), f32)
        k_off = h * T - HALO
        hkx = np.zeros((128, 1024), f32)
        if h == 0:
            hkx[:, 0:512] = -1e30
        m = {
            "x_own": xo, "x_halo": xh,
            "posmeta": np.full((128, 1), k_off, f32),
            "hkx": hkx,
            "wq_comp": inputs["L0_attn_a_wq_comp"], "wq_up": inputs["L0_attn_a_wq_up"],
            "wk": inputs["L0_attn_a_wk"], "wv": inputs["L0_attn_a_wv"],
            "q_norm_w": np.asarray(inputs["L0_attn_a_q_norm_w"]).reshape(1, -1),
            "k_norm_w": np.asarray(inputs["L0_attn_a_k_norm_w"]).reshape(1, -1),
            "sinkrep": np.ascontiguousarray(np.broadcast_to(
                np.asarray(inputs["L0_attn_a_sink_logit"]).reshape(1, -1), (128, NHEAD))),
            "out_w1": inputs["L0_attn_a_out_w1"], "out_w2": inputs["L0_attn_a_out_w2"],
            "wqT": np.ascontiguousarray(np.asarray(inputs["L0_attn_b_Wq"]).T),
            "wkT": np.ascontiguousarray(np.asarray(inputs["L0_attn_b_Wk"]).T),
            "wvT": np.ascontiguousarray(np.asarray(inputs["L0_attn_b_Wv"]).T),
            "bqT": np.ascontiguousarray(np.asarray(inputs["L0_attn_b_bq"]).reshape(NCH, 128).T),
            "bkT": np.ascontiguousarray(np.asarray(inputs["L0_attn_b_bk"]).reshape(NCH, 128).T),
            "bvT": np.ascontiguousarray(np.asarray(inputs["L0_attn_b_bv"]).reshape(NCH, 128).T),
            "wfc_shard": np.ascontiguousarray(
                np.asarray(inputs["L0_ffn_w_fc"])[c * 1024:(c + 1) * 1024, :]),
            "wproj_shard": np.ascontiguousarray(
                np.asarray(inputs["L0_ffn_w_proj"])[c * 256:(c + 1) * 256, :]),
        }
        m = {k: np.asarray(v, dtype=f32) for k, v in m.items()}
        in_maps.append(m)
    res = run_bass_kernel_spmd(nc, in_maps, core_ids=list(range(8)))
    out = np.empty((4, 2048, 2048), np.float32)
    for c in range(8):
        b, h = c // 2, c % 2
        out[b, h * T:(h + 1) * T, :] = np.asarray(res.results[c]["y"]).T
    return out
